# version 05
# baseline (speedup 1.0000x reference)
import numpy as np
import ml_dtypes
import concourse.bass as bass
import concourse.mybir as mybir
from concourse.bass_utils import run_bass_kernel_spmd
from contextlib import ExitStack

F32 = mybir.dt.float32
BF16 = mybir.dt.bfloat16
I32 = mybir.dt.int32
AF = mybir.ActivationFunctionType
ALU = mybir.AluOpType
AX = mybir.AxisListType
BF16NP = ml_dtypes.bfloat16

NCORES = 8
B, T, D = 4, 4096, 1024
TOK = 2048
NCH = 8
TT = 512
NTT = TOK // TT
HALO = 6
TH = TOK + 2 * HALO
PAIRS = [[0, 1], [2, 3], [4, 5], [6, 7]]
TOTAL_PHASES = 8

ENGINES = ("sync", "tensor", "vector", "scalar", "gpsimd")


class Step:
    __slots__ = ("eng", "fn", "deps", "kind", "n_dma", "val", "late", "name",
                 "sem_idx", "gate", "fence")

    def __init__(self, eng, fn, deps, kind, n_dma, val, late, name):
        self.eng, self.fn, self.deps, self.kind = eng, fn, deps, kind
        self.n_dma, self.val, self.late, self.name = n_dma, val, late, name
        self.sem_idx = 0
        self.gate = None
        self.fence = None


class P:

    NDSEM = 12
    NCSEM = 4

    def __init__(self, nc, ctx):
        self.nc = nc
        self.ctx = ctx
        e = ctx.enter_context
        self.esem = {n: e(nc.semaphore(f"es_{n}")) for n in ENGINES}
        self.dsem = {n: [e(nc.semaphore(f"ds_{n}{i}")) for i in range(self.NDSEM)]
                     for n in ("sync", "gpsimd", "scalar", "vector")}
        self.csem = [e(nc.semaphore(f"cs_gp{i}")) for i in range(self.NCSEM)]
        self.ecnt = {n: 0 for n in ENGINES}
        self.dcnt = {n: [0] * self.NDSEM for n in self.dsem}
        self.dstep_i = {n: 0 for n in self.dsem}
        self.ccnt = [0] * self.NCSEM
        self.cstep_i = 0
        self.steps = {n: [] for n in ENGINES}
        self.last_step = {n: None for n in ENGINES}
        self.cur_eng = None
        self.cur_step = None
        self._n_dma_seen = 0
        self.input_names = set()
        self._input_aps = {}
        self.xsem = {n: e(nc.semaphore(f"xs_{n}")) for n in ENGINES}
        self.xcnt = {n: 0 for n in ENGINES}

    def step(self, eng, fn, deps=(), kind="compute", n_dma=0, late=False, name=""):
        deps = [d for d in deps if d is not None]
        s = Step(eng, fn, deps, kind, n_dma, 0, late, name)
        if kind == "dma":
            assert n_dma > 0, name
            idx = self.dstep_i[eng] % self.NDSEM
            self.dstep_i[eng] += 1
            prev = self.dcnt[eng][idx]
            if prev > 0:
                s.gate = prev
            self.dcnt[eng][idx] = prev + 16 * n_dma
            s.sem_idx = idx
            s.val = self.dcnt[eng][idx]
        elif kind == "collective":
            assert eng == "gpsimd"
            idx = self.cstep_i % self.NCSEM
            self.cstep_i += 1
            prev = self.ccnt[idx]
            if prev > 0:
                s.gate = prev
            self.ccnt[idx] = prev + 1
            s.sem_idx = idx
            s.val = self.ccnt[idx]
        else:
            self.ecnt[eng] += 1
            s.val = self.ecnt[eng]
        self.steps[eng].append(s)
        self.last_step[eng] = s
        return s

    def barrier(self):
        lasts = [self.last_step[n] for n in ENGINES if self.last_step[n] is not None]
        fence = {n: list(self.dcnt[n]) for n in self.dcnt}
        fence["_c"] = list(self.ccnt)
        out = []
        for n in ENGINES:
            def fn(n=n):
                return getattr(self.nc, n).nop()
            s = self.step(n, fn, deps=list(lasts), name=f"barrier_{n}")
            s.fence = fence
            out.append(s)
        return out

    def input(self, name, shape, dtype=F32):
        if name in self._input_aps:
            return self._input_aps[name]
        self.input_names.add(name)
        ap = self.nc.dram_tensor(name, list(shape), dtype, kind="ExternalInput").ap()
        self._input_aps[name] = ap
        return ap

    def chain(self, ins):
        eng = self.cur_eng
        self.xcnt[eng] += 1
        ins.then_inc(self.xsem[eng], 1)
        getattr(self.nc, eng).wait_ge(self.xsem[eng], self.xcnt[eng])
        return ins

    def dma(self, out, in_, **kw):
        eng = self.cur_eng
        ins = getattr(self.nc, eng).dma_start(out=out, in_=in_, **kw)
        ins.then_inc(self.dsem[eng][self.cur_step.sem_idx], 16)
        self._n_dma_seen += 1
        return ins

    def emit(self, block):
        waited = {n: {} for n in ENGINES}

        def emit_engine(eng):
            self.cur_eng = eng
            ng = self.nc
            for s in self.steps[eng]:
                w = waited[eng]
                self.cur_step = s
                if s.fence is not None:
                    for qn, counts in s.fence.items():
                        for i, v in enumerate(counts):
                            if v <= 0:
                                continue
                            key = ("c", "gp", i) if qn == "_c" else ("d", qn, i)
                            sem = self.csem[i] if qn == "_c" else self.dsem[qn][i]
                            if w.get(key, 0) >= v:
                                continue
                            w[key] = v
                            getattr(ng, eng).wait_ge(sem, v)
                for d in s.deps:
                    if d.kind == "dma":
                        key, sem = ("d", d.eng, d.sem_idx), self.dsem[d.eng][d.sem_idx]
                    elif d.kind == "collective":
                        key, sem = ("c", "gp", d.sem_idx), self.csem[d.sem_idx]
                    else:
                        key, sem = ("e", d.eng), self.esem[d.eng]
                    if w.get(key, 0) >= d.val:
                        continue
                    w[key] = d.val
                    getattr(ng, eng).wait_ge(sem, d.val)
                if s.gate is not None:
                    key = (("c", "gp", s.sem_idx) if s.kind == "collective"
                           else ("d", eng, s.sem_idx))
                    sem = (self.csem[s.sem_idx] if s.kind == "collective"
                           else self.dsem[eng][s.sem_idx])
                    if w.get(key, 0) < s.gate:
                        w[key] = s.gate
                        getattr(ng, eng).wait_ge(sem, s.gate)
                self._n_dma_seen = 0
                last = s.fn()
                if s.kind == "dma":
                    assert self._n_dma_seen == s.n_dma, (
                        f"step {s.name}: declared {s.n_dma} dmas, emitted {self._n_dma_seen}")
                elif s.kind == "collective":
                    assert last is not None, s.name
                    last.then_inc(self.csem[s.sem_idx], 1)
                else:
                    assert last is not None, f"step {s.name} ({s.eng}) returned no instruction"
                    last.then_inc(self.esem[eng], 1)
            self.cur_eng = None

        for eng in ENGINES:
            getattr(block, eng)(lambda _e=None, _eng=eng: emit_engine(_eng))



def host_prep(inputs):
    f = np.ascontiguousarray
    sh = {}
    x = np.asarray(inputs["L0_attn_x"])
    xt = x.reshape(B * T, D)

    sh["W_qkv"] = f(np.asarray(inputs["L0_attn_qkv_w"]).T)
    sh["B_qkv"] = f(np.asarray(inputs["L0_attn_qkv_b"]).reshape(24, 128).T)
    sh["W_l0proj"] = f(np.asarray(inputs["L0_attn_proj_w"]).T)
    sh["B_l0proj"] = f(np.asarray(inputs["L0_attn_proj_b"]).reshape(8, 128).T)

    rpb = np.asarray(inputs["L0_attn_rpb"])
    W0 = 140
    r_idx = np.arange(W0)[:, None]
    c_idx = np.arange(128)[None, :]
    bidx = 6 + r_idx - c_idx
    band = (bidx >= 6) & (bidx <= 18)
    valid_first = np.where(c_idx < 6, (r_idx >= 6) & (r_idx <= 18), band)
    valid_last = np.where(c_idx >= 122, (r_idx >= 121) & (r_idx <= 133), band)
    g = rpb[:, np.clip(bidx, 0, 24)]
    NEG = np.float32(-1e30)
    rpb_int = np.where(band[None], g, NEG).astype(np.float32)
    rpb_first = np.where(valid_first[None], g, NEG).astype(np.float32)
    rpb_last = np.where(valid_last[None], g, NEG).astype(np.float32)

    sh["W_gate_r"] = f(np.asarray(inputs["L0_ffn_gate_w"]).T)
    tri = np.triu(np.ones((128, 128), dtype=np.float32))
    sh["TRIU"] = f(tri)
    oh = np.zeros((16, 16 * 128), dtype=np.float32)
    for _e in range(16):
        oh[_e, _e * 128:(_e + 1) * 128] = 1.0
    sh["ONEHOT16"] = f(oh)
    sh["EBIAS"] = f(np.broadcast_to(np.asarray(inputs["L0_ffn_expert_bias"])[None, :], (128, 16)).copy())
    sh["W_eg"] = f(np.asarray(inputs["L0_ffn_w_gate"]).transpose(0, 2, 1))
    sh["W_eu"] = f(np.asarray(inputs["L0_ffn_w_up"]).transpose(0, 2, 1))
    sh["W_ed"] = f(np.asarray(inputs["L0_ffn_w_down"]).transpose(0, 2, 1))
    sh["W_sg"] = f(np.asarray(inputs["L0_ffn_shared_gate_w"]).T)
    sh["W_su"] = f(np.asarray(inputs["L0_ffn_shared_up_w"]).T)
    sh["W_sd"] = f(np.asarray(inputs["L0_ffn_shared_down_w"]).T)

    sh["W_q1"] = f(np.asarray(inputs["L1_attn_Wq"]).T)
    sh["W_k1"] = f(np.asarray(inputs["L1_attn_Wk"]).T)
    sh["W_v1"] = f(np.asarray(inputs["L1_attn_Wv"]).T)
    sh["W_g1"] = f(np.asarray(inputs["L1_attn_Wg1"]).T)
    sh["W_g2"] = f(np.asarray(inputs["L1_attn_Wg2"]).T)
    sh["GNW"] = f(np.asarray(inputs["L1_attn_g_norm_weight"]).reshape(8, 128).T)
    sh["W_o1"] = f(np.asarray(inputs["L1_attn_Wo"]).T)

    sh["W1G"] = f(np.asarray(inputs["L1_ffn_w_gate"]).T)
    sh["W1U"] = f(np.asarray(inputs["L1_ffn_w_up"]).T)
    sh["W1D"] = f(np.asarray(inputs["L1_ffn_w_down"]).T)

    sh["W_qkv2"] = f(np.asarray(inputs["L2_attn_W_attn"]))
    sh["B_qkv2"] = f(np.asarray(inputs["L2_attn_b_attn"]).reshape(24, 128).T)
    sh["W_l2proj"] = f(np.asarray(inputs["L2_attn_W_proj"]))
    sh["B_l2proj"] = f(np.asarray(inputs["L2_attn_b_proj"]).reshape(8, 128).T)
    sh["TEMP"] = f(np.asarray(inputs["L2_attn_temperature"]).reshape(1, 1))
    import math
    start = 2.0 ** (-2.0 ** (-(math.log2(16) - 3)))
    slopes = np.asarray([start * start ** i for i in range(16)], dtype=np.float64)
    rels = np.arange(-28, 4)
    pcol = np.arange(128)
    ab = slopes[:, None, None] * (pcol[None, None, :] + 128 * rels[None, :, None])
    sh["_alibi2"] = ab.astype(np.float32)
    jq = np.arange(512)[None, :]
    sh["_negr2"] = (-slopes[:, None] * jq).astype(np.float32)
    kk = np.arange(128)[:, None]
    m2 = np.concatenate(
        [np.where(128 * r + kk <= jq, 0.0, -1e30) for r in range(4)],
        axis=1).astype(np.float32)
    sh["MASK2"] = f(m2)

    sh["W_r2"] = f(np.asarray(inputs["L2_ffn_W_r"]).T)
    sh["W_k2"] = f(np.asarray(inputs["L2_ffn_W_k"]).T)
    sh["W_v2"] = f(np.asarray(inputs["L2_ffn_W_v"]).T)
    sh["MU_R"] = f(np.asarray(inputs["L2_ffn_mu_r"]).reshape(8, 128).T)
    sh["MU_K"] = f(np.asarray(inputs["L2_ffn_mu_k"]).reshape(8, 128).T)

    sh["W_in3"] = f(np.asarray(inputs["L3_attn_W_in"]).T)
    cw = np.asarray(inputs["L3_attn_conv_w"]).reshape(2048, 3)
    sh["CONVW"] = f(cw.reshape(16, 128, 3).transpose(1, 0, 2).reshape(128, 48))
    sh["CONVB"] = f(np.asarray(inputs["L3_attn_conv_b"]).reshape(16, 128).T)
    sh["W_dt3"] = f(np.asarray(inputs["L3_attn_W_dt"]).T)
    sh["A3"] = f(np.asarray(inputs["L3_attn_A_proj"]).T)
    sh["B3"] = f(np.asarray(inputs["L3_attn_B_proj"]).T)
    sh["C3"] = f(np.asarray(inputs["L3_attn_C_proj"]))
    sh["W_out3"] = f(np.asarray(inputs["L3_attn_W_out"]).T)

    rm = np.asarray(inputs["L3_ffn_rail_maker"])
    rup = np.asarray(inputs["L3_ffn_rail_up_proj"])
    rml = rm.transpose(0, 1, 3, 2).reshape(256, 16, 16)
    rupl = rup.transpose(0, 1, 3, 2).reshape(256, 16, 64)
    sh["RM_P"] = f(rml.transpose(1, 0, 2).reshape(16, 256 * 16))
    sh["RUP_P"] = f(rupl.transpose(1, 0, 2).reshape(16, 256 * 64))
    sh["W_cp"] = f(np.asarray(inputs["L3_ffn_w_cproj"]).T)

    per = []
    xf = xt.T
    for c in range(NCORES):
        d = {}
        lo, hi = c * TOK, (c + 1) * TOK
        seq = c // 2
        left = max(lo - HALO, seq * T)
        right = min(hi + HALO, (seq + 1) * T)
        xh = np.zeros((D, TH), dtype=np.float32)
        xh[:, HALO - (lo - left): HALO - (lo - left) + (right - left)] = xf[:, left:right]
        d["X0"] = f(xh)
        d["RPB_Q0"] = rpb_first if c % 2 == 0 else rpb_int
        d["RPB_Q15"] = rpb_int if c % 2 == 0 else rpb_last
        d["RPB_INT"] = rpb_int
        d["PFLAG"] = np.full((128, 1), float(c % 2), dtype=np.float32)
        d["PFLAGR"] = np.full((128, 1), float(1 - c % 2), dtype=np.float32)
        hb = 8 * (c % 2)
        d["ALIBI2_LOC"] = f(sh["_alibi2"][hb:hb + 8].reshape(8 * 32, 128).T)
        d["NEGR2_LOC"] = f(sh["_negr2"][hb:hb + 8].reshape(1, 4096))
        per.append(d)
    return sh, per



class IO:
    pass


CONSTS = [1e-5, 0.125, -0.5, 0.5, 1.0, -1.0, 2.0, 1.0 / 127.0, 1.0 / 1024.0,
          128 ** -0.5, -1e-5, 0.0]


def alloc_globals(p, io):
    nc = p.nc
    e = p.ctx.enter_context
    io.xa = e(nc.sbuf_tensor("xs_a", [128, NCH * TOK], BF16))
    io.xb = e(nc.sbuf_tensor("xs_b", [128, NCH * TOK], BF16))
    io.cur = io.xa
    io.ybuf = io.xb
    io.ones_col = e(nc.sbuf_tensor("ones_col", [128, 2], BF16))
    io.ones_row = e(nc.sbuf_tensor("ones_row", [1, 128], F32))
    io.ones_row_b = e(nc.sbuf_tensor("ones_row_b", [1, 128], BF16))
    io.idn = e(nc.sbuf_tensor("idn", [128, 128], BF16))
    io.cpool = e(nc.sbuf_tensor("cpool", [128, len(CONSTS)], F32))
    for i, v in enumerate(CONSTS):
        nc.const_aps.register(F32, v, io.cpool[:, i:i + 1])
    io.out_f32 = None

    def fn():
        v = nc.vector
        v.memset(io.ones_col[:], 1.0)
        v.memset(io.ones_row[:], 1.0)
        v.memset(io.ones_row_b[:], 1.0)
        for i, cv in enumerate(CONSTS):
            v.memset(io.cpool[:, i:i + 1], cv)
        return v.memset(io.idn[:], 0.0)
    s0 = p.step("vector", fn, name="const_memset")
    IDN = p.input("IDN", [128, 128], F32)

    def fn2():
        return p.dma(io.idn[:], IDN)
    io.s_const = p.step("gpsimd", fn2, deps=[s0], kind="dma", n_dma=1, name="idn_load")


def stream_in(p, io):
    XS = p.input("XS_IN", [1024, TOK], BF16)

    def fn():
        last = None
        for c in range(NCH):
            last = p.dma(io.cur[:, c * TOK:(c + 1) * TOK], XS[c * 128:(c + 1) * 128, :])
        return last
    io.s_stream = p.step("sync", fn, kind="dma", n_dma=NCH, name="stream_in")


def stream_out(p, io):
    XS = p.nc.dram_tensor("XS_OUT", [1024, TOK], BF16, kind="ExternalOutput").ap()
    dep = [p.last_step[n] for n in ENGINES]

    def fn():
        last = None
        for c in range(NCH):
            last = p.dma(XS[c * 128:(c + 1) * 128, :], io.cur[:, c * TOK:(c + 1) * TOK])
        return last
    s = p.step("sync", fn, deps=dep, kind="dma", n_dma=NCH, name="stream_out")
    p.step("sync", lambda: p.nc.sync.nop(), deps=[s], name="stream_out_wait")


def final_out(p, io):
    OUT = p.nc.dram_tensor("OUT", [1024, TOK], F32, kind="ExternalOutput").ap()
    dep = [p.last_step[n] for n in ENGINES]

    def fn():
        last = None
        for c in range(NCH):
            last = p.dma(OUT[c * 128:(c + 1) * 128, :], io.out_f32[:, c * TOK:(c + 1) * TOK])
        return last
    s = p.step("sync", fn, deps=dep, kind="dma", n_dma=NCH, name="final_out")
    p.step("sync", lambda: p.nc.sync.nop(), deps=[s], name="final_out_wait")


def emit_postln(p, io, ctx, y_deps_of_tt, final, tag):
    nc = p.nc
    e = ctx.enter_context
    y_sb = io.ybuf
    ysq = e(nc.sbuf_tensor(f"ln_ysq{tag}", [128, TT], BF16))
    ps_s = e(nc.psum_tensor(f"ln_ps_s{tag}", [1, TT], F32))
    ps_ss = e(nc.psum_tensor(f"ln_ps_ss{tag}", [1, TT], F32))
    stat = e(nc.sbuf_tensor(f"ln_stat{tag}", [1, NTT * TT], F32))
    stat2 = e(nc.sbuf_tensor(f"ln_stat2{tag}", [1, NTT * TT], F32))
    stat3 = e(nc.sbuf_tensor(f"ln_stat3{tag}", [1, NTT * TT], F32))
    ps_a = e(nc.psum_tensor(f"ln_ps_a{tag}", [128, TT], F32))
    ps_b = e(nc.psum_tensor(f"ln_ps_b{tag}", [128, TT], F32))
    a_sb = e(nc.sbuf_tensor(f"ln_a{tag}", [128, TT], BF16))
    b_sb = e(nc.sbuf_tensor(f"ln_b{tag}", [128, TT], BF16))
    tmp = e(nc.sbuf_tensor(f"ln_tmp{tag}", [128, TT], BF16))
    if final:
        io.out_f32 = e(nc.sbuf_tensor("out_f32", [128, NCH * TT], F32))
        OUT = p.nc.dram_tensor("OUT", [1024, TOK], F32, kind="ExternalOutput").ap()
    out_sb = io.out_f32 if final else io.cur

    apply_last = []
    prev_apply = None
    prev_bc = None
    for tt in range(NTT):
        dep_y = y_deps_of_tt(tt)

        def ysl(c, tt=tt):
            return y_sb[:, c * TOK + tt * TT: c * TOK + tt * TT + TT]

        def fn_sum(tt=tt, ysl=ysl):
            last = None
            for c in range(NCH):
                last = nc.tensor.matmul(ps_s[:], io.ones_col[:, 0:1],
                                        ysl(c), start=(c == 0), stop=(c == NCH - 1))
            return last
        s_sum = p.step("tensor", fn_sum, deps=dep_y + [io.s_const, prev_apply],
                       name=f"ln{tag}_sum{tt}")

        s_prev = prev_apply
        s_ssq = None
        for c in range(NCH):
            def fn_sq(c=c, ysl=ysl):
                return nc.scalar.activation(ysq[:], ysl(c), AF.Square)
            s_sq = p.step("scalar", fn_sq, deps=dep_y + [s_prev], name=f"ln{tag}_sq{tt}_{c}")

            def fn_ssq(c=c):
                return nc.tensor.matmul(ps_ss[:], io.ones_col[:, 0:1], ysq[:],
                                        start=(c == 0), stop=(c == NCH - 1))
            s_ssq = p.step("tensor", fn_ssq, deps=[s_sq, s_sum], name=f"ln{tag}_ssq{tt}_{c}")
            s_prev = s_ssq

        sl = slice(tt * TT, (tt + 1) * TT)

        def fn_mean(sl=sl):
            return nc.vector.tensor_scalar(stat[:, sl], ps_s[:], 1.0 / 1024.0, None, ALU.mult)
        s_mean = p.step("vector", fn_mean, deps=[s_sum], late=True, name=f"ln{tag}_mean{tt}")

        def fn_var(sl=sl):
            p.chain(nc.vector.tensor_mul(stat2[:, sl], stat[:, sl], stat[:, sl]))
            return nc.vector.scalar_tensor_tensor(
                stat2[:, sl], ps_ss[:], 1.0 / 1024.0, stat2[:, sl], ALU.mult, ALU.subtract)
        s_var = p.step("vector", fn_var, deps=[s_ssq, s_mean], late=True, name=f"ln{tag}_var{tt}")

        def fn_rstd(sl=sl):
            p.chain(nc.scalar.activation(stat2[:, sl], stat2[:, sl], AF.Ln, bias=1e-5))
            return nc.scalar.activation(stat2[:, sl], stat2[:, sl], AF.Exp, scale=-0.5)
        s_rstd = p.step("scalar", fn_rstd, deps=[s_var], late=True, name=f"ln{tag}_rstd{tt}")

        def fn_nmr(sl=sl):
            return nc.vector.scalar_tensor_tensor(
                stat3[:, sl], stat[:, sl], -1.0, stat2[:, sl], ALU.mult, ALU.mult)
        s_nmr = p.step("vector", fn_nmr, deps=[s_rstd, s_mean], late=True, name=f"ln{tag}_nmr{tt}")

        def fn_bc(sl=sl):
            nc.tensor.matmul(ps_a[:], io.ones_row[:], stat2[:, sl], start=True, stop=True)
            return nc.tensor.matmul(ps_b[:], io.ones_row[:], stat3[:, sl], start=True, stop=True)
        s_bc = p.step("tensor", fn_bc, deps=[s_rstd, s_nmr, prev_bc], name=f"ln{tag}_bc{tt}")

        def fn_ab():
            nc.scalar.copy(a_sb[:], ps_a[:])
            return nc.scalar.copy(b_sb[:], ps_b[:])
        s_ab = p.step("scalar", fn_ab, deps=[s_bc], name=f"ln{tag}_ab{tt}")

        s_app = None
        app_steps = []
        for c in range(NCH):
            def fn_app(c=c, tt=tt, ysl=ysl):
                if final:
                    osl = out_sb[:, c * TT:(c + 1) * TT]
                else:
                    osl = out_sb[:, c * TOK + tt * TT: c * TOK + tt * TT + TT]
                p.chain(nc.vector.tensor_mul(tmp[:], ysl(c), a_sb[:]))
                return nc.vector.tensor_add(osl, tmp[:], b_sb[:])
            s_app = p.step("vector", fn_app, deps=[s_ab, s_app], late=True, name=f"ln{tag}_app{tt}_{c}")
            app_steps.append(s_app)
        if final:
            def fn_od(tt=tt):
                last = None
                for c in range(NCH):
                    last = p.dma(OUT[c * 128:(c + 1) * 128, tt * TT:(tt + 1) * TT],
                                 out_sb[:, c * TT:(c + 1) * TT])
                return last
            s_od = p.step("sync", fn_od, deps=app_steps, kind="dma", n_dma=NCH,
                          name=f"ln{tag}_od{tt}")
            s_app = s_od
        prev_apply = s_app
        prev_bc = s_app
        apply_last.append(s_app)
    return apply_last


def load_weight(p, ctx, name, hbm_name, rows, cols, queue="gpsimd", dtype=F32,
                deps=(), col0=0, ncols=None):
    nc = p.nc
    Wap = p.input(hbm_name, [rows, cols], dtype)
    ncols = ncols if ncols is not None else cols
    rch = rows // 128
    sb = ctx.enter_context(nc.sbuf_tensor(f"w_{name}", [128, rch * ncols], BF16))

    def fn():
        last = None
        for c in range(rch):
            last = p.dma(sb[:, c * ncols:(c + 1) * ncols],
                         Wap[c * 128:(c + 1) * 128, col0:col0 + ncols])
        return last
    s = p.step(queue, fn, deps=list(deps), kind="dma", n_dma=rch, name=f"load_{name}")
    return sb, s



def phase_l0_attn(p, io):
    nc = p.nc
    ctx = ExitStack()
    e = ctx.enter_context
    NQB = TOK // 128

    X0 = p.input("X0", [1024, TH], F32)
    x0 = e(nc.sbuf_tensor("l0_x0", [128, NCH * TH], BF16))

    def fn_x0():
        last = None
        for c in range(NCH):
            last = p.dma(x0[:, c * TH:(c + 1) * TH], X0[c * 128:(c + 1) * 128, :])
        return last
    s_x0 = p.step("gpsimd", fn_x0, kind="dma", n_dma=NCH, name="l0_x0")
    x_res = lambda c: x0[:, c * TH + HALO: c * TH + HALO + TOK]

    BQ = p.input("B_qkv", [128, 24], F32)
    bq_sb = e(nc.sbuf_tensor("l0_bq", [128, 24], F32))
    BP = p.input("B_l0proj", [128, 8], F32)
    bp_sb = e(nc.sbuf_tensor("l0_bp", [128, 8], F32))

    def fn_b():
        p.dma(bq_sb[:], BQ)
        return p.dma(bp_sb[:], BP)
    s_b = p.step("sync", fn_b, kind="dma", n_dma=2, name="l0_b")

    WQKV = p.input("W_qkv", [1024, 3072], F32)
    qf = e(nc.sbuf_tensor("l0_qf", [128, NCH * TOK], BF16))
    kf = e(nc.sbuf_tensor("l0_kf", [128, NCH * TH], BF16))
    VS = nc.dram_tensor("l0_vs", [TH + 4, 1024], BF16)

    qkv_ctx = ExitStack()
    qe = qkv_ctx.enter_context
    wsec = [qe(nc.sbuf_tensor(f"l0_wsec{i}", [128, NCH * 1024], BF16)) for i in range(2)]
    vstage = qe(nc.sbuf_tensor("l0_vstage", [128, 2 * 1024], BF16))
    ps_pool = [qe(nc.psum_tensor(f"l0_ps{i}", [128, TT], F32)) for i in range(4)]
    sec_steps = []

    def mk_sec(sec, dep):
        def fn(sec=sec):
            last = None
            for c in range(NCH):
                last = p.dma(wsec[sec % 2][:, c * 1024:(c + 1) * 1024],
                             WQKV[c * 128:(c + 1) * 128, sec * 1024:(sec + 1) * 1024])
            return last
        return p.step("gpsimd", fn, deps=dep, kind="dma", n_dma=NCH, name=f"l0_wsec{sec}")

    sec_steps.append(mk_sec(0, []))
    sec_steps.append(mk_sec(1, []))

    q_steps = []
    for oc in range(NCH):
        for tt in range(NTT):
            def fn_mm(oc=oc, tt=tt):
                last = None
                for ic in range(NCH):
                    last = nc.tensor.matmul(
                        ps_pool[(oc * NTT + tt) % 2][:],
                        wsec[0][:, ic * 1024 + oc * 128: ic * 1024 + oc * 128 + 128],
                        x0[:, ic * TH + HALO + tt * TT: ic * TH + HALO + tt * TT + TT],
                        start=(ic == 0), stop=(ic == NCH - 1))
                return last
            dep = [sec_steps[0], s_x0, s_b] + ([q_steps[-2]] if len(q_steps) >= 2 else [])
            s_mm = p.step("tensor", fn_mm, deps=dep, name=f"l0_qmm_{oc}_{tt}")

            def fn_ev(oc=oc, tt=tt):
                return nc.vector.tensor_scalar(
                    qf[:, oc * TOK + tt * TT: oc * TOK + tt * TT + TT],
                    ps_pool[(oc * NTT + tt) % 2][:],
                    bq_sb[:, oc:oc + 1], 0.125, ALU.add, ALU.mult)
            q_steps.append(p.step("vector", fn_ev, deps=[s_mm], name=f"l0_qev_{oc}_{tt}"))

    sec_steps.append(mk_sec(2, [q_steps[-1]]))

    KT = [(0, 512), (512, 512), (1024, 512), (1536, 512), (2048, 12)]
    k_steps = []
    kidx = 0
    for oc in range(NCH):
        for ti, (t0, tl) in enumerate(KT):
            def fn_mm(oc=oc, t0=t0, tl=tl, kidx=kidx):
                last = None
                for ic in range(NCH):
                    last = nc.tensor.matmul(
                        ps_pool[2 + kidx % 2][:, 0:tl],
                        wsec[1][:, ic * 1024 + oc * 128: ic * 1024 + oc * 128 + 128],
                        x0[:, ic * TH + t0: ic * TH + t0 + tl],
                        start=(ic == 0), stop=(ic == NCH - 1))
                return last
            dep = [sec_steps[1], s_x0] + ([k_steps[-2]] if len(k_steps) >= 2 else [])
            s_mm = p.step("tensor", fn_mm, deps=dep, name=f"l0_kmm_{oc}_{ti}")

            def fn_ev(oc=oc, t0=t0, tl=tl, kidx=kidx):
                return nc.vector.tensor_copy(
                    kf[:, oc * TH + t0: oc * TH + t0 + tl],
                    ps_pool[2 + kidx % 2][:, 0:tl])
            k_steps.append(p.step("vector", fn_ev, deps=[s_mm], name=f"l0_kev_{oc}_{ti}"))
            kidx += 1

    v_sp = []
    NVC = 17
    for tc in range(NVC):
        t0 = tc * 128
        tl = min(128, TH - t0)
        ev_pair = []
        for nn in range(2):
            def fn_mm(t0=t0, tl=tl, nn=nn, tc=tc):
                last = None
                for ic in range(NCH):
                    last = nc.tensor.matmul(
                        ps_pool[(tc * 2 + nn) % 2][0:tl, 0:512],
                        x0[:, ic * TH + t0: ic * TH + t0 + tl],
                        wsec[0][:, ic * 1024 + nn * 512: ic * 1024 + nn * 512 + 512],
                        start=(ic == 0), stop=(ic == NCH - 1))
                return last
            dep = [sec_steps[2], s_x0]
            if tc == 0 and nn == 0:
                dep += [q_steps[-1], q_steps[-2]]
            if len(v_sp) >= 1:
                dep += [v_sp[-1]]
            s_mm = p.step("tensor", fn_mm, deps=dep, name=f"l0_vmm_{tc}_{nn}")

            def fn_ev(tl=tl, nn=nn, tc=tc):
                return nc.vector.tensor_copy(
                    vstage[0:tl, (tc % 2) * 1024 + nn * 512:(tc % 2) * 1024 + nn * 512 + 512],
                    ps_pool[(tc * 2 + nn) % 2][0:tl, 0:512])
            ev_pair.append(p.step("vector", fn_ev, deps=[s_mm], name=f"l0_vev_{tc}_{nn}"))

        def fn_sp(t0=t0, tl=tl, tc=tc):
            return p.dma(VS[t0:t0 + tl, :], vstage[0:tl, (tc % 2) * 1024:(tc % 2) * 1024 + 1024])
        v_sp.append(p.step("gpsimd", fn_sp, deps=ev_pair, kind="dma", n_dma=1, name=f"l0_vsp_{tc}"))

    qkv_done = [q_steps[-1], k_steps[-1], v_sp[-1]]
    qkv_ctx.close()

    w_pr, s_wpr = load_weight(p, ctx, "l0proj", "W_l0proj", 1024, 1024, queue="gpsimd",
                              deps=qkv_done)
    vb_bf = e(nc.sbuf_tensor("l0_vb", [128, 8], BF16))
    pb_eff = e(nc.sbuf_tensor("l0_pbeff", [128, 8], F32))

    att_ctx = ExitStack()
    ae = att_ctx.enter_context
    ps_ob = [ae(nc.psum_tensor(f"l0_ob{i}", [128, 512], F32)) for i in range(4)]
    ps_oh = [ps_ob[h // 4][:, (h % 4) * 128:(h % 4 + 1) * 128] for h in range(16)]
    ps_stb = [ae(nc.psum_tensor(f"l0_stb{i}", [128, 512], F32)) for i in range(2)]
    ps_st1 = [ps_stb[i][:, 0:128] for i in range(2)]
    ps_st2 = [ps_stb[i][0:12, 256:384] for i in range(2)]
    ps_rb = [ae(nc.psum_tensor(f"l0_rb_{i}", [128, 512], F32)) for i in range(2)]
    p1 = ae(nc.sbuf_tensor("l0_p1", [128, 2 * 128], BF16))
    vw = ae(nc.sbuf_tensor("l0_vw", [128, 1024], BF16))
    lrec = ae(nc.sbuf_tensor("l0_lrec", [1, 1024], F32))
    rb_sb = ae(nc.sbuf_tensor("l0_rb_sb", [128, 1024], BF16))
    of = io.xa

    expb1 = ae(nc.sbuf_tensor("l0_expb1", [128, 2 * 16 * 128], BF16))
    expb2_t = ae(nc.sbuf_tensor("l0_expb2", [12, 2 * 16 * 128], BF16))
    expb2 = expb2_t[:, :]
    vw2_t = ae(nc.sbuf_tensor("l0_vw2", [12, 1024], BF16))
    vw2 = vw2_t[:, :]
    p2_t = ae(nc.sbuf_tensor("l0_p2b", [12, 256], BF16))
    p2 = p2_t[:, :]
    RV = [p.input("RPB_INT", [16, 140, 128]), p.input("RPB_Q0", [16, 140, 128]),
          p.input("RPB_Q15", [16, 140, 128])]

    def mk_rpb_load(vslot, vsrc, deps):
        def fn():
            last = None
            for h in range(16):
                sl = slice(vslot * 2048 + h * 128, vslot * 2048 + h * 128 + 128)
                p.dma(expb1[:, sl], RV[vsrc][h, 0:128, :])
                last = p.dma(expb2[:, sl], RV[vsrc][h, 128:140, :])
            return last
        s_l = p.step("gpsimd", fn, deps=deps, kind="dma", n_dma=32, name=f"l0_rpb{vsrc}")

        def fn_e():
            sl = slice(vslot * 2048, (vslot + 1) * 2048)
            nc.scalar.activation(expb1[:, sl], expb1[:, sl], AF.Exp)
            return nc.scalar.activation(expb2[:, sl], expb2[:, sl], AF.Exp)
        return p.step("scalar", fn_e, deps=[s_l], name=f"l0_expb{vsrc}")

    s_expb0 = mk_rpb_load(0, 0, qkv_done)
    s_expb1 = mk_rpb_load(1, 1, qkv_done)
    s_expb = s_expb1
    s_expb_q15 = None


    prev = {}
    o_evac_by_qb = {}
    qb0_pms = []
    for qb in range(NQB):
        s0 = qb * 128
        variant = 1 if (qb == 0 or qb == NQB - 1) else 0
        if qb == NQB - 1:
            s_expb_q15 = mk_rpb_load(1, 2, qb0_pms)

        def fn_vw(s0=s0):
            p.dma(vw[:], VS[s0:s0 + 128, :])
            return p.dma(vw2[:], VS[s0 + 128:s0 + 140, :])
        dep_vw = [v_sp[min(s0 // 128 + 1, NVC - 1)], v_sp[-1]]
        if qb >= 1:
            dep_vw += o_evac_by_qb[qb - 1]
        s_vw = p.step("gpsimd", fn_vw, deps=dep_vw, kind="dma", n_dma=2, name=f"l0_vw_{qb}")

        l_deps = []
        for h in range(16):
            hc, hr = h // 2, (h % 2) * 64
            bi = h % 2

            def fn_st(s0=s0, hc=hc, hr=hr, bi=bi):
                kbase = hc * TH + s0
                qsl = qf[hr:hr + 64, hc * TOK + s0: hc * TOK + s0 + 128]
                nc.tensor.matmul(ps_st1[bi][:], kf[hr:hr + 64, kbase:kbase + 128], qsl,
                                 start=True, stop=True)
                return nc.tensor.matmul(ps_st2[bi][:], kf[hr:hr + 64, kbase + 128:kbase + 140],
                                        qsl, start=True, stop=True)
            dep = list(qkv_done)
            pk = prev.get(("exp", h - 2, qb)) or (prev.get(("exp", h + 14, qb - 1)) if qb else None)
            if pk is not None:
                dep.append(pk)
            s_st = p.step("tensor", fn_st, deps=dep, name=f"l0_st_{qb}_{h}")

            def fn_exp(bi=bi):
                nc.scalar.activation(p1[:, bi * 128:(bi + 1) * 128], ps_st1[bi][:], AF.Exp)
                return nc.scalar.activation(p2[:, bi * 128:(bi + 1) * 128], ps_st2[bi][:], AF.Exp)
            dep_e = [s_st, s_expb]
            pe = prev.get(("omm", h - 2, qb)) or (prev.get(("omm", h + 14, qb - 1)) if qb else None)
            if pe is not None:
                dep_e.append(pe)
            s_exp = p.step("scalar", fn_exp, deps=dep_e, name=f"l0_exp_{qb}_{h}")
            prev[("exp", h, qb)] = s_exp

            def fn_pm(bi=bi, h=h, variant=variant):
                esl = slice(variant * 2048 + h * 128, variant * 2048 + h * 128 + 128)
                nc.vector.tensor_mul(p1[:, bi * 128:(bi + 1) * 128],
                                     p1[:, bi * 128:(bi + 1) * 128], expb1[:, esl])
                return nc.vector.tensor_mul(p2[:, bi * 128:(bi + 1) * 128],
                                            p2[:, bi * 128:(bi + 1) * 128], expb2[:, esl])
            dep_pm = [s_exp, s_expb0, s_expb]
            if qb == NQB - 1:
                dep_pm.append(s_expb_q15)
            s_pm = p.step("vector", fn_pm, deps=dep_pm, name=f"l0_pm_{qb}_{h}")
            if qb == 0:
                qb0_pms.append(s_pm)

            def fn_o(h=h, bi=bi):
                po = ps_oh[h]
                nc.tensor.matmul(po[0:64, :], vw[:, h * 64:(h + 1) * 64],
                                 p1[:, bi * 128:(bi + 1) * 128], start=True, stop=False)
                nc.tensor.matmul(po[0:64, :], vw2[:, h * 64:(h + 1) * 64],
                                 p2[:, bi * 128:(bi + 1) * 128], start=False, stop=True)
                nc.tensor.matmul(po[64:65, :], io.ones_col[:, 0:1],
                                 p1[:, bi * 128:(bi + 1) * 128], start=True, stop=False)
                return nc.tensor.matmul(po[64:65, :], io.ones_col[0:12, 1:2],
                                        p2[:, bi * 128:(bi + 1) * 128], start=False, stop=True)
            dep_o = [s_pm, s_vw]
            if qb >= 1:
                dep_o.append(prev[("oevbank", h // 4, qb - 1)])
            s_o = p.step("tensor", fn_o, deps=dep_o, name=f"l0_omm_{qb}_{h}")
            prev[("omm", h, qb)] = s_o
            l_deps.append(s_o)

        evs = []
        for i in range(4):
            gb = qb * 4 + i
            sl = (gb % 2) * 512

            def fn_lr(i=i, sl=sl):
                last = None
                for j in range(4):
                    h = i * 4 + j
                    last = nc.vector.reciprocal(lrec[:, sl + j * 128: sl + (j + 1) * 128],
                                                ps_oh[h][64:65, :])
                return last
            dep_lr = [prev[("omm", i * 4 + j, qb)] for j in range(4)]
            dep_lr.append(prev.get(("rbmm", gb - 2)))
            s_lr = p.step("vector", fn_lr, deps=dep_lr, late=True, name=f"l0_lrec_{qb}_{i}")

            def fn_r(sl=sl, gb=gb):
                return nc.tensor.matmul(ps_rb[gb % 2][:], io.ones_row[:],
                                        lrec[:, sl:sl + 512], start=True, stop=True)
            s_r = p.step("tensor", fn_r, deps=[s_lr, prev.get(("rbc", gb - 2))],
                         name=f"l0_rb_{qb}_{i}")
            prev[("rbmm", gb)] = s_r

            def fn_c(sl=sl, gb=gb):
                return nc.scalar.copy(rb_sb[:, sl:sl + 512], ps_rb[gb % 2][:])
            s_c = p.step("scalar", fn_c, deps=[s_r, prev.get(("oevb", gb - 2))],
                         name=f"l0_rbc_{qb}_{i}")
            prev[("rbc", gb)] = s_c

            for j in range(4):
                h = i * 4 + j

                def fn_oe(h=h, qb=qb, sl=sl, j=j):
                    r0 = (h % 2) * 64
                    return nc.vector.tensor_mul(
                        of[r0:r0 + 64, (h // 2) * TOK + qb * 128:(h // 2) * TOK + qb * 128 + 128],
                        ps_oh[h][0:64, :],
                        rb_sb[r0:r0 + 64, sl + j * 128: sl + (j + 1) * 128])
                s_oe = p.step("vector", fn_oe, deps=[s_c, prev[("omm", h, qb)]],
                              name=f"l0_oev_{qb}_{h}")
                prev[("oev", h, qb)] = s_oe
                evs.append(s_oe)
            prev[("oevb", gb)] = evs[-1]
            prev[("oevbank", i, qb)] = evs[-1]
        o_evac_by_qb[qb] = evs

    att_done = [o_evac_by_qb[NQB - 1][h] for h in range(16)]
    att_ctx.close()

    pr_ctx = ExitStack()
    pe_ = pr_ctx.enter_context
    ps_pr = [pe_(nc.psum_tensor(f"l0_pr{i}", [128, TT], F32)) for i in range(2)]
    ps_pb = pe_(nc.psum_tensor("l0_pspb", [128, 8], F32))

    def fn_vb():
        return nc.vector.tensor_copy(vb_bf[:], bq_sb[:, 16:24])
    s_vb = p.step("vector", fn_vb, deps=[s_b], name="l0_vb")

    def fn_pbe():
        last = None
        for oc in range(NCH):
            for ic in range(NCH):
                last = nc.tensor.matmul(
                    ps_pb[:, oc:oc + 1],
                    w_pr[:, (ic * 8 + oc) * 128:(ic * 8 + oc) * 128 + 128],
                    vb_bf[:, ic:ic + 1], start=(ic == 0), stop=(ic == NCH - 1))
        return last
    s_pbe = p.step("tensor", fn_pbe, deps=[s_vb, s_wpr, att_done[-1]], name="l0_pbe")

    def fn_pbe2():
        return nc.vector.tensor_add(pb_eff[:], ps_pb[:], bp_sb[:])
    s_pbe2 = p.step("vector", fn_pbe2, deps=[s_pbe], late=True, name="l0_pbe2")

    y_sb = io.ybuf
    ev_by_tt = {tt: [] for tt in range(NTT)}
    busy = {}
    k = 0
    for tt in range(NTT):
        for oc in range(NCH):
            ps = ps_pr[k % 2]

            def fn_mm(oc=oc, tt=tt, ps=ps):
                last = None
                for ic in range(NCH):
                    last = nc.tensor.matmul(
                        ps[:],
                        w_pr[:, (ic * 8 + oc) * 128:(ic * 8 + oc) * 128 + 128],
                        of[:, ic * TOK + tt * TT: ic * TOK + tt * TT + TT],
                        start=(ic == 0), stop=(ic == NCH - 1))
                return last
            dep = [s_wpr, s_pbe] + att_done + ([busy[k % 2]] if k >= 2 else [])
            s_mm = p.step("tensor", fn_mm, deps=dep, name=f"l0_pr_{tt}_{oc}")

            def fn_ev(oc=oc, tt=tt, ps=ps):
                sl = slice(oc * TOK + tt * TT, oc * TOK + tt * TT + TT)
                return nc.vector.scalar_tensor_tensor(
                    y_sb[:, sl], ps[:], pb_eff[:, oc:oc + 1],
                    x_res(oc)[:, tt * TT:tt * TT + TT], ALU.add, ALU.add)
            s_ev = p.step("vector", fn_ev, deps=[s_mm, s_pbe2], late=True,
                          name=f"l0_prev_{tt}_{oc}")
            busy[k % 2] = s_ev
            ev_by_tt[tt].append(s_ev)
            k += 1

    def ydeps(tt):
        d = list(ev_by_tt[tt])
        if tt + 1 < NTT:
            pass
        return d
    emit_postln(p, io, pr_ctx, ydeps, final=(io.cur_phase == TOTAL_PHASES - 1), tag="p0")
    pr_ctx.close()
    ctx.close()




def phase_l0_ffn(p, io):
    nc = p.nc
    ctx = ExitStack()
    e = ctx.enter_context
    x = io.cur

    def xsl(ic, t0, tl=512):
        return x[:, ic * TOK + t0: ic * TOK + t0 + tl]

    WG = p.input("W_gate_r", [1024, 16], F32)
    EB = p.input("EBIAS", [128, 16], F32)
    comb_fm = e(nc.sbuf_tensor("r_comb", [16, TOK], BF16))
    oh_sb = e(nc.sbuf_tensor("r_oh", [16, 16 * 128], BF16))
    rs_ctx = ExitStack()
    re_ = rs_ctx.enter_context
    wg_sb = re_(nc.sbuf_tensor("r_wg", [128, NCH * 16], BF16))
    eb_sb = re_(nc.sbuf_tensor("r_eb", [128, 16], F32))
    sct = re_(nc.sbuf_tensor("r_sct", [128, 256], F32))
    selv = re_(nc.sbuf_tensor("r_sel", [128, 256], F32))
    selo = re_(nc.sbuf_tensor("r_selo", [128, 256], F32))
    eqb = re_(nc.sbuf_tensor("r_eq", [128, 256], F32))
    smb = re_(nc.sbuf_tensor("r_sm", [128, 256], F32))
    rcp = re_(nc.sbuf_tensor("r_rcp", [128, 16], F32))
    comb_tm = re_(nc.sbuf_tensor("r_combtm", [128, 256], BF16))
    BIG = 65536.0

    OH = p.input("ONEHOT16", [16, 16 * 128], F32)
    rt_ctx = ExitStack()
    ps_r = rt_ctx.enter_context(nc.psum_tensor("r_ps", [128, 256], F32))
    ps_t = [rt_ctx.enter_context(nc.psum_tensor(f"r_pst{i}", [16, 128], BF16))
            for i in range(2)]

    def fn_rwl():
        p.dma(eb_sb[:], EB)
        p.dma(oh_sb[:], OH)
        last = None
        for c in range(NCH):
            last = p.dma(wg_sb[:, c * 16:(c + 1) * 16], WG[c * 128:(c + 1) * 128, :])
        return last
    s_rwl = p.step("gpsimd", fn_rwl, kind="dma", n_dma=10, name="rt_wl")

    def fn_gmm():
        last = None
        for tc in range(16):
            for ic in range(NCH):
                last = nc.tensor.matmul(
                    ps_r[:, tc * 16:(tc + 1) * 16],
                    x[:, ic * TOK + tc * 128: ic * TOK + tc * 128 + 128],
                    wg_sb[:, ic * 16:(ic + 1) * 16],
                    start=(ic == 0), stop=(ic == NCH - 1))
        return last
    s_gmm = p.step("tensor", fn_gmm, deps=[s_rwl], name="rt_gmm")

    def fn_sig():
        return nc.scalar.activation(sct[:], ps_r[:], AF.Sigmoid)
    s_sig = p.step("scalar", fn_sig, deps=[s_gmm], name="rt_sig")

    g3 = lambda ap: ap.rearrange("p (t e) -> p t e", e=16)

    def fn_top8():
        ebb = eb_sb[:].unsqueeze(1).broadcast_to([128, 16, 16])
        p.chain(nc.vector.tensor_tensor(g3(selv[:]), g3(sct[:]), ebb, ALU.add))
        p.chain(nc.vector.tensor_copy(selo[:], selv[:]))
        for it in range(8):
            p.chain(nc.vector.tensor_reduce(rcp[:], g3(selv[:]), AX.X, ALU.max))
            p.chain(nc.vector.tensor_tensor(
                g3(eqb[:]), g3(selv[:]),
                rcp[:].unsqueeze(2).broadcast_to([128, 16, 16]), ALU.is_equal))
            p.chain(nc.vector.scalar_tensor_tensor(
                selv[:], eqb[:], -BIG, selv[:], ALU.mult, ALU.add))
        p.chain(nc.vector.tensor_sub(eqb[:], selo[:], selv[:]))
        p.chain(nc.vector.tensor_scalar(eqb[:], eqb[:], BIG / 2, None, ALU.is_gt))
        p.chain(nc.vector.tensor_mul(smb[:], sct[:], eqb[:]))
        p.chain(nc.vector.tensor_reduce(rcp[:], g3(smb[:]), AX.X, ALU.add))
        p.chain(nc.vector.reciprocal(rcp[:], rcp[:]))
        p.chain(nc.vector.scalar_tensor_tensor(
            g3(smb[:]), g3(smb[:]), 2.826,
            rcp[:].unsqueeze(2).broadcast_to([128, 16, 16]), ALU.mult, ALU.mult))
        return nc.vector.tensor_copy(comb_tm[:], smb[:])
    s_top8 = p.step("vector", fn_top8, deps=[s_sig, io.s_const], late=True, name="rt_top8")

    tr_steps = []
    for tc in range(16):
        def fn_tr(tc=tc):
            return nc.tensor.transpose(ps_t[tc % 2][:],
                                       comb_tm[:, tc * 16:(tc + 1) * 16], io.idn[:])
        dep = [s_top8, io.s_const] + ([tr_steps[-2][1]] if len(tr_steps) >= 2 else [])
        s_tr = p.step("tensor", fn_tr, deps=[d for d in dep if d is not None], name=f"rt_tr{tc}")

        def fn_te(tc=tc):
            return nc.vector.tensor_copy(comb_fm[0:16, tc * 128:(tc + 1) * 128],
                                         ps_t[tc % 2][:])
        s_te = p.step("vector", fn_te, deps=[s_tr], name=f"rt_te{tc}")
        tr_steps.append((s_tr, s_te))
    s_comb = tr_steps[-1][1]
    rt_ctx.close()
    rs_ctx.close()

    W_EG = p.input("W_eg", [16, 1024, 1024], F32)
    W_EU = p.input("W_eu", [16, 1024, 1024], F32)
    W_ED = p.input("W_ed", [16, 1024, 1024], F32)
    W_SG = p.input("W_sg", [1024, 1024], F32)
    W_SU = p.input("W_su", [1024, 1024], F32)
    W_SD = p.input("W_sd", [1024, 1024], F32)

    y_acc = e(nc.sbuf_tensor("ff_y", [128, NCH * 1024], F32))
    h_sb = e(nc.sbuf_tensor("ff_h", [128, NCH * 512], BF16))
    hh = e(nc.sbuf_tensor("ff_hh", [128, 2 * 512], BF16))
    cb_sb = e(nc.sbuf_tensor("ff_cb", [128, 2 * 512], BF16))

    wctx = ExitStack()
    we = wctx.enter_context
    wgu = [we(nc.sbuf_tensor(f"ff_wgu{i}", [128, 16 * 1024], BF16)) for i in range(2)]
    wd = [we(nc.sbuf_tensor("ff_wd0", [128, 8 * 1024], BF16))]
    ps_g = [we(nc.psum_tensor(f"ff_psg{i}", [128, 512], F32)) for i in range(2)]
    ps_u = [we(nc.psum_tensor(f"ff_psu{i}", [128, 512], F32)) for i in range(2)]
    ps_d = [we(nc.psum_tensor(f"ff_psd{i}", [128, 512], F32)) for i in range(2)]
    ps_cb = we(nc.psum_tensor("ff_pscb", [128, 512], F32))

    gu_reader = [None, None]
    d_reader = [None, None]
    cb_reader = {}
    prev_silu = {}
    prev_h1 = {}
    prev_hc = {}
    prev_y = {}
    prev_cbc = None
    prev_cbm = None
    yform = {}
    wl = {}

    order = [(TB, j) for TB in range(2) for j in range(17)]
    for oi_, (TB, j) in enumerate(order):
        ex = j - 1
        buf = oi_ % 2
        bufd = 0

        def fn_wl(ex=ex, buf=buf):
            last = None
            srcs = ((W_SG, W_SU) if ex < 0 else (W_EG[ex], W_EU[ex]))
            for s, SRC in enumerate(srcs):
                for c in range(NCH):
                    last = p.dma(wgu[buf][:, s * 8192 + c * 1024: s * 8192 + (c + 1) * 1024],
                                 SRC[c * 128:(c + 1) * 128, :])
            return last
        wl[oi_] = p.step("gpsimd", fn_wl, deps=[gu_reader[buf]], kind="dma",
                         n_dma=16, name=f"ff_wl{oi_}")

        def fn_wld(ex=ex, bufd=bufd):
            last = None
            SRC = W_SD if ex < 0 else W_ED[ex]
            for c in range(NCH):
                last = p.dma(wd[bufd][:, c * 1024:(c + 1) * 1024],
                             SRC[c * 128:(c + 1) * 128, :])
            return last
        wld = p.step("gpsimd", fn_wld, deps=[d_reader[bufd]], kind="dma",
                     n_dma=8, name=f"ff_wld{oi_}")

        last_d_step = None
        for st in range(2):
            t0 = TB * 1024 + st * 512
            sl5 = slice(st * 512, st * 512 + 512)
            if ex >= 0:
                def fn_cbm(ex=ex, t0=t0):
                    return nc.tensor.matmul(ps_cb[:], oh_sb[0:16, ex * 128:(ex + 1) * 128],
                                            comb_fm[0:16, t0:t0 + 512],
                                            start=True, stop=True)
                s_cbm = p.step("tensor", fn_cbm, deps=[s_comb, prev_cbc, s_rwl],
                               name=f"ff_cbm{oi_}_{st}")
                prev_cbm = s_cbm

                def fn_cbc(st=st, sl5=sl5):
                    return nc.scalar.copy(cb_sb[:, sl5], ps_cb[:])
                s_cbc = p.step("scalar", fn_cbc,
                               deps=[s_cbm, cb_reader.get(st)], name=f"ff_cbc{oi_}_{st}")
                prev_cbc = s_cbc
            else:
                s_cbc = None

            h_steps = []
            for oi in range(8):
                k = oi % 2

                def fn_g(oi=oi, t0=t0, buf=buf, k=k):
                    last = None
                    for ic in range(NCH):
                        last = nc.tensor.matmul(
                            ps_g[k][:],
                            wgu[buf][:, ic * 1024 + oi * 128: ic * 1024 + oi * 128 + 128],
                            xsl(ic, t0), start=(ic == 0), stop=(ic == NCH - 1))
                    return last
                dep_g = [wl[oi_], prev_silu.get(k)]
                if oi_ == 0:
                    dep_g.append(s_comb)
                s_g = p.step("tensor", fn_g, deps=dep_g, name=f"ff_g{oi_}_{st}_{oi}")

                def fn_u(oi=oi, t0=t0, buf=buf, k=k):
                    last = None
                    for ic in range(NCH):
                        last = nc.tensor.matmul(
                            ps_u[k][:],
                            wgu[buf][:, 8192 + ic * 1024 + oi * 128: 8192 + ic * 1024 + oi * 128 + 128],
                            xsl(ic, t0), start=(ic == 0), stop=(ic == NCH - 1))
                    return last
                dep_u = [wl[oi_], prev_h1.get(k)]
                if oi_ == 0:
                    dep_u.append(s_comb)
                s_u = p.step("tensor", fn_u, deps=dep_u, name=f"ff_u{oi_}_{st}_{oi}")
                gu_reader[buf] = s_u

                def fn_silu(k=k):
                    return nc.scalar.activation(hh[:, k * 512:(k + 1) * 512], ps_g[k][:], AF.Silu)
                s_silu = p.step("scalar", fn_silu, deps=[s_g, prev_h1.get(k)],
                                name=f"ff_si{oi_}_{st}_{oi}")
                prev_silu[k] = s_silu

                def fn_h1(oi=oi, k=k):
                    return nc.vector.tensor_mul(h_sb[:, oi * 512:(oi + 1) * 512],
                                                hh[:, k * 512:(k + 1) * 512], ps_u[k][:])
                dep_h1 = [s_silu, s_u, last_d_step, prev_hc.get(oi)]
                s_h1 = p.step("vector", fn_h1, deps=dep_h1, name=f"ff_h1{oi_}_{st}_{oi}")
                prev_h1[k] = s_h1

                if ex >= 0:
                    def fn_hc(oi=oi, sl5=sl5):
                        return nc.gpsimd.tensor_mul(h_sb[:, oi * 512:(oi + 1) * 512],
                                                    h_sb[:, oi * 512:(oi + 1) * 512],
                                                    cb_sb[:, sl5])
                    s_hc = p.step("gpsimd", fn_hc, deps=[s_h1, s_cbc], name=f"ff_hc{oi_}_{st}_{oi}")
                else:
                    s_hc = s_h1
                prev_hc[oi] = s_hc
                h_steps.append(s_hc)
            cb_reader[st] = h_steps[-1]

            for od in range(8):
                k = od % 2

                def fn_d(od=od, bufd=bufd, k=k):
                    last = None
                    for ic in range(NCH):
                        last = nc.tensor.matmul(
                            ps_d[k][:],
                            wd[bufd][:, ic * 1024 + od * 128: ic * 1024 + od * 128 + 128],
                            h_sb[:, ic * 512:(ic + 1) * 512],
                            start=(ic == 0), stop=(ic == NCH - 1))
                    return last
                s_d = p.step("tensor", fn_d,
                             deps=h_steps + [prev_y.get(k), wld], name=f"ff_d{oi_}_{st}_{od}")
                last_d_step = s_d
                d_reader[bufd] = s_d

                def fn_y(od=od, st=st, TB=TB, k=k, ex=ex):
                    ysl = y_acc[:, od * 1024 + st * 512: od * 1024 + st * 512 + 512]
                    if ex < 0:
                        return nc.vector.tensor_copy(ysl, ps_d[k][:])
                    return nc.vector.tensor_add(ysl, ysl, ps_d[k][:])
                dep_y = [s_d, prev_y.get(("ysl", od, st))]
                if ex < 0 and TB == 1:
                    dep_y += yform.get(0, []) + yform.get(1, [])
                s_y = p.step("vector", fn_y, deps=dep_y, late=True, name=f"ff_y{oi_}_{st}_{od}")
                prev_y[k] = s_y
                prev_y[("ysl", od, st)] = s_y

        if j == 16:
            for st in range(2):
                tt = TB * 2 + st
                steps = []
                for c in range(NCH):
                    def fn_yf(c=c, st=st, TB=TB, tt=tt):
                        ysl = y_acc[:, c * 1024 + st * 512: c * 1024 + st * 512 + 512]
                        osl = io.ybuf[:, c * TOK + tt * TT: c * TOK + tt * TT + TT]
                        return nc.vector.tensor_add(
                            osl, ysl, x[:, c * TOK + tt * TT: c * TOK + tt * TT + TT])
                    steps.append(p.step("vector", fn_yf,
                                        deps=[prev_y[("ysl", c, st)]], late=True,
                                        name=f"ff_yf{TB}_{st}_{c}"))
                yform[tt] = steps

    wctx.close()

    ln_ctx = ExitStack()
    emit_postln(p, io, ln_ctx, lambda tt: yform[tt],
                final=(io.cur_phase == TOTAL_PHASES - 1), tag="p1")
    ln_ctx.close()
    ctx.close()





def phase_l1_attn(p, io):
    nc = p.nc
    ctx = ExitStack()
    e = ctx.enter_context
    x = io.cur
    NC16 = 16
    SC = 128 ** -0.5

    def xsl(ic, t0, tl=512):
        return x[:, ic * TOK + t0: ic * TOK + t0 + tl]

    FLAG = p.input("PFLAG", [128, 1], F32)
    TRI = p.input("TRIU", [128, 128], F32)
    GNW = p.input("GNW", [128, 8], F32)
    flag_sb = e(nc.sbuf_tensor("l1_flag", [128, 1], F32))
    tri_sb = e(nc.sbuf_tensor("l1_tri", [128, 128], BF16))
    gnw_sb = e(nc.sbuf_tensor("l1_gnw", [128, 8], F32))

    def fn_cst():
        p.dma(flag_sb[:], FLAG)
        p.dma(gnw_sb[:], GNW)
        return p.dma(tri_sb[:], TRI)
    s_cst = p.step("gpsimd", fn_cst, kind="dma", n_dma=3, name="l1_cst")
    ones_big = e(nc.sbuf_tensor("l1_ones", [128, TOK], BF16))

    def fn_zos():
        p.chain(nc.vector.memset(ones_big[:], 1.0))
        return nc.vector.memset(send_sb[:], 0.0)
    s_szero = p.step("vector", fn_zos, name="l1_szero")

    v_tm = e(nc.sbuf_tensor("l1_vtm", [128, NC16 * 1024], BF16))
    g1_sb = e(nc.sbuf_tensor("l1_g1", [128, TOK], BF16))
    N_sb = e(nc.sbuf_tensor("l1_nsb", [128, 128], BF16))
    nacc = e(nc.sbuf_tensor("l1_nacc", [128, 128], F32))
    recv_sb = e(nc.sbuf_tensor("l1_recv", [128, 132], F32))
    dh_col = e(nc.sbuf_tensor("l1_dhc", [128, 1], F32))
    send_sb = e(nc.sbuf_tensor("l1_send", [128, 132], F32))
    ek_sb = e(nc.sbuf_tensor("l1_ek", [128, TOK], BF16))
    ekT_sb = e(nc.sbuf_tensor("l1_ekt", [128, TOK], BF16))
    qs_sb = e(nc.sbuf_tensor("l1_qs", [128, TOK], F32))
    d_sb = e(nc.sbuf_tensor("l1_d", [128, TOK], F32))
    qp_sb = e(nc.sbuf_tensor("l1_qp", [128, TOK], BF16))
    p_sb = e(nc.sbuf_tensor("l1_pp", [128, 2 * 128], BF16))

    w_q, s_wq = load_weight(p, ctx, "l1q", "W_q1", 1024, 1024)
    w_k, s_wk = load_weight(p, ctx, "l1k", "W_k1", 1024, 1024)
    w_g1, s_wg1 = load_weight(p, ctx, "l1g1", "W_g1", 1024, 128)
    w_g2, s_wg2 = load_weight(p, ctx, "l1g2", "W_g2", 128, 1024)

    cc_in = [nc.dram_tensor(f"l1cc_in{h}", [128, 132], F32) for h in range(8)]
    cc_out = [nc.dram_tensor(f"l1cc_out{h}", [256, 132], F32) for h in range(8)]

    va_ctx = ExitStack()
    w_v, s_wv = load_weight(p, va_ctx, "l1v", "W_v1", 1024, 1024)
    ps_a = [va_ctx.enter_context(nc.psum_tensor(f"l1_psa{i}", [128, 512], F32))
            for i in range(2)]
    v_steps = []
    vi = 0
    for tc in range(NC16):
        for nn in range(2):
            def fn_mm(tc=tc, nn=nn, vi=vi):
                last = None
                for ic in range(NCH):
                    last = nc.tensor.matmul(
                        ps_a[vi % 2][:],
                        x[:, ic * TOK + tc * 128: ic * TOK + tc * 128 + 128],
                        w_v[:, ic * 1024 + nn * 512: ic * 1024 + nn * 512 + 512],
                        start=(ic == 0), stop=(ic == NCH - 1))
                return last
            dep = [s_wv] + ([v_steps[-2]] if len(v_steps) >= 2 else [])
            s_mm = p.step("tensor", fn_mm, deps=dep, name=f"l1_vmm_{tc}_{nn}")

            def fn_ev(tc=tc, nn=nn, vi=vi):
                return nc.vector.tensor_copy(
                    v_tm[:, tc * 1024 + nn * 512: tc * 1024 + nn * 512 + 512],
                    ps_a[vi % 2][:])
            v_steps.append(p.step("vector", fn_ev, deps=[s_mm], name=f"l1_vev_{tc}_{nn}"))
            vi += 1

    g1_steps = []
    for tt in range(NTT):
        def fn_mm(tt=tt, vi=vi):
            last = None
            for ic in range(NCH):
                last = nc.tensor.matmul(
                    ps_a[vi % 2][:],
                    w_g1[:, ic * 128: (ic + 1) * 128],
                    xsl(ic, tt * TT), start=(ic == 0), stop=(ic == NCH - 1))
            return last
        dep = [s_wg1] + [g1_steps[-2] if len(g1_steps) >= 2 else
                         (v_steps[-1] if len(g1_steps) == 1 else v_steps[-2])]
        s_mm = p.step("tensor", fn_mm, deps=dep, name=f"l1_g1m_{tt}")

        def fn_ev(tt=tt, vi=vi):
            return nc.vector.tensor_copy(g1_sb[:, tt * TT:(tt + 1) * TT], ps_a[vi % 2][:])
        g1_steps.append(p.step("vector", fn_ev, deps=[s_mm], name=f"l1_g1e_{tt}"))
        vi += 1
    s_g1 = g1_steps[-1]
    s_vdone = v_steps[-1]
    va_ctx.close()

    at_ctx = ExitStack()
    ae = at_ctx.enter_context
    ps_pk = [ae(nc.psum_tensor(f"l1_pspk{i}", [128, 512], F32)) for i in range(2)]
    ps_st = [ae(nc.psum_tensor(f"l1_psst{i}", [128, 128], F32)) for i in range(2)]
    ps_o = ae(nc.psum_tensor("l1_pso", [128, 128], F32))
    ps_nt = [ae(nc.psum_tensor(f"l1_psnt{i}", [128, 128], BF16)) for i in range(2)]
    ps_nn = ae(nc.psum_tensor("l1_ps_nn", [128, 256], F32))
    ps_ntot = ps_nn[:, 0:128]
    ps_np = ps_nn[:, 128:256]

    of = io.ybuf
    prev = {}
    for h in range(8):
        ek_steps = []
        for tt in range(NTT):
            def fn_km(h=h, tt=tt):
                last = None
                for ic in range(NCH):
                    last = nc.tensor.matmul(
                        ps_pk[tt % 2][:],
                        w_k[:, ic * 1024 + h * 128: ic * 1024 + h * 128 + 128],
                        xsl(ic, tt * TT), start=(ic == 0), stop=(ic == NCH - 1))
                return last
            dep = [s_wk, s_g1 if h == 0 else None]
            if tt >= 2:
                dep.append(ek_steps[-2])
            elif h > 0:
                dep.append(prev[("qe", h - 1, NTT - 1)])
            s_km = p.step("tensor", fn_km, deps=dep, name=f"l1_km_{h}_{tt}")

            def fn_ke(h=h, tt=tt):
                return nc.scalar.activation(ek_sb[:, tt * TT:(tt + 1) * TT],
                                            ps_pk[tt % 2][:], AF.Exp)
            dep_ke = [s_km]
            if h > 0:
                dep_ke += [prev[("st", h - 1, NC16 - 1)], prev[("nt", h - 1, NC16 - 1)]]
            ek_steps.append(p.step("scalar", fn_ke, deps=dep_ke, name=f"l1_ke_{h}_{tt}"))

        q_steps = []
        for tt in range(NTT):
            def fn_qm(h=h, tt=tt):
                last = None
                for ic in range(NCH):
                    last = nc.tensor.matmul(
                        ps_pk[tt % 2][:],
                        w_q[:, ic * 1024 + h * 128: ic * 1024 + h * 128 + 128],
                        xsl(ic, tt * TT), start=(ic == 0), stop=(ic == NCH - 1))
                return last
            dep = [s_wq, ek_steps[-2] if tt == 0 else None,
                   ek_steps[-1] if tt == 1 else None]
            if tt >= 2:
                dep.append(q_steps[-2])
            s_qm = p.step("tensor", fn_qm, deps=dep, name=f"l1_qm_{h}_{tt}")

            def fn_qe(h=h, tt=tt):
                return nc.scalar.activation(qs_sb[:, tt * TT:(tt + 1) * TT],
                                            ps_pk[tt % 2][:], AF.Silu)
            dep_qe = [s_qm]
            if h > 0:
                dep_qe.append(prev[("qdiv", h - 1)])
            s_qe = p.step("scalar", fn_qe, deps=dep_qe, name=f"l1_qe_{h}_{tt}")
            prev[("qe", h, tt)] = s_qe
            q_steps.append(s_qe)

        def fn_dscan():
            return nc.vector.tensor_tensor_scan(
                d_sb[:], ones_big[:], ek_sb[:], 0.0, ALU.mult, ALU.add)
        dep_ds = ek_steps + [io.s_const, s_szero, prev.get(("qdiv", h - 1))]
        s_dscan = p.step("vector", fn_dscan, deps=dep_ds, late=True, name=f"l1_ds_{h}")

        nt_steps = []
        for i in range(NC16):
            def fn_nt(i=i):
                return nc.tensor.transpose(ps_nt[i % 2][:],
                                           ek_sb[:, i * 128:(i + 1) * 128], io.idn[:])
            dep = [ek_steps[-1], io.s_const]
            if i >= 2:
                dep.append(nt_steps[-2][1])
            elif h > 0:
                dep.append(prev[("nu", h - 1, NC16 - 1)])
            s_nt = p.step("tensor", fn_nt, deps=dep, name=f"l1_nt_{h}_{i}")

            def fn_nte(i=i):
                return nc.vector.tensor_copy(ekT_sb[:, i * 128:(i + 1) * 128],
                                             ps_nt[i % 2][:])
            s_nte = p.step("vector", fn_nte, deps=[s_nt], name=f"l1_nte_{h}_{i}")
            nt_steps.append((s_nt, s_nte))
            prev[("nt", h, i)] = s_nt

        def fn_ntot(h=h):
            last = None
            for i in range(NC16):
                last = nc.tensor.matmul(
                    ps_ntot, ekT_sb[:, i * 128:(i + 1) * 128],
                    v_tm[:, i * 1024 + h * 128: i * 1024 + (h + 1) * 128],
                    start=(i == 0), stop=(i == NC16 - 1))
            return last
        dep_ntot = [s[1] for s in nt_steps] + [s_vdone, prev.get(("stage", h - 1)),
                    prev.get(("ne", h - 1, NC16 - 1))]
        s_ntot = p.step("tensor", fn_ntot, deps=dep_ntot, name=f"l1_ntot_{h}")

        def fn_stage():
            p.chain(nc.vector.tensor_copy(send_sb[:, 0:128], ps_ntot))
            return nc.vector.tensor_copy(send_sb[:, 128:129], d_sb[:, TOK - 1:TOK])
        s_stage = p.step("vector", fn_stage, deps=[s_ntot, s_dscan, s_szero,
                                                   prev.get(("snd", h - 1))],
                         late=True, name=f"l1_stage_{h}")
        prev[("stage", h)] = s_stage

        def fn_snd(h=h):
            return p.dma(cc_in[h].ap(), send_sb[:])
        s_snd = p.step("gpsimd", fn_snd, deps=[s_stage], kind="dma", n_dma=1,
                       name=f"l1_snd_{h}")
        prev[("snd", h)] = s_snd

        def fn_cc(h=h):
            return nc.gpsimd.collective_compute(
                "AllGather", ALU.bypass, replica_groups=PAIRS,
                ins=[cc_in[h].ap().opt()], outs=[cc_out[h].ap().opt()])
        s_cc = p.step("gpsimd", fn_cc, deps=[s_snd], kind="collective", name=f"l1_cc_{h}")

        def fn_rcv(h=h):
            return p.dma(recv_sb[:], cc_out[h][0:128, :])
        s_rcv = p.step("gpsimd", fn_rcv, deps=[s_cc, prev.get(("ne", h - 1, NC16 - 1)),
                                               prev.get(("qdiv", h - 1))],
                       kind="dma", n_dma=1, name=f"l1_rcv_{h}")

        def fn_qdiv():
            p.chain(nc.vector.tensor_scalar(dh_col[:], recv_sb[:, 128:129],
                                            flag_sb[:, 0:1], None, ALU.mult))
            p.chain(nc.vector.tensor_scalar(d_sb[:], d_sb[:], dh_col[:, 0:1],
                                            None, ALU.add))
            p.chain(nc.vector.reciprocal(d_sb[:], d_sb[:]))
            return nc.vector.tensor_mul(qp_sb[:], qs_sb[:], d_sb[:])
        dep_qd = [s_rcv, s_dscan, s_cst] + q_steps
        if h > 0:
            dep_qd += [prev[("st", h - 1, NC16 - 1)], prev[("oi", h - 1, NC16 - 1)]]
        s_qdiv = p.step("vector", fn_qdiv, deps=dep_qd, late=True, name=f"l1_qdiv_{h}")
        prev[("qdiv", h)] = s_qdiv

        def fn_ninit():
            return nc.vector.tensor_scalar(N_sb[:], recv_sb[:, 0:128],
                                           flag_sb[:, 0:1], None, ALU.mult)
        dep_ni = [s_rcv, s_cst, prev.get(("oi", h - 1, NC16 - 1))]
        s_ninit = p.step("vector", fn_ninit, deps=dep_ni, late=True, name=f"l1_ninit_{h}")
        prev[("ne", h, -1)] = s_ninit

        for i in range(NC16):
            def fn_st(h=h, i=i):
                return nc.tensor.matmul(
                    ps_st[i % 2][:], ek_sb[:, i * 128:(i + 1) * 128],
                    qp_sb[:, i * 128:(i + 1) * 128], start=True, stop=True)
            dep = [s_qdiv, prev.get(("pm", h, i - 2))]
            s_st = p.step("tensor", fn_st, deps=dep, name=f"l1_st_{h}_{i}")
            prev[("st", h, i)] = s_st

            def fn_pm(i=i):
                return nc.vector.tensor_mul(p_sb[:, (i % 2) * 128:(i % 2 + 1) * 128],
                                            ps_st[i % 2][:], tri_sb[:])
            dep = [s_st, s_cst, prev.get(("oi", h, i - 2))]
            s_pm = p.step("vector", fn_pm, deps=dep, name=f"l1_pm_{h}_{i}")
            prev[("pm", h, i)] = s_pm

            def fn_oi(h=h, i=i):
                nc.tensor.matmul(ps_o[:], N_sb[:],
                                 qp_sb[:, i * 128:(i + 1) * 128], start=True, stop=False)
                return nc.tensor.matmul(
                    ps_o[:], v_tm[:, i * 1024 + h * 128: i * 1024 + (h + 1) * 128],
                    p_sb[:, (i % 2) * 128:(i % 2 + 1) * 128], start=False, stop=True)
            dep = [s_pm, prev[("ne", h, i - 1)], prev.get(("oe", h, i - 1))]
            if i == 0 and h > 0:
                dep.append(prev[("oe", h - 1, NC16 - 1)])
            s_oi = p.step("tensor", fn_oi, deps=dep, name=f"l1_oi_{h}_{i}")
            prev[("oi", h, i)] = s_oi

            def fn_nu(h=h, i=i):
                return nc.tensor.matmul(
                    ps_np, ekT_sb[:, i * 128:(i + 1) * 128],
                    v_tm[:, i * 1024 + h * 128: i * 1024 + (h + 1) * 128],
                    start=True, stop=True)
            dep = [nt_steps[i][1], prev.get(("ne", h, i - 1)), s_oi if i == 0 else None]
            if i == 0 and h > 0:
                dep.append(prev[("ne", h - 1, NC16 - 1)])
            s_nu = p.step("tensor", fn_nu, deps=dep, name=f"l1_nu_{h}_{i}")
            prev[("nu", h, i)] = s_nu

            def fn_ne(i=i):
                if i == 0:
                    p.chain(nc.vector.tensor_copy(nacc[:], ps_np))
                else:
                    p.chain(nc.vector.tensor_add(nacc[:], nacc[:], ps_np))
                return nc.vector.scalar_tensor_tensor(
                    N_sb[:], recv_sb[:, 0:128], flag_sb[:, 0:1], nacc[:],
                    ALU.mult, ALU.add)
            dep = [s_nu, s_oi, s_rcv, s_cst]
            s_ne = p.step("vector", fn_ne, deps=dep, late=True, name=f"l1_ne_{h}_{i}")
            prev[("ne", h, i)] = s_ne

            def fn_oe(h=h, i=i):
                return nc.vector.tensor_scalar(
                    of[:, h * TOK + i * 128: h * TOK + (i + 1) * 128],
                    ps_o[:], SC, None, ALU.mult)
            s_oe = p.step("vector", fn_oe, deps=[s_oi], name=f"l1_oe_{h}_{i}")
            prev[("oe", h, i)] = s_oe

    heads_done = [prev[("oe", h, NC16 - 1)] for h in range(8)] + \
                 [prev[("ne", 7, NC16 - 1)], prev[("qdiv", 7)]]
    at_ctx.close()

    rm_ctx = ExitStack()
    re_ = rm_ctx.enter_context
    ps_ss = re_(nc.psum_tensor("l1_ps_ss", [1, 512], F32))
    ps_rb = [re_(nc.psum_tensor(f"l1_ps_rb{i}", [128, 512], F32)) for i in range(2)]
    ps_g = [re_(nc.psum_tensor(f"l1_ps_g{i}", [128, 512], F32)) for i in range(2)]
    ssq = re_(nc.sbuf_tensor("l1_ssq", [128, 512], BF16))
    mrow = re_(nc.sbuf_tensor("l1_mrow", [1, 512], F32))
    rb_sb = re_(nc.sbuf_tensor("l1_rb", [128, 512], BF16))
    sg_sb = re_(nc.sbuf_tensor("l1_sg", [128, 2 * 512], BF16))
    t1_sb = re_(nc.sbuf_tensor("l1_t1", [128, 2 * 512], BF16))
    gg_sb = re_(nc.sbuf_tensor("l1_gg", [128, 2 * 512], BF16))

    prev_rb = None
    prev_oms = {}
    for tt in range(NTT):
        s_ssm = None
        s_sq_prev = heads_done[-1] if tt == 0 else prev_oms[(tt - 1, 7)]
        for oc in range(NCH):
            def fn_sq(oc=oc, tt=tt):
                return nc.scalar.activation(ssq[:], of[:, oc * TOK + tt * TT: oc * TOK + tt * TT + TT],
                                            AF.Square)
            s_sq = p.step("scalar", fn_sq, deps=heads_done + [s_sq_prev], name=f"l1_sq_{tt}_{oc}")

            def fn_ssm(oc=oc):
                return nc.tensor.matmul(ps_ss[:], io.ones_col[:, 0:1], ssq[:],
                                        start=(oc == 0), stop=(oc == NCH - 1))
            dep = [s_sq, s_ssm, prev.get(("ms", tt - 1))]
            s_ssm = p.step("tensor", fn_ssm, deps=dep, name=f"l1_ssm_{tt}_{oc}")
            s_sq_prev = s_ssm

        def fn_ms():
            return nc.vector.tensor_scalar(mrow[:], ps_ss[:], 1.0 / 1024.0, None, ALU.mult)
        s_ms = p.step("vector", fn_ms, deps=[s_ssm], late=True, name=f"l1_ms_{tt}")
        prev[("ms", tt)] = s_ms

        def fn_rstd():
            p.chain(nc.scalar.activation(mrow[:], mrow[:], AF.Ln, bias=1e-5))
            return nc.scalar.activation(mrow[:], mrow[:], AF.Exp, scale=-0.5)
        s_rstd = p.step("scalar", fn_rstd, deps=[s_ms], late=True, name=f"l1_rstd_{tt}")

        def fn_rbm(tt=tt):
            return nc.tensor.matmul(ps_rb[tt % 2][:], io.ones_row[:], mrow[:],
                                    start=True, stop=True)
        s_rbm = p.step("tensor", fn_rbm, deps=[s_rstd, prev.get(("rbc", tt - 2))],
                       name=f"l1_rbm_{tt}")

        def fn_rbc(tt=tt):
            return nc.scalar.copy(rb_sb[:], ps_rb[tt % 2][:])
        s_rbc = p.step("scalar", fn_rbc, deps=[s_rbm, prev_oms.get((tt - 1, 7))],
                       name=f"l1_rbc_{tt}")
        prev[("rbc", tt)] = s_rbc

        for oc in range(NCH):
            k2 = oc % 2

            def fn_gm(oc=oc, tt=tt, k2=k2):
                return nc.tensor.matmul(
                    ps_g[k2][:], w_g2[:, oc * 128:(oc + 1) * 128],
                    g1_sb[:, tt * TT:(tt + 1) * TT], start=True, stop=True)
            dep = [s_g1, s_wg2, prev_oms.get((tt, oc - 2)), s_rbm]
            s_gm = p.step("tensor", fn_gm, deps=dep, name=f"l1_gm_{tt}_{oc}")

            def fn_sg(k2=k2):
                return nc.scalar.activation(sg_sb[:, k2 * 512:(k2 + 1) * 512],
                                            ps_g[k2][:], AF.Sigmoid)
            s_sg = p.step("scalar", fn_sg, deps=[s_gm, prev_oms.get((tt, oc - 2))],
                          name=f"l1_sg_{tt}_{oc}")

            def fn_t1(oc=oc, tt=tt, k2=k2):
                return nc.vector.scalar_tensor_tensor(
                    t1_sb[:, k2 * 512:(k2 + 1) * 512],
                    of[:, oc * TOK + tt * TT: oc * TOK + tt * TT + TT],
                    gnw_sb[:, oc:oc + 1], rb_sb[:], ALU.mult, ALU.mult)
            s_t1 = p.step("vector", fn_t1, deps=[s_rbc, s_cst,
                                                 prev_oms.get((tt, oc - 2))],
                          name=f"l1_t1_{tt}_{oc}")

            def fn_gg(k2=k2):
                return nc.vector.tensor_mul(gg_sb[:, k2 * 512:(k2 + 1) * 512],
                                            ps_g[k2][:], sg_sb[:, k2 * 512:(k2 + 1) * 512])
            s_gg = p.step("vector", fn_gg, deps=[s_sg, prev_oms.get((tt, oc - 2))],
                          name=f"l1_gg_{tt}_{oc}")

            def fn_om(oc=oc, tt=tt, k2=k2):
                return nc.vector.tensor_mul(
                    of[:, oc * TOK + tt * TT: oc * TOK + tt * TT + TT],
                    t1_sb[:, k2 * 512:(k2 + 1) * 512], gg_sb[:, k2 * 512:(k2 + 1) * 512])
            s_om = p.step("vector", fn_om, deps=[s_t1, s_gg], late=True,
                          name=f"l1_om_{tt}_{oc}")
            prev_oms[(tt, oc)] = s_om

    om_all = [prev_oms[(tt, oc)] for tt in range(NTT) for oc in range(NCH)]
    rm_ctx.close()

    pr_ctx = ExitStack()
    w_o, s_wo = load_weight(p, pr_ctx, "l1o", "W_o1", 1024, 1024, deps=[om_all[-1]])
    ps_pr = [pr_ctx.enter_context(nc.psum_tensor(f"l1_pr{i}", [128, 512], F32))
             for i in range(8)]
    yform = {}
    prev_ev = {}
    for tt in range(NTT):
        mm_steps = []
        for oc in range(NCH):
            def fn_pm(oc=oc, tt=tt):
                last = None
                for ic in range(NCH):
                    last = nc.tensor.matmul(
                        ps_pr[oc][:],
                        w_o[:, ic * NCH * 128 + oc * 128: ic * NCH * 128 + oc * 128 + 128],
                        of[:, ic * TOK + tt * TT: ic * TOK + tt * TT + TT],
                        start=(ic == 0), stop=(ic == NCH - 1))
                return last
            dep = om_all + [s_wo, prev_ev.get((tt - 1, oc))]
            mm_steps.append(p.step("tensor", fn_pm, deps=dep, name=f"l1_prm_{tt}_{oc}"))
        evs = []
        for oc in range(NCH):
            def fn_pe(oc=oc, tt=tt):
                sl = slice(oc * TOK + tt * TT, oc * TOK + tt * TT + TT)
                return nc.vector.tensor_add(io.ybuf[:, sl], ps_pr[oc][:], x[:, sl])
            s_pe = p.step("vector", fn_pe, deps=mm_steps, late=True,
                          name=f"l1_pre_{tt}_{oc}")
            prev_ev[(tt, oc)] = s_pe
            evs.append(s_pe)
        yform[tt] = evs
    pr_ctx.close()

    ln_ctx = ExitStack()
    emit_postln(p, io, ln_ctx, lambda tt: yform[tt] + yform[NTT - 1],
                final=(io.cur_phase == TOTAL_PHASES - 1), tag="p2")
    ln_ctx.close()
    ctx.close()




def phase_l1_ffn(p, io):
    nc = p.nc
    ctx = ExitStack()
    e = ctx.enter_context
    x = io.cur
    Q = 1.0 / 127.0

    W1G = p.input("W1G", [1024, 3072], I32)
    W1U = p.input("W1U", [1024, 3072], I32)
    W1D = p.input("W1D", [3072, 1024], I32)

    y_acc = e(nc.sbuf_tensor("f1_y", [128, NCH * 1024], F32))
    h_sec = e(nc.sbuf_tensor("f1_h", [128, 6 * 512], BF16))
    hh = e(nc.sbuf_tensor("f1_hh", [128, 2 * 512], BF16))
    wbuf = [e(nc.sbuf_tensor(f"f1_w{i}", [128, 18 * 1024], BF16)) for i in range(2)]

    f1ps_ctx = ExitStack()
    fe = f1ps_ctx.enter_context
    ps_g = [fe(nc.psum_tensor(f"f1_psg{i}", [128, 512], F32)) for i in range(2)]
    ps_u = [fe(nc.psum_tensor(f"f1_psu{i}", [128, 512], F32)) for i in range(2)]
    ps_d = [fe(nc.psum_tensor(f"f1_psd{i}", [128, 512], F32)) for i in range(2)]

    def xsl(ic, t0):
        return x[:, ic * TOK + t0: ic * TOK + t0 + 512]

    prev_silu = {}
    prev_h1 = {}
    prev_y = {}
    wl_reader = [None, None]
    yform = {}

    si = 0
    for TB in range(2):
        for sec in range(4):
            buf = si % 2

            def fn_wl(sec=sec, buf=buf):
                last = None
                for c in range(NCH):
                    last = p.dma(wbuf[buf][:, c * 768: c * 768 + 768],
                                 W1G[c * 128:(c + 1) * 128, sec * 768:(sec + 1) * 768])
                for c in range(NCH):
                    last = p.dma(wbuf[buf][:, 6144 + c * 768: 6144 + c * 768 + 768],
                                 W1U[c * 128:(c + 1) * 128, sec * 768:(sec + 1) * 768])
                for c in range(6):
                    last = p.dma(wbuf[buf][:, 12288 + c * 1024: 12288 + (c + 1) * 1024],
                                 W1D[(sec * 6 + c) * 128:(sec * 6 + c + 1) * 128, :])
                return last
            s_wl = p.step("gpsimd", fn_wl, deps=[wl_reader[buf]], kind="dma",
                          n_dma=22, name=f"f1_wl{si}")

            for st in range(2):
                t0 = TB * 1024 + st * 512
                h_steps = []
                for c6 in range(6):
                    k = c6 % 2

                    def fn_g(c6=c6, t0=t0, buf=buf, k=k):
                        last = None
                        for ic in range(NCH):
                            last = nc.tensor.matmul(
                                ps_g[k][:],
                                wbuf[buf][:, ic * 768 + c6 * 128: ic * 768 + c6 * 128 + 128],
                                xsl(ic, t0), start=(ic == 0), stop=(ic == NCH - 1))
                        return last
                    s_g = p.step("tensor", fn_g, deps=[s_wl, prev_silu.get(k)],
                                 name=f"f1_g{si}_{st}_{c6}")

                    def fn_u(c6=c6, t0=t0, buf=buf, k=k):
                        last = None
                        for ic in range(NCH):
                            last = nc.tensor.matmul(
                                ps_u[k][:],
                                wbuf[buf][:, 6144 + ic * 768 + c6 * 128: 6144 + ic * 768 + c6 * 128 + 128],
                                xsl(ic, t0), start=(ic == 0), stop=(ic == NCH - 1))
                        return last
                    s_u = p.step("tensor", fn_u, deps=[s_wl, prev_h1.get(k)],
                                 name=f"f1_u{si}_{st}_{c6}")

                    def fn_silu(k=k):
                        return nc.scalar.activation(hh[:, k * 512:(k + 1) * 512],
                                                    ps_g[k][:], AF.Silu, scale=Q)
                    s_silu = p.step("scalar", fn_silu, deps=[s_g, prev_h1.get(k)],
                                    name=f"f1_si{si}_{st}_{c6}")
                    prev_silu[k] = s_silu

                    def fn_h1(c6=c6, k=k):
                        return nc.vector.scalar_tensor_tensor(
                            h_sec[:, c6 * 512:(c6 + 1) * 512], ps_u[k][:], Q,
                            hh[:, k * 512:(k + 1) * 512], ALU.mult, ALU.mult)
                    dep_h1 = [s_silu, s_u, prev_y.get("last_d")]
                    s_h1 = p.step("vector", fn_h1, deps=dep_h1, name=f"f1_h1{si}_{st}_{c6}")
                    prev_h1[k] = s_h1
                    h_steps.append(s_h1)

                for od in range(NCH):
                    k = od % 2

                    def fn_d(od=od, buf=buf, k=k):
                        last = None
                        for c6 in range(6):
                            last = nc.tensor.matmul(
                                ps_d[k][:],
                                wbuf[buf][:, 12288 + c6 * 1024 + od * 128: 12288 + c6 * 1024 + od * 128 + 128],
                                h_sec[:, c6 * 512:(c6 + 1) * 512],
                                start=(c6 == 0), stop=(c6 == 5))
                        return last
                    s_d = p.step("tensor", fn_d, deps=h_steps + [prev_y.get(k)],
                                 name=f"f1_d{si}_{st}_{od}")
                    wl_reader[buf] = s_d
                    prev_y["last_d"] = s_d

                    def fn_y(od=od, st=st, TB=TB, k=k, sec=sec):
                        ysl = y_acc[:, od * 1024 + st * 512: od * 1024 + st * 512 + 512]
                        if sec == 0:
                            return nc.vector.tensor_scalar(ysl, ps_d[k][:], Q, None, ALU.mult)
                        return nc.vector.scalar_tensor_tensor(ysl, ps_d[k][:], Q, ysl,
                                                              ALU.mult, ALU.add)
                    dep_y = [s_d, prev_y.get(("ysl", od, st))]
                    if sec == 0 and TB == 1:
                        dep_y += yform.get(0, []) + yform.get(1, [])
                    s_y = p.step("vector", fn_y, deps=dep_y, late=True,
                                 name=f"f1_y{si}_{st}_{od}")
                    prev_y[k] = s_y
                    prev_y[("ysl", od, st)] = s_y

            si += 1
        for st in range(2):
            tt = TB * 2 + st
            steps = []
            for c in range(NCH):
                def fn_yf(c=c, st=st, tt=tt):
                    ysl = y_acc[:, c * 1024 + st * 512: c * 1024 + st * 512 + 512]
                    osl = io.ybuf[:, c * TOK + tt * TT: c * TOK + tt * TT + TT]
                    return nc.vector.tensor_add(
                        osl, ysl, x[:, c * TOK + tt * TT: c * TOK + tt * TT + TT])
                steps.append(p.step("vector", fn_yf, deps=[prev_y[("ysl", c, st)]],
                                    late=True, name=f"f1_yf{TB}_{st}_{c}"))
            yform[tt] = steps

    f1ps_ctx.close()
    ln_ctx = ExitStack()
    emit_postln(p, io, ln_ctx, lambda tt: yform[tt] + yform[3],
                final=(io.cur_phase == TOTAL_PHASES - 1), tag="p3")
    ln_ctx.close()
    ctx.close()



def phase_l2_attn(p, io):
    nc = p.nc
    ctx = ExitStack()
    e = ctx.enter_context
    x = io.cur
    NQT = 8
    NKC = 32

    FLAG = p.input("PFLAG", [128, 1], F32)
    ALB = p.input("ALIBI2_LOC", [128, 256], F32)
    MK2 = p.input("MASK2", [128, 2048], F32)
    NEGR = p.input("NEGR2_LOC", [1, 4096], F32)
    BQ2 = p.input("B_qkv2", [128, 24], F32)
    BP2 = p.input("B_l2proj", [128, 8], F32)
    TEMP = p.input("TEMP", [1, 1], F32)

    flag_sb = e(nc.sbuf_tensor("l2_flag", [128, 1], F32))
    alb_sb = e(nc.sbuf_tensor("l2_alb", [128, 256], F32))
    mask2 = e(nc.sbuf_tensor("l2_mk", [128, 2048], BF16))
    bq_sb = e(nc.sbuf_tensor("l2_bq", [128, 24], F32))
    bp_sb = e(nc.sbuf_tensor("l2_bp", [128, 8], F32))
    temp_sb = e(nc.sbuf_tensor("l2_temp", [1, 1], F32))
    scol = e(nc.sbuf_tensor("l2_scol", [128, 2], F32))

    def fn_cst():
        p.dma(flag_sb[:], FLAG)
        p.dma(alb_sb[:], ALB)
        p.dma(mask2[:], MK2)
        p.dma(bq_sb[:], BQ2)
        p.dma(bp_sb[:], BP2)
        return p.dma(temp_sb[:], TEMP)
    s_cst = p.step("gpsimd", fn_cst, kind="dma", n_dma=6, name="l2_cst")

    cc1k_in = nc.dram_tensor("l2cc1ki", [1024, TOK], BF16)
    cc1k_out = nc.dram_tensor("l2cc1ko", [2048, TOK], BF16)
    cc1q_in = nc.dram_tensor("l2cc1qi", [1024, TOK], BF16)
    cc1q_out = nc.dram_tensor("l2cc1qo", [2048, TOK], BF16)
    cc2_in = nc.dram_tensor("l2cc2i", [TOK, 1024], BF16)
    cc2_out = nc.dram_tensor("l2cc2o", [2 * TOK, 1024], BF16)
    cc3_in = nc.dram_tensor("l2cc3i", [512, T], BF16)
    cc3_out = nc.dram_tensor("l2cc3o", [1024, T], BF16)

    sc_ctx = ExitStack()
    ps_sc = sc_ctx.enter_context(nc.psum_tensor("l2_pssc", [128, 2], F32))

    def fn_scm():
        return nc.tensor.matmul(ps_sc[:, 0:1], io.ones_row[:], temp_sb[:],
                                start=True, stop=True)
    s_scm = p.step("tensor", fn_scm, deps=[s_cst, io.s_const], name="l2_scm")

    def fn_scv():
        p.chain(nc.vector.reciprocal(scol[:, 0:1], ps_sc[:, 0:1]))
        return nc.vector.tensor_scalar(scol[:, 0:1], scol[:, 0:1], 0.125, None, ALU.mult)
    s_scv = p.step("vector", fn_scv, deps=[s_scm], late=True, name="l2_scv")

    k65 = e(nc.sbuf_tensor("l2_k65", [65, 8 * T], BF16))
    v_full = e(nc.sbuf_tensor("l2_vf", [128, NKC * 520], BF16))
    btmp = e(nc.sbuf_tensor("l2_btmp", [128, 2048], BF16))

    def fn_kone():
        p.chain(nc.vector.memset(k65[64:65, :], 1.0))
        return nc.vector.memset(
            v_full[:].rearrange("p (t h c) -> p t h c", h=8, c=65)[:, :, :, 64:65], 1.0)
    s_kone = p.step("vector", fn_kone, name="l2_kone")

    WQ2 = p.input("W_qkv2", [1024, 3072], F32)
    qk_ctx = ExitStack()
    qe = qk_ctx.enter_context
    wsec = [qe(nc.sbuf_tensor(f"l2_ws{i}", [128, NCH * 1024], BF16)) for i in range(2)]
    kstage = qe(nc.sbuf_tensor("l2_kst", [128, 2 * 512], BF16))
    ps_qk = [qe(nc.psum_tensor(f"l2_psqk{i}", [128, 512], F32)) for i in range(2)]

    def mk_sec_l2(sec, buf, dep):
        def fn(sec=sec, buf=buf):
            last = None
            for c in range(NCH):
                last = p.dma(wsec[buf][:, c * 1024:(c + 1) * 1024],
                             WQ2[c * 128:(c + 1) * 128, sec * 1024:(sec + 1) * 1024])
            return last
        return p.step("gpsimd", fn, deps=dep, kind="dma", n_dma=NCH, name=f"l2_wsec{sec}")

    prevq = {}
    qki = 0
    s_seck = mk_sec_l2(1, 1, [])
    k_staged = []
    for oc in range(NCH):
        for tt in range(NTT):
            def fn_mm(oc=oc, tt=tt, qki=qki):
                last = None
                for ic in range(NCH):
                    last = nc.tensor.matmul(
                        ps_qk[qki % 2][:],
                        wsec[1][:, ic * 1024 + oc * 128: ic * 1024 + oc * 128 + 128],
                        x[:, ic * TOK + tt * TT: ic * TOK + tt * TT + TT],
                        start=(ic == 0), stop=(ic == NCH - 1))
                return last
            s_mm = p.step("tensor", fn_mm, deps=[s_seck, prevq.get(("ev", qki - 2))],
                          name=f"l2_kmm_{oc}_{tt}")
            prevq[("mm", qki)] = s_mm

            def fn_ev(qki=qki):
                return nc.vector.tensor_copy(
                    kstage[:, (qki % 2) * 512:(qki % 2) * 512 + 512], ps_qk[qki % 2][:])
            s_ev = p.step("vector", fn_ev, deps=[s_mm, prevq.get(("st", qki - 2))],
                          name=f"l2_kev_{oc}_{tt}")
            prevq[("ev", qki)] = s_ev

            def fn_st(oc=oc, tt=tt, qki=qki):
                return p.dma(cc1k_in[oc * 128:(oc + 1) * 128, tt * TT:(tt + 1) * TT],
                             kstage[:, (qki % 2) * 512:(qki % 2) * 512 + 512])
            s_st = p.step("gpsimd", fn_st, deps=[s_ev], kind="dma", n_dma=1,
                          name=f"l2_kstg_{oc}_{tt}")
            prevq[("st", qki)] = s_st
            k_staged.append(s_st)
            qki += 1
    k_mm_last = prevq[("mm", qki - 1)]

    def fn_cc1k():
        return nc.gpsimd.collective_compute(
            "AllGather", ALU.bypass, replica_groups=PAIRS,
            ins=[cc1k_in.ap().opt()], outs=[cc1k_out.ap().opt()])
    s_cc1k = p.step("gpsimd", fn_cc1k, deps=k_staged, kind="collective", name="l2_cc1k")

    s_secv = mk_sec_l2(2, 0, [])
    v_staged = []
    for tc in range(16):
        for nn in range(2):
            def fn_mm(tc=tc, nn=nn, qki=qki):
                last = None
                for ic in range(NCH):
                    last = nc.tensor.matmul(
                        ps_qk[qki % 2][:],
                        x[:, ic * TOK + tc * 128: ic * TOK + tc * 128 + 128],
                        wsec[0][:, ic * 1024 + nn * 512: ic * 1024 + nn * 512 + 512],
                        start=(ic == 0), stop=(ic == NCH - 1))
                return last
            s_mm = p.step("tensor", fn_mm, deps=[s_secv, prevq.get(("ev", qki - 2))],
                          name=f"l2_vmm_{tc}_{nn}")
            prevq[("mm", qki)] = s_mm

            def fn_ev(qki=qki):
                return nc.vector.tensor_copy(
                    kstage[:, (qki % 2) * 512:(qki % 2) * 512 + 512], ps_qk[qki % 2][:])
            s_ev = p.step("vector", fn_ev, deps=[s_mm, prevq.get(("st", qki - 2))],
                          name=f"l2_vev_{tc}_{nn}")
            prevq[("ev", qki)] = s_ev

            def fn_st(tc=tc, nn=nn, qki=qki):
                return p.dma(cc2_in[tc * 128:(tc + 1) * 128, nn * 512:(nn + 1) * 512],
                             kstage[:, (qki % 2) * 512:(qki % 2) * 512 + 512])
            s_st = p.step("gpsimd", fn_st, deps=[s_ev], kind="dma", n_dma=1,
                          name=f"l2_vstg_{tc}_{nn}")
            prevq[("st", qki)] = s_st
            v_staged.append(s_st)
            qki += 1

    def fn_cc2():
        return nc.gpsimd.collective_compute(
            "AllGather", ALU.bypass, replica_groups=PAIRS,
            ins=[cc2_in.ap().opt()], outs=[cc2_out.ap().opt()])
    s_cc2 = p.step("gpsimd", fn_cc2, deps=v_staged, kind="collective", name="l2_cc2")

    s_secq = mk_sec_l2(0, 1, [k_mm_last])
    q_staged = []
    for oc in range(NCH):
        for tt in range(NTT):
            def fn_mm(oc=oc, tt=tt, qki=qki):
                last = None
                for ic in range(NCH):
                    last = nc.tensor.matmul(
                        ps_qk[qki % 2][:],
                        wsec[1][:, ic * 1024 + oc * 128: ic * 1024 + oc * 128 + 128],
                        x[:, ic * TOK + tt * TT: ic * TOK + tt * TT + TT],
                        start=(ic == 0), stop=(ic == NCH - 1))
                return last
            s_mm = p.step("tensor", fn_mm, deps=[s_secq, prevq.get(("ev", qki - 2))],
                          name=f"l2_qmm_{oc}_{tt}")
            prevq[("mm", qki)] = s_mm

            def fn_ev(oc=oc, qki=qki):
                return nc.vector.tensor_scalar(
                    kstage[:, (qki % 2) * 512:(qki % 2) * 512 + 512], ps_qk[qki % 2][:],
                    bq_sb[:, oc:oc + 1], scol[:, 0:1], ALU.add, ALU.mult)
            s_ev = p.step("vector", fn_ev,
                          deps=[s_mm, prevq.get(("st", qki - 2)), s_cst, s_scv],
                          name=f"l2_qev_{oc}_{tt}")
            prevq[("ev", qki)] = s_ev

            def fn_st(oc=oc, tt=tt, qki=qki):
                return p.dma(cc1q_in[oc * 128:(oc + 1) * 128, tt * TT:(tt + 1) * TT],
                             kstage[:, (qki % 2) * 512:(qki % 2) * 512 + 512])
            s_st = p.step("gpsimd", fn_st, deps=[s_ev], kind="dma", n_dma=1,
                          name=f"l2_qstg_{oc}_{tt}")
            prevq[("st", qki)] = s_st
            q_staged.append(s_st)
            qki += 1

    def fn_cc1q():
        return nc.gpsimd.collective_compute(
            "AllGather", ALU.bypass, replica_groups=PAIRS,
            ins=[cc1q_in.ap().opt()], outs=[cc1q_out.ap().opt()])
    s_cc1q = p.step("gpsimd", fn_cc1q, deps=q_staged, kind="collective", name="l2_cc1q")
    qk_ctx.close()
    sc_ctx.close()

    at_ctx = ExitStack()
    ae = at_ctx.enter_context
    q65 = ae(nc.sbuf_tensor("l2_q65", [65, 2 * 4096], BF16))
    p_sb = ae(nc.sbuf_tensor("l2_p", [128, 3 * 512], BF16))
    sm_sb = ae(nc.sbuf_tensor("l2_sm", [128, 2 * 512], F32))
    ostage = ae(nc.sbuf_tensor("l2_ost", [128, 2 * 2048], BF16))
    rb_sb = ae(nc.sbuf_tensor("l2_rb", [128, 512], BF16))
    lrec = ae(nc.sbuf_tensor("l2_lrec", [1, 512], F32))
    ps_st = [ae(nc.psum_tensor(f"l2_pst{i}", [128, 512], F32)) for i in range(3)]
    ps_ob = [ae(nc.psum_tensor(f"l2_ob{i}", [128, 512], F32)) for i in range(2)]
    ps_rb = [ae(nc.psum_tensor(f"l2_prb{i}", [128, 512], F32)) for i in range(2)]

    def fn_negr():
        p.dma(q65[64:65, 0:4096], NEGR)
        return p.dma(q65[64:65, 4096:8192], NEGR)
    s_negr = p.step("gpsimd", fn_negr, deps=[s_cc1q], kind="dma", n_dma=2,
                    name="l2_negr")

    _last_blend = [None]

    def blend(dst, tmp, fnA, fnB, n_dma, name, deps):
        deps = deps + [_last_blend[0], s_cst]
        sA = p.step("gpsimd", fnA, deps=deps, kind="dma", n_dma=n_dma, name=name + "_a")
        sB = p.step("gpsimd", fnB, deps=deps + [sA], kind="dma", n_dma=n_dma, name=name + "_b")

        def fn_bl(dst=dst, tmp=tmp):
            npart = dst.shape[0]
            p.chain(nc.vector.tensor_sub(tmp, tmp, dst))
            return nc.vector.scalar_tensor_tensor(dst, tmp, flag_sb[0:npart, 0:1], dst,
                                                  ALU.mult, ALU.add)
        s = p.step("vector", fn_bl, deps=[sA, sB], late=True, name=name + "_bl")
        _last_blend[0] = s
        return s

    kf_blends = []
    for lh in range(8):
        kb = []
        for th in range(2):
            def fnA(lh=lh, th=th):
                return p.dma(k65[0:64, lh * T + th * TOK: lh * T + (th + 1) * TOK],
                             cc1k_out[1024 * th + lh * 64: 1024 * th + lh * 64 + 64, :])

            def fnB(lh=lh, th=th):
                return p.dma(btmp[0:64, 0:TOK],
                             cc1k_out[1024 * th + 512 + lh * 64:
                                      1024 * th + 512 + lh * 64 + 64, :])
            kb.append(blend(k65[0:64, lh * T + th * TOK: lh * T + (th + 1) * TOK],
                            btmp[0:64, 0:TOK], fnA, fnB, 1, f"l2_krb{lh}_{th}",
                            [s_cc1k]))
        kf_blends.append(kb[1])

    def vdst(tc):
        return v_full[:, tc * 520:(tc + 1) * 520].rearrange(
            "p (h c) -> p h c", h=8)[:, :, 0:64]

    vf_blends = []
    for tc in range(NKC):
        def fnA(tc=tc):
            return p.dma(vdst(tc), cc2_out[tc * 128:(tc + 1) * 128, 0:512]
                         .rearrange("p (h c) -> p h c", h=8))

        def fnB(tc=tc):
            return p.dma(btmp[:, 0:512], cc2_out[tc * 128:(tc + 1) * 128, 512:1024])
        vf_blends.append(blend(vdst(tc), btmp[:, 0:512].rearrange("p (h c) -> p h c", h=8),
                               fnA, fnB, 1, f"l2_vrb{tc}", [s_cc2]))

    slot_reader = [None, None, None]
    slot_om = [None, None, None]
    sm_free = [None, None]
    ob_free = [None, None]
    rb_free = [None, None]
    last_rbm = [None]
    prev = {}
    gi = 0
    for qt in range(NQT):
        qb2 = qt % 2
        rowbase = 0 if qt < 4 else 1024
        qcol = (qt % 4) * 512

        dep_q = [s_cc1q, s_negr] + ([prev[("sclast", qt - 2)]] if qt >= 2 else [])
        for hh in range(2):
            def fnA(qb2=qb2, rowbase=rowbase, qcol=qcol, hh=hh):
                return p.dma(
                    q65[0:64, qb2 * 4096 + hh * 2048: qb2 * 4096 + hh * 2048 + 2048]
                    .rearrange("d (h j) -> d h j", h=4),
                    cc1q_out[rowbase + hh * 256: rowbase + hh * 256 + 256,
                             qcol:qcol + 512]
                    .rearrange("(h d) j -> d h j", d=64))

            def fnB(rowbase=rowbase, qcol=qcol, hh=hh):
                return p.dma(
                    btmp[0:64, 0:2048].rearrange("d (h j) -> d h j", h=4),
                    cc1q_out[rowbase + 512 + hh * 256: rowbase + 512 + hh * 256 + 256,
                             qcol:qcol + 512]
                    .rearrange("(h d) j -> d h j", d=64))
            s_qbl = blend(
                q65[0:64, qb2 * 4096 + hh * 2048: qb2 * 4096 + hh * 2048 + 2048],
                btmp[0:64, 0:2048], fnA, fnB, 1, f"l2_qrb{qt}_{hh}", dep_q)

        nkc = 4 * qt + 4
        s_sc = None
        for lh in range(8):
            bank = lh % 2
            oms = []

            def emit_om(item, lh=lh, bank=bank, nkc=nkc, qt=qt):
                kc, slot, s_exp = item

                def fn_om(kc=kc, slot=slot, lh=lh, bank=bank, nkc=nkc):
                    return nc.tensor.matmul(
                        ps_ob[bank][0:65, :],
                        v_full[:, kc * 520 + lh * 65: kc * 520 + (lh + 1) * 65],
                        p_sb[:, slot * 512: slot * 512 + 512],
                        start=(kc == 0), stop=(kc == nkc - 1))
                dep = [s_exp, vf_blends[kc], s_kone]
                if kc == 0:
                    dep.append(ob_free[bank])
                s_om = p.step("tensor", fn_om, deps=dep, name=f"l2_om_{qt}_{lh}_{kc}")
                slot_om[slot] = s_om
                oms.append(s_om)

            pend = []
            for kc in range(nkc):
                slot = gi % 3
                rel = kc - 4 * qt

                def fn_sc(lh=lh, kc=kc, qb2=qb2, slot=slot):
                    return nc.tensor.matmul(
                        ps_st[slot][:],
                        k65[0:65, lh * T + kc * 128: lh * T + kc * 128 + 128],
                        q65[0:65, qb2 * 4096 + lh * 512: qb2 * 4096 + (lh + 1) * 512],
                        start=True, stop=True)
                dep = [s_qbl, kf_blends[lh], s_kone, slot_reader[slot]]
                s_sc = p.step("tensor", fn_sc, deps=dep, name=f"l2_sc_{qt}_{lh}_{kc}")

                masked = rel >= 0
                smi = gi % 2
                if masked:
                    def fn_ma(slot=slot, rel=rel, smi=smi):
                        return nc.vector.tensor_add(
                            sm_sb[:, smi * 512: smi * 512 + 512],
                            ps_st[slot][:], mask2[:, rel * 512: rel * 512 + 512])
                    s_ma = p.step("vector", fn_ma, deps=[s_sc, s_cst, sm_free[smi]],
                                  late=True, name=f"l2_ma_{qt}_{lh}_{kc}")

                def fn_exp(lh=lh, rel=rel, slot=slot, masked=masked, smi=smi):
                    src = (sm_sb[:, smi * 512: smi * 512 + 512] if masked
                           else ps_st[slot][:])
                    return nc.scalar.activation(
                        p_sb[:, slot * 512: slot * 512 + 512], src, AF.Exp,
                        bias=alb_sb[:, lh * 32 + rel + 28: lh * 32 + rel + 29])
                dep = [(s_ma if masked else s_sc), s_cst, slot_om[slot]]
                s_exp = p.step("scalar", fn_exp, deps=dep, name=f"l2_exp_{qt}_{lh}_{kc}")
                slot_reader[slot] = s_ma if masked else s_exp
                if masked:
                    sm_free[smi] = s_exp

                pend.append((kc, slot, s_exp))
                if len(pend) > 2:
                    emit_om(pend.pop(0))
                gi += 1
            for item in pend:
                emit_om(item)

            def fn_lr(bank=bank):
                return nc.vector.reciprocal(lrec[:], ps_ob[bank][64:65, :])
            s_lr = p.step("vector", fn_lr, deps=[oms[-1], last_rbm[0]],
                          late=True, name=f"l2_lr_{qt}_{lh}")

            def fn_rbm(bank=bank):
                return nc.tensor.matmul(ps_rb[bank][:], io.ones_row[:], lrec[:],
                                        start=True, stop=True)
            s_rbm = p.step("tensor", fn_rbm, deps=[s_lr, rb_free[bank]],
                           name=f"l2_rbm_{qt}_{lh}")
            last_rbm[0] = s_rbm

            def fn_rbc(bank=bank):
                return nc.vector.tensor_copy(rb_sb[:], ps_rb[bank][:])
            s_rbc = p.step("vector", fn_rbc, deps=[s_rbm], late=True,
                           name=f"l2_rbc_{qt}_{lh}")
            rb_free[bank] = s_rbc

            def fn_oe(qt=qt, lh=lh, bank=bank):
                return nc.vector.tensor_mul(
                    ostage[bank * 64: bank * 64 + 64,
                           (qt % 2) * 2048 + (lh // 2) * 512:
                           (qt % 2) * 2048 + (lh // 2) * 512 + 512],
                    ps_ob[bank][0:64, :],
                    rb_sb[bank * 64: bank * 64 + 64, :])
            dep = [s_rbc] + ([prev[("ostd", qt - 2)]] if qt >= 2 else [])
            s_oe = p.step("vector", fn_oe, deps=dep, late=True, name=f"l2_oe_{qt}_{lh}")
            ob_free[bank] = s_oe
            prev[("oe", qt, lh)] = s_oe
        prev[("sclast", qt)] = s_sc

        def fn_ostd(qt=qt):
            last = None
            for hc in range(4):
                last = p.dma(cc3_in[hc * 128:(hc + 1) * 128, qt * 512:(qt + 1) * 512],
                             ostage[:, (qt % 2) * 2048 + hc * 512:
                                    (qt % 2) * 2048 + (hc + 1) * 512])
            return last
        s_ostd = p.step("gpsimd", fn_ostd,
                        deps=[prev[("oe", qt, lh)] for lh in range(8)],
                        kind="dma", n_dma=4, name=f"l2_ostd_{qt}")
        prev[("ostd", qt)] = s_ostd
    at_ctx.close()

    def fn_cc3():
        return nc.gpsimd.collective_compute(
            "AllGather", ALU.bypass, replica_groups=PAIRS,
            ins=[cc3_in.ap().opt()], outs=[cc3_out.ap().opt()])
    s_cc3 = p.step("gpsimd", fn_cc3, deps=[prev[("ostd", g)] for g in range(NQT)],
                   kind="collective", name="l2_cc3")

    of = io.ybuf
    pr_ctx = ExitStack()
    otmp = pr_ctx.enter_context(nc.sbuf_tensor("l2_otmp", [128, TOK], BF16))
    vb_bf = pr_ctx.enter_context(nc.sbuf_tensor("l2_vb", [128, 8], BF16))
    pb_eff = pr_ctx.enter_context(nc.sbuf_tensor("l2_pbeff", [128, 8], F32))
    of_blends = []
    for oc in range(NCH):
        def fnA(oc=oc):
            return p.dma(of[:, oc * TOK:(oc + 1) * TOK],
                         cc3_out[oc * 128:(oc + 1) * 128, 0:TOK])

        def fnB(oc=oc):
            return p.dma(otmp[:, 0:TOK], cc3_out[oc * 128:(oc + 1) * 128, TOK:T])
        of_blends.append(blend(of[:, oc * TOK:(oc + 1) * TOK], otmp[:, 0:TOK],
                               fnA, fnB, 1, f"l2_orb{oc}", [s_cc3]))

    w_pr2, s_wpr2 = load_weight(p, pr_ctx, "l2proj", "W_l2proj", 1024, 1024,
                                deps=[prev[("ostd", NQT - 1)]])
    pb_ctx = ExitStack()
    ps_pb2 = pb_ctx.enter_context(nc.psum_tensor("l2_pspb", [128, 8], F32))

    def fn_vb():
        return nc.vector.tensor_copy(vb_bf[:], bq_sb[:, 16:24])
    s_vb = p.step("vector", fn_vb, deps=[s_cst], name="l2_vb")

    def fn_pbe():
        last = None
        for oc in range(NCH):
            for ic in range(NCH):
                last = nc.tensor.matmul(
                    ps_pb2[:, oc:oc + 1],
                    w_pr2[:, (ic * 8 + oc) * 128:(ic * 8 + oc) * 128 + 128],
                    vb_bf[:, ic:ic + 1], start=(ic == 0), stop=(ic == NCH - 1))
        return last
    s_pbe = p.step("tensor", fn_pbe, deps=[s_vb, s_wpr2, prev[("oe", NQT - 1, 7)]],
                   name="l2_pbe")

    def fn_pbe2():
        return nc.vector.tensor_add(pb_eff[:], ps_pb2[:], bp_sb[:])
    s_pbe2 = p.step("vector", fn_pbe2, deps=[s_pbe, s_cst], late=True, name="l2_pbe2")
    pb_ctx.close()

    ps_pr = [pr_ctx.enter_context(nc.psum_tensor(f"l2_pr{i}", [128, 512], F32))
             for i in range(8)]
    yform = {}
    prev_ev = {}
    for tt in range(NTT):
        mm_steps = []
        for oc in range(NCH):
            def fn_pm(oc=oc, tt=tt):
                last = None
                for ic in range(NCH):
                    last = nc.tensor.matmul(
                        ps_pr[oc][:],
                        w_pr2[:, (ic * 8 + oc) * 128:(ic * 8 + oc) * 128 + 128],
                        of[:, ic * TOK + tt * TT: ic * TOK + tt * TT + TT],
                        start=(ic == 0), stop=(ic == NCH - 1))
                return last
            dep = of_blends + [s_wpr2, s_pbe, s_pbe2, prev_ev.get((tt - 1, oc))]
            mm_steps.append(p.step("tensor", fn_pm, deps=dep, name=f"l2_prm_{tt}_{oc}"))
        evs = []
        for oc in range(NCH):
            def fn_pe(oc=oc, tt=tt):
                sl = slice(oc * TOK + tt * TT, oc * TOK + tt * TT + TT)
                return nc.vector.scalar_tensor_tensor(
                    io.ybuf[:, sl], ps_pr[oc][:], pb_eff[:, oc:oc + 1],
                    x[:, sl], ALU.add, ALU.add)
            s_pe = p.step("vector", fn_pe,
                          deps=[mm_steps[oc], s_pbe2, mm_steps[-1]], late=True,
                          name=f"l2_pre_{tt}_{oc}")
            prev_ev[(tt, oc)] = s_pe
            evs.append(s_pe)
        yform[tt] = evs
    pr_ctx.close()

    ln_ctx = ExitStack()
    emit_postln(p, io, ln_ctx, lambda tt: yform[tt] + yform[NTT - 1],
                final=(io.cur_phase == TOTAL_PHASES - 1), tag="p4")
    ln_ctx.close()
    ctx.close()




def phase_l2_ffn(p, io):
    nc = p.nc
    ctx = ExitStack()
    e = ctx.enter_context
    x = io.cur

    FLAG = p.input("PFLAG", [128, 1], F32)
    MUR = p.input("MU_R", [128, 8], F32)
    MUK = p.input("MU_K", [128, 8], F32)
    flag_sb = e(nc.sbuf_tensor("f2_flag", [128, 1], F32))
    mur_sb = e(nc.sbuf_tensor("f2_mur", [128, 8], F32))
    muk_sb = e(nc.sbuf_tensor("f2_muk", [128, 8], F32))
    bnd = e(nc.sbuf_tensor("f2_bnd", [128, 8], BF16))
    stg = e(nc.sbuf_tensor("f2_stg", [128, 8], BF16))

    def fn_cst():
        p.dma(flag_sb[:], FLAG)
        p.dma(mur_sb[:], MUR)
        return p.dma(muk_sb[:], MUK)
    s_cst = p.step("gpsimd", fn_cst, kind="dma", n_dma=3, name="f2_cst")

    cc_in = nc.dram_tensor("f2cc_i", [128, 8], BF16)
    cc_out = nc.dram_tensor("f2cc_o", [256, 8], BF16)

    def fn_stg():
        return nc.vector.tensor_copy(
            stg[:], x[:].rearrange("p (c t) -> p c t", t=TOK)[:, :, TOK - 1:TOK].squeeze(2))
    s_stg = p.step("vector", fn_stg, deps=[], name="f2_stg")

    def fn_snd():
        return p.dma(cc_in.ap(), stg[:])
    s_snd = p.step("gpsimd", fn_snd, deps=[s_stg], kind="dma", n_dma=1, name="f2_snd")

    def fn_cc():
        return nc.gpsimd.collective_compute(
            "AllGather", ALU.bypass, replica_groups=PAIRS,
            ins=[cc_in.ap().opt()], outs=[cc_out.ap().opt()])
    s_cc = p.step("gpsimd", fn_cc, deps=[s_snd], kind="collective", name="f2_cc")

    def fn_rcv():
        return p.dma(bnd[:], cc_out[0:128, :])
    s_rcv = p.step("gpsimd", fn_rcv, deps=[s_cc], kind="dma", n_dma=1, name="f2_rcv")

    def fn_bnd():
        return nc.vector.tensor_scalar(bnd[:], bnd[:], flag_sb[:, 0:1], None, ALU.mult)
    s_bnd = p.step("vector", fn_bnd, deps=[s_rcv, s_cst], late=True, name="f2_bnd")

    xp_t = e(nc.sbuf_tensor("f2_xp", [128, 2 * 512], BF16))
    dx_t = e(nc.sbuf_tensor("f2_dx", [128, 2 * 512], BF16))
    rt_t = e(nc.sbuf_tensor("f2_rt", [128, 2 * 512], BF16))
    big_ctx = ExitStack()
    be_ = big_ctx.enter_context
    w_r, s_wr = load_weight(p, big_ctx, "f2r", "W_r2", 1024, 1024)
    w_k, s_wk = load_weight(p, big_ctx, "f2k", "W_k2", 1024, 1024)
    w_v, s_wv = load_weight(p, big_ctx, "f2v", "W_v2", 1024, 1024)
    r_sb = be_(nc.sbuf_tensor("f2_r", [128, NCH * 1024], BF16))
    ka_sb = be_(nc.sbuf_tensor("f2_ka", [128, NCH * 1024], BF16))
    xr_sb = be_(nc.sbuf_tensor("f2_xr", [128, NCH * 1024], BF16))
    xk_sb = be_(nc.sbuf_tensor("f2_xk", [128, NCH * 1024], BF16))

    ps_ctx = ExitStack()
    pse = ps_ctx.enter_context
    ps_a = [pse(nc.psum_tensor(f"f2_psa{i}", [128, 512], F32)) for i in range(2)]
    ps_b = [pse(nc.psum_tensor(f"f2_psb{i}", [128, 512], F32)) for i in range(2)]

    yform = {}
    prev = {}
    for TB in range(2):
        for st in range(2):
            tt = TB * 2 + st
            t0 = tt * TT
            for oc in range(NCH):
                k2 = (tt * NCH + oc) % 2

                def fn_sh(oc=oc, t0=t0, k2=k2, tt=tt):
                    if tt == 0:
                        p.chain(nc.vector.tensor_copy(
                            xp_t[:, k2 * 512 + 1: k2 * 512 + 512],
                            x[:, oc * TOK: oc * TOK + 511]))
                        p.chain(nc.vector.tensor_copy(
                            xp_t[:, k2 * 512: k2 * 512 + 1], bnd[:, oc:oc + 1]))
                    else:
                        p.chain(nc.vector.tensor_copy(
                            xp_t[:, k2 * 512: k2 * 512 + 512],
                            x[:, oc * TOK + t0 - 1: oc * TOK + t0 + 511]))
                    p.chain(nc.vector.tensor_sub(
                        dx_t[:, k2 * 512:(k2 + 1) * 512],
                        x[:, oc * TOK + t0: oc * TOK + t0 + 512],
                        xp_t[:, k2 * 512:(k2 + 1) * 512]))
                    p.chain(nc.vector.scalar_tensor_tensor(
                        xr_sb[:, oc * 1024 + (tt % 2) * 512: oc * 1024 + (tt % 2) * 512 + 512],
                        dx_t[:, k2 * 512:(k2 + 1) * 512], mur_sb[:, oc:oc + 1],
                        xp_t[:, k2 * 512:(k2 + 1) * 512], ALU.mult, ALU.add))
                    return nc.vector.scalar_tensor_tensor(
                        xk_sb[:, oc * 1024 + (tt % 2) * 512: oc * 1024 + (tt % 2) * 512 + 512],
                        dx_t[:, k2 * 512:(k2 + 1) * 512], muk_sb[:, oc:oc + 1],
                        xp_t[:, k2 * 512:(k2 + 1) * 512], ALU.mult, ALU.add)
                dep = [s_bnd, s_cst, prev.get(("mm_r_last", TB - 1)),
                       prev.get(("mm_k_last", TB - 1)), prev.get(("mm_v_last", TB - 1))]
                prev[("sh", tt, oc)] = p.step("vector", fn_sh, deps=dep, late=True,
                                              name=f"f2_sh_{tt}_{oc}")

        sh_all = [prev[("sh", TB * 2 + st, oc)] for st in range(2) for oc in range(NCH)]

        mi = 0
        for oc in range(NCH):
            for st in range(2):
                def fn_mm(oc=oc, st=st, mi=mi):
                    last = None
                    for ic in range(NCH):
                        last = nc.tensor.matmul(
                            ps_a[mi % 2][:],
                            w_r[:, ic * 1024 + oc * 128: ic * 1024 + oc * 128 + 128],
                            xr_sb[:, ic * 1024 + st * 512: ic * 1024 + st * 512 + 512],
                            start=(ic == 0), stop=(ic == NCH - 1))
                    return last
                dep = sh_all + [s_wr, prev.get(("ev_r", mi - 2))]
                s_mm = p.step("tensor", fn_mm, deps=dep, name=f"f2_rm_{TB}_{oc}_{st}")
                prev[("mm_r_last", TB)] = s_mm

                def fn_ev(oc=oc, st=st, mi=mi):
                    return nc.scalar.activation(
                        r_sb[:, oc * 1024 + st * 512: oc * 1024 + st * 512 + 512],
                        ps_a[mi % 2][:], AF.Sigmoid)
                prev[("ev_r", mi)] = p.step("scalar", fn_ev,
                                            deps=[s_mm, prev.get(("ev_r", mi - 2))],
                                            name=f"f2_re_{TB}_{oc}_{st}")
                mi += 1

        mi = 0
        for oc in range(NCH):
            for st in range(2):
                def fn_mm(oc=oc, st=st, mi=mi):
                    last = None
                    for ic in range(NCH):
                        last = nc.tensor.matmul(
                            ps_b[mi % 2][:],
                            w_k[:, ic * 1024 + oc * 128: ic * 1024 + oc * 128 + 128],
                            xk_sb[:, ic * 1024 + st * 512: ic * 1024 + st * 512 + 512],
                            start=(ic == 0), stop=(ic == NCH - 1))
                    return last
                dep = sh_all + [s_wk, prev.get(("ev_k", mi - 2))]
                s_mm = p.step("tensor", fn_mm, deps=dep, name=f"f2_km_{TB}_{oc}_{st}")
                prev[("mm_k_last", TB)] = s_mm

                def fn_rl(mi=mi):
                    return nc.scalar.activation(rt_t[:, (mi % 2) * 512:(mi % 2 + 1) * 512],
                                                ps_b[mi % 2][:], AF.Relu)
                s_rl = p.step("scalar", fn_rl, deps=[s_mm, prev.get(("ev_k", mi - 2))],
                              name=f"f2_kr_{TB}_{oc}_{st}")

                def fn_ev(oc=oc, st=st, mi=mi):
                    return nc.vector.tensor_mul(
                        ka_sb[:, oc * 1024 + st * 512: oc * 1024 + st * 512 + 512],
                        rt_t[:, (mi % 2) * 512:(mi % 2 + 1) * 512], ps_b[mi % 2][:])
                prev[("ev_k", mi)] = p.step("vector", fn_ev, deps=[s_rl],
                                            name=f"f2_ke_{TB}_{oc}_{st}")
                mi += 1

        ka_all = [prev[("ev_k", i)] for i in range(16)]
        r_all = [prev[("ev_r", i)] for i in range(16)]

        mi = 0
        for oc in range(NCH):
            for st in range(2):
                tt = TB * 2 + st

                def fn_mm(oc=oc, st=st, mi=mi):
                    last = None
                    for ic in range(NCH):
                        last = nc.tensor.matmul(
                            ps_a[mi % 2][:],
                            w_v[:, ic * 1024 + oc * 128: ic * 1024 + oc * 128 + 128],
                            ka_sb[:, ic * 1024 + st * 512: ic * 1024 + st * 512 + 512],
                            start=(ic == 0), stop=(ic == NCH - 1))
                    return last
                dep = ka_all + [s_wv, prev.get(("ev_v", mi - 2)), prev.get(("ev_r", 15))]
                s_mm = p.step("tensor", fn_mm, deps=dep, name=f"f2_vm_{TB}_{oc}_{st}")
                prev[("mm_v_last", TB)] = s_mm

                def fn_ev(oc=oc, st=st, mi=mi, tt=tt):
                    sl = slice(oc * TOK + tt * TT, oc * TOK + tt * TT + TT)
                    p.chain(nc.vector.tensor_mul(
                        rt_t[:, (mi % 2) * 512:(mi % 2 + 1) * 512],
                        r_sb[:, oc * 1024 + st * 512: oc * 1024 + st * 512 + 512],
                        ps_a[mi % 2][:]))
                    return nc.vector.tensor_add(io.ybuf[:, sl],
                                                rt_t[:, (mi % 2) * 512:(mi % 2 + 1) * 512],
                                                x[:, sl])
                s_ev = p.step("vector", fn_ev, deps=[s_mm] + r_all, late=True,
                              name=f"f2_ve_{TB}_{oc}_{st}")
                prev[("ev_v", mi)] = s_ev
                yform.setdefault(tt, []).append(s_ev)
                mi += 1

    ps_ctx.close()
    big_ctx.close()
    ln_ctx = ExitStack()
    emit_postln(p, io, ln_ctx, lambda tt: yform[tt] + yform[NTT - 1],
                final=(io.cur_phase == TOTAL_PHASES - 1), tag="p5")
    ln_ctx.close()
    ctx.close()




def phase_l3_ffn(p, io):
    nc = p.nc
    ctx = ExitStack()
    x = io.cur
    NT4 = 4

    RMP = p.input("RM_P", [16, 256 * 16], F32)
    RUPP = p.input("RUP_P", [16, 256 * 64], F32)
    WCP = p.input("W_cp", [16384, 1024], F32)

    C_scr = nc.dram_tensor("f3_cscr", [2, 16, 8192], BF16)
    XB_scr = nc.dram_tensor("f3_xb", [1024, 512], BF16)
    Z_scr = nc.dram_tensor("f3_z", [16384, TOK], BF16)

    cp_ctx = ExitStack()
    cp = cp_ctx.enter_context
    rm_sb = cp(nc.sbuf_tensor("f3_rm", [16, 256 * 16], BF16))
    rup_sb = cp(nc.sbuf_tensor("f3_rup", [16, 256 * 64], BF16))
    cstg = cp(nc.sbuf_tensor("f3_cstg", [16, 8192], BF16))
    ps_c = cp(nc.psum_tensor("f3_psc", [128, 512], F32))
    prev = {}

    def fn_rl():
        p.dma(rm_sb[:], RMP)
        return p.dma(rup_sb[:], RUPP)
    s_rl = p.step("gpsimd", fn_rl, kind="dma", n_dma=2, name="f3_rl")

    for j in range(2):
        for g in range(16):
            def fn_cm(j=j, g=g):
                last = None
                for s in range(8):
                    hrp = g * 8 + s
                    h, rp = hrp // 32, hrp % 32
                    hr = h * 64 + rp * 2 + j
                    last = nc.tensor.matmul(
                        ps_c[0:16, s * 64:(s + 1) * 64],
                        rm_sb[:, hr * 16:(hr + 1) * 16],
                        rup_sb[:, hr * 64:(hr + 1) * 64],
                        start=True, stop=True)
                return last
            dep = [s_rl, prev.get(("ce", j, g - 1))]
            if g == 0 and j > 0:
                dep.append(prev.get(("ce", j - 1, 15)))
            s_cm = p.step("tensor", fn_cm, deps=dep, name=f"f3_cm_{j}_{g}")

            def fn_ce(g=g):
                return nc.vector.tensor_copy(cstg[:, g * 512:(g + 1) * 512],
                                             ps_c[0:16, :])
            s_ce = p.step("vector", fn_ce, deps=[s_cm, prev.get(("cso", j - 1))],
                          name=f"f3_ce_{j}_{g}")
            prev[("ce", j, g)] = s_ce

        def fn_cso(j=j):
            return p.dma(C_scr[j], cstg[0:16, :].rearrange("p (a c) -> p a c", a=16))
        s_cso = p.step("gpsimd", fn_cso, deps=[prev[("ce", j, g)] for g in range(16)],
                       kind="dma", n_dma=1, name=f"f3_cso_{j}")
        prev[("cso", j)] = s_cso
    cp_ctx.close()

    z1_ctx = ExitStack()
    z1 = z1_ctx.enter_context
    C_sb = z1(nc.sbuf_tensor("f3_C", [128, 128 * 64], BF16))
    xr2 = z1(nc.sbuf_tensor("f3_xr2", [128, 32 * 512], BF16))
    zstage = z1(nc.sbuf_tensor("f3_zst", [128, 2 * 512], BF16))
    ps_z = [z1(nc.psum_tensor(f"f3_psz{i}", [128, 512], F32)) for i in range(2)]

    def fn_cli():
        last = None
        for j in range(2):
            last = p.dma(C_sb[64 * j:64 * j + 16, :]
                         .rearrange("p (a c) -> p a c", a=16), C_scr[j])
        return last
    s_cli = p.step("gpsimd", fn_cli, deps=[prev[("cso", 1)]], kind="dma", n_dma=2,
                   name="f3_cli")

    xb_view = XB_scr.ap().rearrange("(rp j i) t -> rp j i t", j=2, i=16)
    for t4 in range(NT4):
        t0 = t4 * 512

        def fn_xo(t0=t0):
            last = None
            for c in range(NCH):
                last = p.dma(XB_scr[c * 128:(c + 1) * 128, :],
                             x[:, c * TOK + t0: c * TOK + t0 + 512])
            return last
        dep = [prev.get(("xrl", t4 - 1))] + ([s_cli] if t4 == 0 else [])
        s_xo = p.step("gpsimd", fn_xo, deps=dep, kind="dma", n_dma=8, name=f"f3_xo_{t4}")

        def fn_xrl():
            last = None
            for j in range(2):
                src = xb_view[:, j, :, :].transpose([1, 0, 2])
                last = p.dma(xr2[64 * j:64 * j + 16, :]
                             .rearrange("p (rp t) -> p rp t", t=512), src)
            return last
        dep = [s_xo] + [prev[("zm", hh, 31, t4 - 1)] for hh in range(4) if t4 >= 1]
        s_xrl = p.step("gpsimd", fn_xrl, deps=dep, kind="dma", n_dma=2, name=f"f3_xrl_{t4}")
        prev[("xrl", t4)] = s_xrl

        for h in range(4):
            for rp in range(32):
                bk = rp % 2

                def fn_zm(h=h, rp=rp, bk=bk):
                    nc.tensor.matmul(
                        ps_z[bk][0:64, :],
                        C_sb[0:16, (h * 32 + rp) * 64:(h * 32 + rp + 1) * 64],
                        xr2[0:16, rp * 512:(rp + 1) * 512],
                        start=True, stop=True)
                    return nc.tensor.matmul(
                        ps_z[bk][64:128, :],
                        C_sb[64:80, (h * 32 + rp) * 64:(h * 32 + rp + 1) * 64],
                        xr2[64:80, rp * 512:(rp + 1) * 512],
                        start=True, stop=True)
                dep = [s_cli, s_xrl, prev.get(("ze", h, rp - 2, t4))]
                if rp < 2 and h > 0:
                    dep.append(prev.get(("ze", h - 1, 30 + rp, t4)))
                s_zm = p.step("tensor", fn_zm, deps=dep, name=f"f3_zm_{t4}_{h}_{rp}")
                prev[("zm", h, rp, t4)] = s_zm

                def fn_ze(bk=bk):
                    return nc.scalar.activation(zstage[:, bk * 512:(bk + 1) * 512],
                                                ps_z[bk][:], AF.Gelu)
                dep_e = [s_zm, prev.get(("zsp", h, rp - 2, t4)),
                         (prev.get(("zsp", h - 1, 30 + rp, t4)) if rp < 2 and h > 0 else None),
                         (prev.get(("zsp", 3, 30 + rp, t4 - 1)) if rp < 2 and h == 0 else None)]
                s_ze = p.step("scalar", fn_ze, deps=dep_e, name=f"f3_ze_{t4}_{h}_{rp}")
                prev[("ze", h, rp, t4)] = s_ze

                def fn_zs(h=h, rp=rp, bk=bk, t0=t0):
                    zr = h * 4096 + rp * 128
                    return p.dma(Z_scr[zr:zr + 128, t0:t0 + 512],
                                 zstage[:, bk * 512:(bk + 1) * 512])
                s_zs = p.step("gpsimd", fn_zs, deps=[s_ze], kind="dma", n_dma=1,
                              name=f"f3_zs_{t4}_{h}_{rp}")
                prev[("zsp", h, rp, t4)] = s_zs
    zsp_all = [prev[("zsp", h, rp, t4)] for h in range(4) for rp in range(32)
               for t4 in range(NT4)]
    z1_ctx.close()

    c2_ctx = ExitStack()
    c2 = c2_ctx.enter_context
    wcp_sb = [c2(nc.sbuf_tensor(f"f3_wcp{i}", [128, 8 * 1024], BF16)) for i in range(2)]
    zsec = [c2(nc.sbuf_tensor(f"f3_zsec{i}", [128, 8 * 512], BF16)) for i in range(2)]
    y_acc = c2(nc.sbuf_tensor("f3_y", [128, 32 * 512], F32))
    ps_y = [c2(nc.psum_tensor(f"f3_psy{i}", [128, 512], F32)) for i in range(2)]

    wl_reader = [None, None]
    zl_reader = [None, None]
    prev_y = {}
    zi = 0
    for ws in range(16):
        buf = ws % 2

        def fn_wl(ws=ws, buf=buf):
            last = None
            for c in range(8):
                last = p.dma(wcp_sb[buf][:, c * 1024:(c + 1) * 1024],
                             WCP[(ws * 8 + c) * 128:(ws * 8 + c + 1) * 128, :])
            return last
        dep = [wl_reader[buf]] + (zsp_all if ws < 2 else [])
        s_wl = p.step("gpsimd", fn_wl, deps=dep, kind="dma", n_dma=8,
                      name=f"f3_wl_{ws}")

        for tt in range(NTT):
            zbuf = zi % 2

            def fn_zl(ws=ws, zbuf=zbuf, tt=tt):
                last = None
                for c in range(8):
                    last = p.dma(zsec[zbuf][:, c * 512:(c + 1) * 512],
                                 Z_scr[(ws * 8 + c) * 128:(ws * 8 + c + 1) * 128,
                                       tt * TT: tt * TT + 512])
                return last
            dep = [zl_reader[zbuf]] + \
                [prev[("zsp", ws // 4, (ws % 4) * 8 + c, tt)] for c in range(8)] + \
                (zsp_all if ws < 1 else [])
            s_zl = p.step("sync", fn_zl, deps=dep, kind="dma", n_dma=8,
                          name=f"f3_zl_{ws}_{tt}")

            for oc in range(NCH):
                k2 = oc % 2

                def fn_ym(oc=oc, buf=buf, zbuf=zbuf, k2=k2):
                    last = None
                    for c in range(8):
                        last = nc.tensor.matmul(
                            ps_y[k2][:],
                            wcp_sb[buf][:, c * 1024 + oc * 128: c * 1024 + oc * 128 + 128],
                            zsec[zbuf][:, c * 512:(c + 1) * 512],
                            start=(c == 0), stop=(c == 7))
                    return last
                dep = [s_wl, s_zl, prev_y.get(k2)]
                s_ym = p.step("tensor", fn_ym, deps=dep, name=f"f3_ym_{ws}_{tt}_{oc}")
                wl_reader[buf] = s_ym
                zl_reader[zbuf] = s_ym

                def fn_ye(oc=oc, tt=tt, ws=ws, k2=k2):
                    ysl = y_acc[:, (tt * 8 + oc) * 512:(tt * 8 + oc + 1) * 512]
                    if ws == 0:
                        return nc.vector.tensor_copy(ysl, ps_y[k2][:])
                    return nc.vector.tensor_add(ysl, ysl, ps_y[k2][:])
                s_ye = p.step("vector", fn_ye, deps=[s_ym], late=True,
                              name=f"f3_ye_{ws}_{tt}_{oc}")
                prev_y[k2] = s_ye
                prev_y[("ysl", tt, oc)] = s_ye
            zi += 1

    yform = {}
    for tt in range(NTT):
        steps = []
        for oc in range(NCH):
            def fn_yf(oc=oc, tt=tt):
                sl = slice(oc * TOK + tt * TT, oc * TOK + tt * TT + TT)
                return nc.vector.tensor_add(
                    io.ybuf[:, sl],
                    y_acc[:, (tt * 8 + oc) * 512:(tt * 8 + oc + 1) * 512], x[:, sl])
            steps.append(p.step("vector", fn_yf, deps=[prev_y[("ysl", tt, oc)]],
                                late=True, name=f"f3_yf_{tt}_{oc}"))
        yform[tt] = steps
    c2_ctx.close()

    ln_ctx = ExitStack()
    emit_postln(p, io, ln_ctx, lambda tt: yform[tt] + yform[NTT - 1],
                final=(io.cur_phase == TOTAL_PHASES - 1), tag="p7")
    ln_ctx.close()
    ctx.close()





def phase_l3_attn(p, io):
    nc = p.nc
    ctx = ExitStack()
    e = ctx.enter_context
    x = io.cur
    DI = 2048
    NDC = 16

    FLAG = p.input("PFLAG", [128, 1], F32)
    FLAGR = p.input("PFLAGR", [128, 1], F32)
    CONVW = p.input("CONVW", [128, 48], F32)
    CONVB = p.input("CONVB", [128, 16], F32)
    WIN = p.input("W_in3", [1024, 4096], F32)
    WDT = p.input("W_dt3", [2048, 2048], F32)
    A3 = p.input("A3", [2048, 2048], F32)
    B3 = p.input("B3", [2048, 2048], F32)
    C3 = p.input("C3", [2048, 2048], F32)
    WOUT = p.input("W_out3", [2048, 1024], F32)

    flag_sb = e(nc.sbuf_tensor("l3_flag", [128, 1], F32))
    flagr_sb = e(nc.sbuf_tensor("l3_flagr", [128, 1], F32))
    cw_sb = e(nc.sbuf_tensor("l3_cw", [128, 48], F32))
    cb_sb = e(nc.sbuf_tensor("l3_cb", [128, 16], F32))
    stg16 = e(nc.sbuf_tensor("l3_stg16", [128, 16], BF16))
    recv16 = e(nc.sbuf_tensor("l3_recv16", [128, 16], BF16))
    recv16b = e(nc.sbuf_tensor("l3_recv16b", [128, 16], BF16))
    xhl = e(nc.sbuf_tensor("l3_xhl", [128, 8], BF16))
    xhr = e(nc.sbuf_tensor("l3_xhr", [128, 8], BF16))
    ones512 = e(nc.sbuf_tensor("l3_ones", [128, 512], BF16))
    carry = e(nc.sbuf_tensor("l3_carry", [128, 4 * 16], F32))
    recv32 = e(nc.sbuf_tensor("l3_recv32", [128, 32], F32))
    xs0c = e(nc.sbuf_tensor("l3_xs0c", [128, 16], F32))
    dsoff = e(nc.sbuf_tensor("l3_dsoff", [128, 16], F32))

    U_scr = nc.dram_tensor("l3_u", [2048, 2050], BF16)
    Z_scr = nc.dram_tensor("l3_zs", [2048, TOK], BF16)
    UCV_scr = nc.dram_tensor("l3_ucv", [2048, TOK], BF16)
    XS_scr = nc.dram_tensor("l3_xs", [2048, TOK], BF16)
    PP_scr = nc.dram_tensor("l3_pp", [2048, TOK], BF16)
    DS_scr = nc.dram_tensor("l3_ds", [2048, TOK], BF16)
    CPU_scr = nc.dram_tensor("l3_cpu", [2048, TOK], BF16)
    cc1_in = nc.dram_tensor("l3cc1i", [128, 16], BF16)
    cc1_out = nc.dram_tensor("l3cc1o", [256, 16], BF16)
    cc2_in = nc.dram_tensor("l3cc2i", [128, 32], F32)
    cc2_out = nc.dram_tensor("l3cc2o", [256, 32], F32)

    def fn_cst():
        p.dma(flag_sb[:], FLAG)
        p.dma(flagr_sb[:], FLAGR)
        p.dma(cw_sb[:], CONVW)
        return p.dma(cb_sb[:], CONVB)
    s_cst = p.step("gpsimd", fn_cst, kind="dma", n_dma=4, name="l3_cst")

    def fn_ini():
        p.chain(nc.vector.memset(ones512[:], 1.0))
        p.chain(nc.vector.memset(carry[:, 0:16], 0.0))
        p.chain(nc.vector.memset(carry[:, 16:32], 1.0))
        p.chain(nc.vector.memset(carry[:, 32:48], 1.0))
        return nc.vector.memset(carry[:, 48:64], 0.0)
    s_ini = p.step("vector", fn_ini, name="l3_ini")

    def fn_stg():
        xv = x[:].rearrange("p (c t) -> p c t", t=TOK)
        p.chain(nc.vector.tensor_copy(stg16[:, 0:8], xv[:, :, 0:1].squeeze(2)))
        return nc.vector.tensor_copy(stg16[:, 8:16], xv[:, :, TOK - 1:TOK].squeeze(2))
    s_stg = p.step("vector", fn_stg, deps=[], name="l3_stg")

    def fn_snd():
        return p.dma(cc1_in.ap(), stg16[:])
    s_snd = p.step("gpsimd", fn_snd, deps=[s_stg], kind="dma", n_dma=1, name="l3_snd")

    def fn_cc1():
        return nc.gpsimd.collective_compute(
            "AllGather", ALU.bypass, replica_groups=PAIRS,
            ins=[cc1_in.ap().opt()], outs=[cc1_out.ap().opt()])
    s_cc1 = p.step("gpsimd", fn_cc1, deps=[s_snd], kind="collective", name="l3_cc1")

    def fn_rcv():
        p.dma(recv16[:], cc1_out[0:128, :])
        return p.dma(recv16b[:], cc1_out[128:256, :])
    s_rcv = p.step("gpsimd", fn_rcv, deps=[s_cc1], kind="dma", n_dma=2, name="l3_rcv")

    def fn_xh():
        p.chain(nc.vector.tensor_scalar(xhl[:], recv16[:, 8:16], flag_sb[:, 0:1],
                                        None, ALU.mult))
        return nc.vector.tensor_scalar(xhr[:], recv16b[:, 0:8], flagr_sb[:, 0:1],
                                       None, ALU.mult)
    s_xh = p.step("vector", fn_xh, deps=[s_rcv, s_cst], late=True, name="l3_xh")

    l1_ctx = ExitStack()
    l1 = l1_ctx.enter_context
    wsec = [l1(nc.sbuf_tensor(f"l3_ws{i}", [128, NCH * 1024], BF16)) for i in range(2)]
    xzst = l1(nc.sbuf_tensor("l3_xzst", [128, 2 * 512], BF16))
    hst = l1(nc.sbuf_tensor("l3_hst", [128, 4], BF16))
    ps_1 = [l1(nc.psum_tensor(f"l3_ps1{i}", [128, 512], F32)) for i in range(2)]
    ps_h = l1(nc.psum_tensor("l3_psh", [128, 4], F32))

    prev = {}
    mi = 0
    for sec in range(4):
        buf = sec % 2

        def fn_wl(sec=sec, buf=buf):
            last = None
            for c in range(NCH):
                last = p.dma(wsec[buf][:, c * 1024:(c + 1) * 1024],
                             WIN[c * 128:(c + 1) * 128, sec * 1024:(sec + 1) * 1024])
            return last
        s_wl = p.step("gpsimd", fn_wl, deps=[prev.get(("l1mm_last", sec - 2))],
                      kind="dma", n_dma=NCH, name=f"l3_wl{sec}")

        for oc8 in range(8):
            ocg = sec * 8 + oc8
            if sec < 2:
                def fn_hm2(oc8=oc8, buf=buf):
                    last = None
                    for ic in range(NCH):
                        last = nc.tensor.matmul(
                            ps_h[:, 0:1],
                            wsec[buf][:, ic * 1024 + oc8 * 128: ic * 1024 + oc8 * 128 + 128],
                            xhl[:, ic:ic + 1], start=(ic == 0), stop=(ic == NCH - 1))
                    for ic in range(NCH):
                        last = nc.tensor.matmul(
                            ps_h[:, 1:2],
                            wsec[buf][:, ic * 1024 + oc8 * 128: ic * 1024 + oc8 * 128 + 128],
                            xhr[:, ic:ic + 1], start=(ic == 0), stop=(ic == NCH - 1))
                    return last
                s_hm = p.step("tensor", fn_hm2, deps=[s_wl, s_xh, prev.get(("he", ocg - 1))],
                              name=f"l3_hm_{ocg}")

                def fn_he():
                    return nc.vector.tensor_copy(hst[:, 0:2], ps_h[:, 0:2])
                s_he = p.step("vector", fn_he, deps=[s_hm, prev.get(("hs", ocg - 1))],
                              name=f"l3_he_{ocg}")
                prev[("he", ocg)] = s_he

                def fn_hs(ocg=ocg):
                    p.dma(U_scr[ocg * 128:(ocg + 1) * 128, 0:1], hst[:, 0:1],
                          allow_slow_non_contiguous=True)
                    return p.dma(U_scr[ocg * 128:(ocg + 1) * 128, 2049:2050], hst[:, 1:2],
                                 allow_slow_non_contiguous=True)
                s_hs = p.step("gpsimd", fn_hs, deps=[s_he], kind="dma", n_dma=2,
                              name=f"l3_hs_{ocg}")
                prev[("hs", ocg)] = s_hs

            for tile in range(NTT):
                def fn_mm(oc8=oc8, tile=tile, buf=buf, mi=mi):
                    last = None
                    for ic in range(NCH):
                        last = nc.tensor.matmul(
                            ps_1[mi % 2][:],
                            wsec[buf][:, ic * 1024 + oc8 * 128: ic * 1024 + oc8 * 128 + 128],
                            x[:, ic * TOK + tile * TT: ic * TOK + tile * TT + TT],
                            start=(ic == 0), stop=(ic == NCH - 1))
                    return last
                dep = [s_wl, prev.get(("l1e", mi - 2))]
                s_mm = p.step("tensor", fn_mm, deps=dep, name=f"l3_xzm_{ocg}_{tile}")
                prev[("l1mm_last", sec)] = s_mm

                def fn_ev(mi=mi):
                    return nc.vector.tensor_copy(
                        xzst[:, (mi % 2) * 512:(mi % 2 + 1) * 512], ps_1[mi % 2][:])
                s_ev = p.step("vector", fn_ev, deps=[s_mm, prev.get(("l1s", mi - 2))],
                              name=f"l3_xze_{ocg}_{tile}")
                prev[("l1e", mi)] = s_ev

                def fn_sp(ocg=ocg, tile=tile, mi=mi):
                    if ocg < 16:
                        return p.dma(U_scr[ocg * 128:(ocg + 1) * 128,
                                           1 + tile * TT: 1 + tile * TT + TT],
                                     xzst[:, (mi % 2) * 512:(mi % 2 + 1) * 512])
                    return p.dma(Z_scr[(ocg - 16) * 128:(ocg - 15) * 128,
                                       tile * TT: tile * TT + TT],
                                 xzst[:, (mi % 2) * 512:(mi % 2 + 1) * 512])
                s_sp = p.step("gpsimd", fn_sp, deps=[s_ev], kind="dma", n_dma=1,
                              name=f"l3_xzs_{ocg}_{tile}")
                prev[("l1s", mi)] = s_sp
                mi += 1
    l1_done = [prev[("l1s", i)] for i in range(mi)] + \
              [prev[("hs", g)] for g in range(16)]
    l1_ctx.close()

    l2_ctx = ExitStack()
    l2 = l2_ctx.enter_context
    ub = [l2(nc.sbuf_tensor(f"l3_ub{i}", [128, 2050], BF16)) for i in range(2)]
    ucb = [l2(nc.sbuf_tensor(f"l3_ucb{i}", [128, TOK], BF16)) for i in range(2)]
    ct1 = l2(nc.sbuf_tensor("l3_ct1", [128, TOK], F32))

    for oc in range(NDC):
        def fn_ul(oc=oc):
            return p.dma(ub[oc % 2][:], U_scr[oc * 128:(oc + 1) * 128, :])
        dep = l1_done if oc < 2 else [prev[("cve", oc - 2)]]
        s_ul = p.step("gpsimd", fn_ul, deps=dep, kind="dma", n_dma=1, name=f"l3_ul{oc}")

        def fn_cv(oc=oc):
            b = ub[oc % 2]
            p.chain(nc.vector.tensor_scalar(ct1[:], b[:, 0:2048],
                                            cw_sb[:, oc * 3:oc * 3 + 1], None, ALU.mult))
            p.chain(nc.vector.scalar_tensor_tensor(
                ct1[:], b[:, 1:2049], cw_sb[:, oc * 3 + 1:oc * 3 + 2], ct1[:],
                ALU.mult, ALU.add))
            return nc.vector.scalar_tensor_tensor(
                ct1[:], b[:, 2:2050], cw_sb[:, oc * 3 + 2:oc * 3 + 3], ct1[:],
                ALU.mult, ALU.add)
        s_cv = p.step("vector", fn_cv, deps=[s_ul, s_cst, prev.get(("cva", oc - 1))],
                      late=True, name=f"l3_cv{oc}")

        def fn_cva(oc=oc):
            return nc.scalar.activation(ucb[oc % 2][:], ct1[:], AF.Silu,
                                        bias=cb_sb[:, oc:oc + 1])
        s_cva = p.step("scalar", fn_cva, deps=[s_cv, prev.get(("cve", oc - 2))],
                       name=f"l3_cva{oc}")
        prev[("cva", oc)] = s_cva

        def fn_cve(oc=oc):
            return p.dma(UCV_scr[oc * 128:(oc + 1) * 128, :], ucb[oc % 2][:])
        s_cve = p.step("gpsimd", fn_cve, deps=[s_cva], kind="dma", n_dma=1,
                       name=f"l3_cve{oc}")
        prev[("cve", oc)] = s_cve
    l2_done = [prev[("cve", NDC - 1)], prev[("cve", NDC - 2)]]
    l2_ctx.close()

    l34_ctx = ExitStack()
    delt_full = l34_ctx.enter_context(
        nc.sbuf_tensor("l3_delt", [128, NDC * 2048], BF16))

    l3_ctx = ExitStack()
    l3 = l3_ctx.enter_context
    wdt = [l3(nc.sbuf_tensor(f"l3_wdt{i}", [128, 16 * 512], BF16)) for i in range(2)]
    ucvt = [l3(nc.sbuf_tensor(f"l3_ucvt{i}", [128, 16 * 512], BF16)) for i in range(2)]
    sps = l3(nc.sbuf_tensor("l3_sps", [128, 512], F32))
    ps_3 = [l3(nc.psum_tensor(f"l3_ps3{i}", [128, 512], F32)) for i in range(2)]

    di = 0
    for sec in range(4):
        buf = sec % 2

        def fn_wdl(sec=sec, buf=buf):
            last = None
            for c in range(NDC):
                last = p.dma(wdt[buf][:, c * 512:(c + 1) * 512],
                             WDT[c * 128:(c + 1) * 128, sec * 512:(sec + 1) * 512])
            return last
        dep = l2_done + [prev.get(("l3mm_last", sec - 2))]
        s_wdl = p.step("gpsimd", fn_wdl, deps=dep, kind="dma", n_dma=NDC,
                       name=f"l3_wdl{sec}")

        for tile in range(NTT):
            tix = sec * NTT + tile
            tslot = tix % 2

            def fn_uld(tile=tile, tslot=tslot):
                last = None
                for c in range(NDC):
                    last = p.dma(ucvt[tslot][:, c * 512:(c + 1) * 512],
                                 UCV_scr[c * 128:(c + 1) * 128, tile * TT: tile * TT + TT])
                return last
            dep = l2_done + [prev.get(("l3mmt", tix - 2))]
            s_uld = p.step("gpsimd", fn_uld, deps=dep, kind="dma", n_dma=NDC,
                           name=f"l3_uld{sec}_{tile}")

            for oc4 in range(4):
                ocg = sec * 4 + oc4

                def fn_mm(oc4=oc4, buf=buf, di=di, tslot=tslot):
                    last = None
                    for ic in range(NDC):
                        last = nc.tensor.matmul(
                            ps_3[di % 2][:],
                            wdt[buf][:, ic * 512 + oc4 * 128: ic * 512 + oc4 * 128 + 128],
                            ucvt[tslot][:, ic * 512:(ic + 1) * 512],
                            start=(ic == 0), stop=(ic == NDC - 1))
                    return last
                dep = [s_wdl, s_uld, prev.get(("l3sx", di - 2))]
                s_mm = p.step("tensor", fn_mm, deps=dep, name=f"l3_dm_{ocg}_{tile}")
                prev[("l3mm_last", sec)] = s_mm
                prev[("l3mmt", tix)] = s_mm

                def fn_spx(di=di, ocg=ocg, tile=tile):
                    p.chain(nc.scalar.activation(sps[:], ps_3[di % 2][:], AF.Exp))
                    return nc.scalar.activation(
                        delt_full[:, ocg * 2048 + tile * TT: ocg * 2048 + tile * TT + TT],
                        sps[:], AF.Ln, bias=1.0)
                s_sx = p.step("scalar", fn_spx, deps=[s_mm, prev.get(("l3sx", di - 1))],
                              name=f"l3_sx_{ocg}_{tile}")
                prev[("l3sx", di)] = s_sx
                di += 1
    l3_sx_last = prev[("l3sx", di - 1)]
    l3_mm_last = prev[("l3mm_last", 3)]
    l3_ctx.close()

    l4_ctx = ExitStack()
    l4 = l4_ctx.enter_context
    af4 = [l4(nc.sbuf_tensor(f"l3_af4{i}", [128, 2048], F32)) for i in range(2)]
    ea = [l4(nc.sbuf_tensor(f"l3_ea{i}", [128, 2048], BF16)) for i in range(2)]
    bb = [l4(nc.sbuf_tensor(f"l3_bb{i}", [128, 2048], BF16)) for i in range(2)]
    ucv4 = [l4(nc.sbuf_tensor(f"l3_ucv4{i}", [128, 512], BF16)) for i in range(2)]
    dA = l4(nc.sbuf_tensor("l3_dA", [128, 512], F32))
    dbu = l4(nc.sbuf_tensor("l3_dbu", [128, 512], BF16))
    xs4 = [l4(nc.sbuf_tensor(f"l3_x4{i}", [128, 512], BF16)) for i in range(2)]
    pp4 = [l4(nc.sbuf_tensor(f"l3_p4{i}", [128, 512], BF16)) for i in range(2)]
    ds4 = [l4(nc.sbuf_tensor(f"l3_d4{i}", [128, 512], BF16)) for i in range(2)]
    cp4 = [l4(nc.sbuf_tensor(f"l3_c4{i}", [128, 512], BF16)) for i in range(2)]
    ym4 = l4(nc.sbuf_tensor("l3_ym4", [128, 512], BF16))
    up4 = l4(nc.sbuf_tensor("l3_up4", [128, 512], BF16))
    ps_a4 = l4(nc.psum_tensor("l3_psa4", [128, 512], F32))
    ps_b4 = l4(nc.psum_tensor("l3_psb4", [128, 512], F32))

    li = 0
    for oc in range(NDC):
        def fn_al(oc=oc):
            return p.dma(af4[oc % 2][:].rearrange("p (c d) -> p c d", d=128),
                         A3.rearrange("(c p) d -> c p d", p=128)
                         [:, :, oc * 128:(oc + 1) * 128].transpose([1, 0, 2]))
        dep = [prev.get(("ax", oc - 2))] + ([l3_mm_last, l3_sx_last] if oc < 2 else [])
        s_al = p.step("sync", fn_al, deps=dep, kind="dma", n_dma=1, name=f"l3_al{oc}")

        def fn_bl(oc=oc):
            return p.dma(bb[oc % 2][:].rearrange("p (c d) -> p c d", d=128),
                         B3.rearrange("(c p) d -> c p d", p=128)
                         [:, :, oc * 128:(oc + 1) * 128].transpose([1, 0, 2]))
        dep = [prev.get(("mm4last", oc - 2))] + ([l3_mm_last, l3_sx_last] if oc < 2 else [])
        s_bl = p.step("gpsimd", fn_bl, deps=dep, kind="dma", n_dma=1, name=f"l3_bl{oc}")

        def fn_ax(oc=oc):
            return nc.scalar.activation(ea[oc % 2][:], af4[oc % 2][:], AF.Exp)
        s_ax = p.step("scalar", fn_ax, deps=[s_al, prev.get(("mm4last", oc - 2))],
                      name=f"l3_ax{oc}")
        prev[("ax", oc)] = s_ax

        for TB in range(NTT):
            slot = li % 2

            def fn_ul4(oc=oc, TB=TB, slot=slot):
                return p.dma(ucv4[slot][:],
                             UCV_scr[oc * 128:(oc + 1) * 128, TB * TT: TB * TT + TT])
            s_ul4 = p.step("sync", fn_ul4, deps=[prev.get(("sc4", li - 2))],
                           kind="dma", n_dma=1, name=f"l3_ul4_{oc}_{TB}")

            def fn_mm4(oc=oc, TB=TB):
                last = None
                for ic in range(NDC):
                    nc.tensor.matmul(
                        ps_a4[:], ea[oc % 2][:, ic * 128:(ic + 1) * 128],
                        delt_full[:, ic * 2048 + TB * TT: ic * 2048 + TB * TT + TT],
                        start=(ic == 0), stop=(ic == NDC - 1))
                for ic in range(NDC):
                    last = nc.tensor.matmul(
                        ps_b4[:], bb[oc % 2][:, ic * 128:(ic + 1) * 128],
                        delt_full[:, ic * 2048 + TB * TT: ic * 2048 + TB * TT + TT],
                        start=(ic == 0), stop=(ic == NDC - 1))
                return last
            dep = [s_ax, s_bl, l3_sx_last,
                   prev.get(("da", li - 1)), prev.get(("dbu", li - 1))]
            s_mm4 = p.step("tensor", fn_mm4, deps=dep, name=f"l3_mm4_{oc}_{TB}")
            prev[("mm4last", oc)] = s_mm4

            def fn_da():
                return nc.scalar.activation(dA[:], ps_a4[:], AF.Exp, scale=-1.0)
            s_da = p.step("scalar", fn_da, deps=[s_mm4, prev.get(("sc4", li - 1))],
                          name=f"l3_da_{oc}_{TB}")
            prev[("da", li)] = s_da

            def fn_dbu(slot=slot):
                return nc.vector.tensor_mul(dbu[:], ps_b4[:], ucv4[slot][:])
            s_dbu = p.step("vector", fn_dbu, deps=[s_mm4, s_ul4,
                                                   prev.get(("sc4", li - 1))],
                           late=True, name=f"l3_dbu_{oc}_{TB}")
            prev[("dbu", li)] = s_dbu

            def fn_sc4(oc=oc, slot=slot):
                b = slot
                p.chain(nc.vector.tensor_tensor_scan(
                    xs4[b][:], dA[:], dbu[:], carry[:, oc:oc + 1], ALU.mult, ALU.add))
                p.chain(nc.vector.tensor_copy(carry[:, oc:oc + 1], xs4[b][:, 511:512]))
                p.chain(nc.vector.tensor_tensor_scan(
                    pp4[b][:], dA[:], ones512[:], carry[:, 16 + oc:17 + oc],
                    ALU.mult, ALU.mult))
                p.chain(nc.vector.tensor_copy(carry[:, 16 + oc:17 + oc], pp4[b][:, 511:512]))
                p.chain(nc.vector.tensor_mul(ym4[:], ucv4[slot][:], xs4[b][:]))
                p.chain(nc.vector.tensor_tensor_scan(
                    ds4[b][:], ones512[:], ym4[:], carry[:, 32 + oc:33 + oc],
                    ALU.mult, ALU.add))
                p.chain(nc.vector.tensor_copy(carry[:, 32 + oc:33 + oc], ds4[b][:, 511:512]))
                p.chain(nc.vector.tensor_mul(up4[:], ucv4[slot][:], pp4[b][:]))
                p.chain(nc.vector.tensor_tensor_scan(
                    cp4[b][:], ones512[:], up4[:], carry[:, 48 + oc:49 + oc],
                    ALU.mult, ALU.add))
                return nc.vector.tensor_copy(carry[:, 48 + oc:49 + oc], cp4[b][:, 511:512])
            dep = [s_da, s_dbu, s_ini, prev.get(("sp4", li - 2))]
            s_sc4 = p.step("vector", fn_sc4, deps=dep, late=True,
                           name=f"l3_sc4_{oc}_{TB}")
            prev[("sc4", li)] = s_sc4

            def fn_sp4(oc=oc, TB=TB, slot=slot):
                b = slot
                sl = slice(oc * 128, (oc + 1) * 128)
                tsl = slice(TB * TT, TB * TT + TT)
                p.dma(XS_scr[sl, tsl], xs4[b][:])
                p.dma(PP_scr[sl, tsl], pp4[b][:])
                p.dma(DS_scr[sl, tsl], ds4[b][:])
                return p.dma(CPU_scr[sl, tsl], cp4[b][:])
            s_sp4 = p.step("gpsimd", fn_sp4, deps=[s_sc4], kind="dma", n_dma=4,
                           name=f"l3_sp4_{oc}_{TB}")
            prev[("sp4", li)] = s_sp4
            li += 1
    l4_done = [prev[("sp4", i)] for i in range(li)]
    sc4_last = prev[("sc4", li - 1)]
    l4_ctx.close()
    l34_ctx.close()

    def fn_stg2():
        p.chain(nc.vector.tensor_copy(recv32[:, 0:16], carry[:, 0:16]))
        return nc.vector.tensor_copy(recv32[:, 16:32], carry[:, 32:48])
    s_stg2 = p.step("vector", fn_stg2, deps=[sc4_last],
                    late=True, name="l3_stg2")

    def fn_snd2():
        return p.dma(cc2_in.ap(), recv32[:])
    s_snd2 = p.step("gpsimd", fn_snd2, deps=[s_stg2], kind="dma", n_dma=1, name="l3_snd2")

    def fn_cc2():
        return nc.gpsimd.collective_compute(
            "AllGather", ALU.bypass, replica_groups=PAIRS,
            ins=[cc2_in.ap().opt()], outs=[cc2_out.ap().opt()])
    s_cc2 = p.step("gpsimd", fn_cc2, deps=[s_snd2], kind="collective", name="l3_cc2")

    def fn_rcv2():
        return p.dma(recv32[:], cc2_out[0:128, :])
    s_rcv2 = p.step("gpsimd", fn_rcv2, deps=[s_cc2, s_stg2], kind="dma", n_dma=1,
                    name="l3_rcv2")

    def fn_fixc():
        p.chain(nc.vector.tensor_scalar(xs0c[:], recv32[:, 0:16], flag_sb[:, 0:1],
                                        None, ALU.mult))
        p.chain(nc.vector.tensor_scalar(dsoff[:], recv32[:, 16:32], -1.0, None, ALU.add))
        return nc.vector.tensor_scalar(dsoff[:], dsoff[:], flag_sb[:, 0:1],
                                       None, ALU.mult)
    s_fixc = p.step("vector", fn_fixc, deps=[s_rcv2, s_cst], late=True, name="l3_fixc")

    l6_ctx = ExitStack()
    l6 = l6_ctx.enter_context
    xst = l6(nc.sbuf_tensor("l3_xst", [128, 16 * 512], BF16))
    dst6 = l6(nc.sbuf_tensor("l3_dst6", [128, 512], BF16))
    zt6 = l6(nc.sbuf_tensor("l3_zt6", [128, 16 * 512], BF16))
    dsf = l6(nc.sbuf_tensor("l3_dsf", [128, 16 * 512], BF16))
    ybr = l6(nc.sbuf_tensor("l3_ybr", [128, 16 * 512], BF16))
    cw6 = [l6(nc.sbuf_tensor(f"l3_cw6{i}", [128, 16 * 128], BF16)) for i in range(2)]
    wo6 = [l6(nc.sbuf_tensor(f"l3_wo6{i}", [128, 16 * 128], BF16)) for i in range(2)]
    pt6 = l6(nc.sbuf_tensor("l3_pt6", [128, 2 * 512], BF16))
    ps_6 = [l6(nc.psum_tensor(f"l3_ps6{i}", [128, 512], F32)) for i in range(2)]

    yform = {}
    for TB in range(NTT):
        tsl = slice(TB * TT, TB * TT + TT)

        def fn_ld6(TB=TB, tsl=tsl):
            last = None
            for c in range(NDC):
                p.dma(xst[:, c * 512:(c + 1) * 512], XS_scr[c * 128:(c + 1) * 128, tsl])
                p.dma(zt6[:, c * 512:(c + 1) * 512], Z_scr[c * 128:(c + 1) * 128, tsl])
                last = p.dma(dsf[:, c * 512:(c + 1) * 512],
                             DS_scr[c * 128:(c + 1) * 128, tsl])
            return last
        dep = l4_done + [prev.get(("xcm_last", TB - 1)), prev.get(("ye6", TB - 1, NDC - 1)),
                         prev.get(("we_last", TB - 1))]
        s_ld6 = p.step("gpsimd", fn_ld6, deps=dep, kind="dma", n_dma=3 * NDC,
                       name=f"l3_ld6_{TB}")

        fix_steps = []
        for oc in range(NDC):
            def fn_ldp(oc=oc, tsl=tsl):
                p.dma(pt6[:, 0:512], PP_scr[oc * 128:(oc + 1) * 128, tsl])
                return p.dma(pt6[:, 512:1024], CPU_scr[oc * 128:(oc + 1) * 128, tsl])
            dep = [prev.get(("fx", oc - 1))] + ([s_ld6] if oc == 0 else [])
            s_ldp = p.step("gpsimd", fn_ldp, deps=dep + l4_done, kind="dma", n_dma=2,
                           name=f"l3_ldp_{TB}_{oc}")

            def fn_fx(oc=oc):
                p.chain(nc.vector.scalar_tensor_tensor(
                    xst[:, oc * 512:(oc + 1) * 512], pt6[:, 0:512],
                    xs0c[:, oc:oc + 1], xst[:, oc * 512:(oc + 1) * 512],
                    ALU.mult, ALU.add))
                p.chain(nc.vector.scalar_tensor_tensor(
                    dsf[:, oc * 512:(oc + 1) * 512], pt6[:, 512:1024],
                    xs0c[:, oc:oc + 1], dsf[:, oc * 512:(oc + 1) * 512],
                    ALU.mult, ALU.add))
                return nc.vector.tensor_scalar(
                    dsf[:, oc * 512:(oc + 1) * 512], dsf[:, oc * 512:(oc + 1) * 512],
                    dsoff[:, oc:oc + 1], None, ALU.add)
            s_fx = p.step("vector", fn_fx, deps=[s_ldp, s_ld6, s_fixc,
                                                 prev.get(("fx", oc - 1))],
                          late=True, name=f"l3_fx_{TB}_{oc}")
            prev[("fx", oc)] = s_fx
            fix_steps.append(s_fx)

        for ro in range(NDC):
            def fn_cwl(ro=ro):
                return p.dma(cw6[ro % 2][:].rearrange("p (c r) -> p c r", r=128),
                             C3.rearrange(
                    "(c p) r -> c p r", p=128)[:, :, ro * 128:(ro + 1) * 128]
                    .transpose([1, 0, 2]))
            dep = [prev.get(("ye6", TB, ro - 2))] + ([fix_steps[0]] if ro == 0 else [])
            s_cwl = p.step("gpsimd", fn_cwl, deps=dep, kind="dma", n_dma=1,
                           name=f"l3_cwl_{TB}_{ro}")

            def fn_xcm(ro=ro):
                last = None
                for ic in range(NDC):
                    last = nc.tensor.matmul(
                        ps_6[ro % 2][:], cw6[ro % 2][:, ic * 128:(ic + 1) * 128],
                        xst[:, ic * 512:(ic + 1) * 512],
                        start=(ic == 0), stop=(ic == NDC - 1))
                return last
            dep = fix_steps + [s_cwl, prev.get(("ye6", TB, ro - 2))]
            s_xcm = p.step("tensor", fn_xcm, deps=dep, name=f"l3_xcm_{TB}_{ro}")
            prev[("xcm_last", TB)] = s_xcm

            def fn_ye6(ro=ro):
                p.chain(nc.vector.tensor_mul(ybr[:, ro * 512:(ro + 1) * 512],
                                             ps_6[ro % 2][:],
                                             zt6[:, ro * 512:(ro + 1) * 512]))
                return nc.vector.tensor_mul(ybr[:, ro * 512:(ro + 1) * 512],
                                            ybr[:, ro * 512:(ro + 1) * 512],
                                            dsf[:, ro * 512:(ro + 1) * 512])
            s_ye6 = p.step("vector", fn_ye6, deps=[s_xcm], late=True,
                           name=f"l3_ye6_{TB}_{ro}")
            prev[("ye6", TB, ro)] = s_ye6
        ye_all = [prev[("ye6", TB, ro)] for ro in range(NDC)]

        steps = []
        for od in range(NCH):
            def fn_wol(od=od):
                return p.dma(wo6[od % 2][:].rearrange("p (c r) -> p c r", r=128),
                             WOUT.rearrange(
                    "(c p) r -> c p r", p=128)[:, :, od * 128:(od + 1) * 128]
                    .transpose([1, 0, 2]))
            dep = [prev.get(("we6", TB, od - 2)), ye_all[-1] if od == 0 else None]
            s_wol = p.step("gpsimd", fn_wol, deps=dep, kind="dma", n_dma=1,
                           name=f"l3_wol_{TB}_{od}")

            def fn_wm(od=od):
                last = None
                for ic in range(NDC):
                    last = nc.tensor.matmul(
                        ps_6[od % 2][:], wo6[od % 2][:, ic * 128:(ic + 1) * 128],
                        ybr[:, ic * 512:(ic + 1) * 512],
                        start=(ic == 0), stop=(ic == NDC - 1))
                return last
            dep = ye_all + [s_wol, prev.get(("we6", TB, od - 2))]
            s_wm = p.step("tensor", fn_wm, deps=dep, name=f"l3_wm_{TB}_{od}")
            prev[("we_last", TB)] = s_wm

            def fn_we(od=od, TB=TB):
                sl = slice(od * TOK + TB * TT, od * TOK + TB * TT + TT)
                return nc.vector.tensor_add(io.ybuf[:, sl], ps_6[od % 2][:], x[:, sl])
            s_we = p.step("vector", fn_we, deps=[s_wm], late=True,
                          name=f"l3_we_{TB}_{od}")
            prev[("we6", TB, od)] = s_we
            steps.append(s_we)
        yform[TB] = steps
    l6_ctx.close()

    ln_ctx = ExitStack()
    emit_postln(p, io, ln_ctx, lambda tt: yform[tt] + yform[NTT - 1],
                final=(io.cur_phase == TOTAL_PHASES - 1), tag="p6")
    ln_ctx.close()
    ctx.close()


PHASES = {0: phase_l0_attn, 1: phase_l0_ffn, 2: phase_l1_attn, 3: phase_l1_ffn, 4: phase_l2_attn, 5: phase_l2_ffn, 7: phase_l3_ffn, 6: phase_l3_attn}



def build_program(phase_ids):
    nc = bass.Bass("TRN2", target_bir_lowering=False)
    ctx = ExitStack()
    p = P(nc, ctx)
    io = IO()
    with ctx:
        alloc_globals(p, io)
        p.barrier()
        if phase_ids[0] != 0:
            stream_in(p, io)
            p.barrier()
        for pid in phase_ids:
            io.cur_phase = pid
            PHASES[pid](p, io)
            p.barrier()
        if phase_ids[-1] == TOTAL_PHASES - 1:
            def fn_fin():
                return p.nc.sync.nop()
            p.step("sync", fn_fin, deps=[p.last_step[n] for n in ENGINES],
                   name="final_wait")
        else:
            stream_out(p, io)
        with nc.Block() as block:
            p.emit(block)
    return nc, p


_idn = np.eye(128, dtype=np.float32)


def _launch(nc, p, sh, per, stream):
    in_maps = []
    for c in range(NCORES):
        m = {}
        for n in p.input_names:
            if n == "IDN":
                m[n] = _idn
            elif n == "XS_IN":
                m[n] = stream[c]
            elif n in per[c]:
                m[n] = per[c][n]
            else:
                m[n] = sh[n]
        in_maps.append(m)
    return run_bass_kernel_spmd(nc, in_maps, core_ids=list(range(NCORES)))


def _run(inputs, groups):
    sh, per = host_prep(inputs)
    stream = None
    out = None
    for g in groups:
        nc, p = build_program(g)
        res = _launch(nc, p, sh, per, stream)
        if "XS_OUT" in res.results[0]:
            stream = [res.results[c]["XS_OUT"] for c in range(NCORES)]
        if "OUT" in res.results[0]:
            out = [res.results[c]["OUT"] for c in range(NCORES)]
    return stream, out


FUSED = True


def kernel(**inputs):
    if FUSED:
        groups = [sorted(PHASES)]
    else:
        groups = [[i] for i in sorted(PHASES)]
    stream, out = _run(inputs, groups)
    full = np.concatenate([o.T for o in out], axis=0)
    return np.ascontiguousarray(full.reshape(B, T, D))


# version 07
# speedup vs baseline: 1.1592x; 1.1592x over previous
import numpy as np
import ml_dtypes
import concourse.bass as bass
import concourse.mybir as mybir
from concourse.bass_utils import run_bass_kernel_spmd
from contextlib import ExitStack

F32 = mybir.dt.float32
BF16 = mybir.dt.bfloat16
I32 = mybir.dt.int32
AF = mybir.ActivationFunctionType
ALU = mybir.AluOpType
AX = mybir.AxisListType
BF16NP = ml_dtypes.bfloat16

NCORES = 8
B, T, D = 4, 4096, 1024
TOK = 2048
NCH = 8
TT = 512
NTT = TOK // TT
HALO = 6
TH = TOK + 2 * HALO
PAIRS = [[0, 1], [2, 3], [4, 5], [6, 7]]
TOTAL_PHASES = 8

ENGINES = ("sync", "tensor", "vector", "scalar", "gpsimd")


class Step:
    __slots__ = ("eng", "fn", "deps", "kind", "n_dma", "val", "late", "name",
                 "sem_idx", "gate", "fence")

    def __init__(self, eng, fn, deps, kind, n_dma, val, late, name):
        self.eng, self.fn, self.deps, self.kind = eng, fn, deps, kind
        self.n_dma, self.val, self.late, self.name = n_dma, val, late, name
        self.sem_idx = 0
        self.gate = None
        self.fence = None


class P:

    NDSEM = 12
    NCSEM = 4

    def __init__(self, nc, ctx):
        self.nc = nc
        self.ctx = ctx
        e = ctx.enter_context
        self.esem = {n: e(nc.semaphore(f"es_{n}")) for n in ENGINES}
        self.dsem = {n: [e(nc.semaphore(f"ds_{n}{i}")) for i in range(self.NDSEM)]
                     for n in ("sync", "gpsimd", "scalar", "vector")}
        self.csem = [e(nc.semaphore(f"cs_gp{i}")) for i in range(self.NCSEM)]
        self.ecnt = {n: 0 for n in ENGINES}
        self.dcnt = {n: [0] * self.NDSEM for n in self.dsem}
        self.dstep_i = {n: 0 for n in self.dsem}
        self.ccnt = [0] * self.NCSEM
        self.cstep_i = 0
        self.steps = {n: [] for n in ENGINES}
        self.last_step = {n: None for n in ENGINES}
        self.cur_eng = None
        self.cur_step = None
        self._n_dma_seen = 0
        self.input_names = set()
        self._input_aps = {}
        self.xsem = {n: e(nc.semaphore(f"xs_{n}")) for n in ENGINES}
        self.xcnt = {n: 0 for n in ENGINES}

    def step(self, eng, fn, deps=(), kind="compute", n_dma=0, late=False, name=""):
        deps = [d for d in deps if d is not None]
        s = Step(eng, fn, deps, kind, n_dma, 0, late, name)
        if kind == "dma":
            assert n_dma > 0, name
            idx = self.dstep_i[eng] % self.NDSEM
            self.dstep_i[eng] += 1
            prev = self.dcnt[eng][idx]
            if prev > 0:
                s.gate = prev
            self.dcnt[eng][idx] = prev + 16 * n_dma
            s.sem_idx = idx
            s.val = self.dcnt[eng][idx]
        elif kind == "collective":
            assert eng == "gpsimd"
            idx = self.cstep_i % self.NCSEM
            self.cstep_i += 1
            prev = self.ccnt[idx]
            if prev > 0:
                s.gate = prev
            self.ccnt[idx] = prev + 1
            s.sem_idx = idx
            s.val = self.ccnt[idx]
        else:
            self.ecnt[eng] += 1
            s.val = self.ecnt[eng]
        self.steps[eng].append(s)
        self.last_step[eng] = s
        return s

    def barrier(self):
        lasts = [self.last_step[n] for n in ENGINES if self.last_step[n] is not None]
        fence = {n: list(self.dcnt[n]) for n in self.dcnt}
        fence["_c"] = list(self.ccnt)
        out = []
        for n in ENGINES:
            def fn(n=n):
                return getattr(self.nc, n).nop()
            s = self.step(n, fn, deps=list(lasts), name=f"barrier_{n}")
            s.fence = fence
            out.append(s)
        return out

    def input(self, name, shape, dtype=F32):
        if name in self._input_aps:
            return self._input_aps[name]
        self.input_names.add(name)
        ap = self.nc.dram_tensor(name, list(shape), dtype, kind="ExternalInput").ap()
        self._input_aps[name] = ap
        return ap

    def chain(self, ins):
        eng = self.cur_eng
        self.xcnt[eng] += 1
        ins.then_inc(self.xsem[eng], 1)
        getattr(self.nc, eng).wait_ge(self.xsem[eng], self.xcnt[eng])
        return ins

    def dma(self, out, in_, **kw):
        eng = self.cur_eng
        ins = getattr(self.nc, eng).dma_start(out=out, in_=in_, **kw)
        ins.then_inc(self.dsem[eng][self.cur_step.sem_idx], 16)
        self._n_dma_seen += 1
        return ins

    def emit(self, block):
        waited = {n: {} for n in ENGINES}

        def emit_engine(eng):
            self.cur_eng = eng
            ng = self.nc
            for s in self.steps[eng]:
                w = waited[eng]
                self.cur_step = s
                if s.fence is not None:
                    for qn, counts in s.fence.items():
                        for i, v in enumerate(counts):
                            if v <= 0:
                                continue
                            key = ("c", "gp", i) if qn == "_c" else ("d", qn, i)
                            sem = self.csem[i] if qn == "_c" else self.dsem[qn][i]
                            if w.get(key, 0) >= v:
                                continue
                            w[key] = v
                            getattr(ng, eng).wait_ge(sem, v)
                for d in s.deps:
                    if d.kind == "dma":
                        key, sem = ("d", d.eng, d.sem_idx), self.dsem[d.eng][d.sem_idx]
                    elif d.kind == "collective":
                        key, sem = ("c", "gp", d.sem_idx), self.csem[d.sem_idx]
                    else:
                        key, sem = ("e", d.eng), self.esem[d.eng]
                    if w.get(key, 0) >= d.val:
                        continue
                    w[key] = d.val
                    getattr(ng, eng).wait_ge(sem, d.val)
                if s.gate is not None:
                    key = (("c", "gp", s.sem_idx) if s.kind == "collective"
                           else ("d", eng, s.sem_idx))
                    sem = (self.csem[s.sem_idx] if s.kind == "collective"
                           else self.dsem[eng][s.sem_idx])
                    if w.get(key, 0) < s.gate:
                        w[key] = s.gate
                        getattr(ng, eng).wait_ge(sem, s.gate)
                self._n_dma_seen = 0
                last = s.fn()
                if s.kind == "dma":
                    assert self._n_dma_seen == s.n_dma, (
                        f"step {s.name}: declared {s.n_dma} dmas, emitted {self._n_dma_seen}")
                elif s.kind == "collective":
                    assert last is not None, s.name
                    last.then_inc(self.csem[s.sem_idx], 1)
                else:
                    assert last is not None, f"step {s.name} ({s.eng}) returned no instruction"
                    last.then_inc(self.esem[eng], 1)
            self.cur_eng = None

        for eng in ENGINES:
            getattr(block, eng)(lambda _e=None, _eng=eng: emit_engine(_eng))



def host_prep(inputs):
    f = np.ascontiguousarray
    sh = {}
    x = np.asarray(inputs["L0_attn_x"])
    xt = x.reshape(B * T, D)

    sh["W_qkv"] = f(np.asarray(inputs["L0_attn_qkv_w"]).T)
    sh["B_qkv"] = f(np.asarray(inputs["L0_attn_qkv_b"]).reshape(24, 128).T)
    sh["W_l0proj"] = f(np.asarray(inputs["L0_attn_proj_w"]).T)
    sh["B_l0proj"] = f(np.asarray(inputs["L0_attn_proj_b"]).reshape(8, 128).T)

    rpb = np.asarray(inputs["L0_attn_rpb"])
    W0 = 140
    r_idx = np.arange(W0)[:, None]
    c_idx = np.arange(128)[None, :]
    bidx = 6 + r_idx - c_idx
    band = (bidx >= 6) & (bidx <= 18)
    valid_first = np.where(c_idx < 6, (r_idx >= 6) & (r_idx <= 18), band)
    valid_last = np.where(c_idx >= 122, (r_idx >= 121) & (r_idx <= 133), band)
    g = rpb[:, np.clip(bidx, 0, 24)]
    NEG = np.float32(-1e30)
    rpb_int = np.where(band[None], g, NEG).astype(np.float32)
    rpb_first = np.where(valid_first[None], g, NEG).astype(np.float32)
    rpb_last = np.where(valid_last[None], g, NEG).astype(np.float32)

    sh["W_gate_r"] = f(np.asarray(inputs["L0_ffn_gate_w"]).T)
    tri = np.triu(np.ones((128, 128), dtype=np.float32))
    sh["TRIU"] = f(tri)
    oh = np.zeros((16, 16 * 128), dtype=np.float32)
    for _e in range(16):
        oh[_e, _e * 128:(_e + 1) * 128] = 1.0
    sh["ONEHOT16"] = f(oh)
    sh["EBIAS"] = f(np.broadcast_to(np.asarray(inputs["L0_ffn_expert_bias"])[None, :], (128, 16)).copy())
    sh["W_eg"] = f(np.asarray(inputs["L0_ffn_w_gate"]).transpose(0, 2, 1))
    sh["W_eu"] = f(np.asarray(inputs["L0_ffn_w_up"]).transpose(0, 2, 1))
    sh["W_ed"] = f(np.asarray(inputs["L0_ffn_w_down"]).transpose(0, 2, 1))
    sh["W_sg"] = f(np.asarray(inputs["L0_ffn_shared_gate_w"]).T)
    sh["W_su"] = f(np.asarray(inputs["L0_ffn_shared_up_w"]).T)
    sh["W_sd"] = f(np.asarray(inputs["L0_ffn_shared_down_w"]).T)

    sh["W_q1"] = f(np.asarray(inputs["L1_attn_Wq"]).T)
    sh["W_k1"] = f(np.asarray(inputs["L1_attn_Wk"]).T)
    sh["W_v1"] = f(np.asarray(inputs["L1_attn_Wv"]).T)
    sh["W_g1"] = f(np.asarray(inputs["L1_attn_Wg1"]).T)
    sh["W_g2"] = f(np.asarray(inputs["L1_attn_Wg2"]).T)
    sh["GNW"] = f(np.asarray(inputs["L1_attn_g_norm_weight"]).reshape(8, 128).T)
    sh["W_o1"] = f(np.asarray(inputs["L1_attn_Wo"]).T)

    sh["W1G"] = f(np.asarray(inputs["L1_ffn_w_gate"]).T)
    sh["W1U"] = f(np.asarray(inputs["L1_ffn_w_up"]).T)
    sh["W1D"] = f(np.asarray(inputs["L1_ffn_w_down"]).T)

    sh["W_qkv2"] = f(np.asarray(inputs["L2_attn_W_attn"]))
    sh["B_qkv2"] = f(np.asarray(inputs["L2_attn_b_attn"]).reshape(24, 128).T)
    sh["W_l2proj"] = f(np.asarray(inputs["L2_attn_W_proj"]))
    sh["B_l2proj"] = f(np.asarray(inputs["L2_attn_b_proj"]).reshape(8, 128).T)
    sh["TEMP"] = f(np.asarray(inputs["L2_attn_temperature"]).reshape(1, 1))
    import math
    start = 2.0 ** (-2.0 ** (-(math.log2(16) - 3)))
    slopes = np.asarray([start * start ** i for i in range(16)], dtype=np.float64)
    rels = np.arange(-28, 4)
    pcol = np.arange(128)
    ab = slopes[:, None, None] * (pcol[None, None, :] + 128 * rels[None, :, None])
    sh["_alibi2"] = ab.astype(np.float32)
    jq = np.arange(512)[None, :]
    sh["_negr2"] = (-slopes[:, None] * jq).astype(np.float32)
    kk = np.arange(128)[:, None]
    m2 = np.concatenate(
        [np.where(128 * r + kk <= jq, 0.0, -1e30) for r in range(4)],
        axis=1).astype(np.float32)
    sh["MASK2"] = f(m2)

    sh["W_r2"] = f(np.asarray(inputs["L2_ffn_W_r"]).T)
    sh["W_k2"] = f(np.asarray(inputs["L2_ffn_W_k"]).T)
    sh["W_v2"] = f(np.asarray(inputs["L2_ffn_W_v"]).T)
    sh["MU_R"] = f(np.asarray(inputs["L2_ffn_mu_r"]).reshape(8, 128).T)
    sh["MU_K"] = f(np.asarray(inputs["L2_ffn_mu_k"]).reshape(8, 128).T)

    sh["W_in3"] = f(np.asarray(inputs["L3_attn_W_in"]).T)
    cw = np.asarray(inputs["L3_attn_conv_w"]).reshape(2048, 3)
    sh["CONVW"] = f(cw.reshape(16, 128, 3).transpose(1, 0, 2).reshape(128, 48))
    sh["CONVB"] = f(np.asarray(inputs["L3_attn_conv_b"]).reshape(16, 128).T)
    sh["W_dt3"] = f(np.asarray(inputs["L3_attn_W_dt"]).T)
    sh["A3"] = f(np.asarray(inputs["L3_attn_A_proj"]).T)
    sh["B3"] = f(np.asarray(inputs["L3_attn_B_proj"]).T)
    sh["C3"] = f(np.asarray(inputs["L3_attn_C_proj"]))
    sh["W_out3"] = f(np.asarray(inputs["L3_attn_W_out"]).T)

    rm = np.asarray(inputs["L3_ffn_rail_maker"])
    rup = np.asarray(inputs["L3_ffn_rail_up_proj"])
    rml = rm.transpose(0, 1, 3, 2).reshape(256, 16, 16)
    rupl = rup.transpose(0, 1, 3, 2).reshape(256, 16, 64)
    sh["RM_P"] = f(rml.transpose(1, 0, 2).reshape(16, 256 * 16))
    sh["RUP_P"] = f(rupl.transpose(1, 0, 2).reshape(16, 256 * 64))
    sh["W_cp"] = f(np.asarray(inputs["L3_ffn_w_cproj"]).T)

    per = []
    xf = xt.T
    for c in range(NCORES):
        d = {}
        lo, hi = c * TOK, (c + 1) * TOK
        seq = c // 2
        left = max(lo - HALO, seq * T)
        right = min(hi + HALO, (seq + 1) * T)
        xh = np.zeros((D, TH), dtype=np.float32)
        xh[:, HALO - (lo - left): HALO - (lo - left) + (right - left)] = xf[:, left:right]
        d["X0"] = f(xh)
        d["RPB_Q0"] = rpb_first if c % 2 == 0 else rpb_int
        d["RPB_Q15"] = rpb_int if c % 2 == 0 else rpb_last
        d["RPB_INT"] = rpb_int
        d["PFLAG"] = np.full((128, 1), float(c % 2), dtype=np.float32)
        d["PFLAGR"] = np.full((128, 1), float(1 - c % 2), dtype=np.float32)
        hb = 8 * (c % 2)
        d["ALIBI2_LOC"] = f(sh["_alibi2"][hb:hb + 8].reshape(8 * 32, 128).T)
        d["NEGR2_LOC"] = f(sh["_negr2"][hb:hb + 8].reshape(1, 4096))
        per.append(d)
    return sh, per



class IO:
    pass


CONSTS = [1e-5, 0.125, -0.5, 0.5, 1.0, -1.0, 2.0, 1.0 / 127.0, 1.0 / 1024.0,
          128 ** -0.5, -1e-5, 0.0]


def alloc_globals(p, io):
    nc = p.nc
    e = p.ctx.enter_context
    io.xa = e(nc.sbuf_tensor("xs_a", [128, NCH * TOK], BF16))
    io.xb = e(nc.sbuf_tensor("xs_b", [128, NCH * TOK], BF16))
    io.cur = io.xa
    io.ybuf = io.xb
    io.ones_col = e(nc.sbuf_tensor("ones_col", [128, 2], BF16))
    io.ones_row = e(nc.sbuf_tensor("ones_row", [1, 128], F32))
    io.ones_row_b = e(nc.sbuf_tensor("ones_row_b", [1, 128], BF16))
    io.idn = e(nc.sbuf_tensor("idn", [128, 128], BF16))
    io.cpool = e(nc.sbuf_tensor("cpool", [128, len(CONSTS)], F32))
    for i, v in enumerate(CONSTS):
        nc.const_aps.register(F32, v, io.cpool[:, i:i + 1])
    io.out_f32 = None

    def fn():
        v = nc.vector
        v.memset(io.ones_col[:], 1.0)
        v.memset(io.ones_row[:], 1.0)
        v.memset(io.ones_row_b[:], 1.0)
        for i, cv in enumerate(CONSTS):
            v.memset(io.cpool[:, i:i + 1], cv)
        return v.memset(io.idn[:], 0.0)
    s0 = p.step("vector", fn, name="const_memset")
    IDN = p.input("IDN", [128, 128], F32)

    def fn2():
        return p.dma(io.idn[:], IDN)
    io.s_const = p.step("gpsimd", fn2, deps=[s0], kind="dma", n_dma=1, name="idn_load")


def stream_in(p, io):
    XS = p.input("XS_IN", [1024, TOK], BF16)

    def fn():
        last = None
        for c in range(NCH):
            last = p.dma(io.cur[:, c * TOK:(c + 1) * TOK], XS[c * 128:(c + 1) * 128, :])
        return last
    io.s_stream = p.step("sync", fn, kind="dma", n_dma=NCH, name="stream_in")


def stream_out(p, io):
    XS = p.nc.dram_tensor("XS_OUT", [1024, TOK], BF16, kind="ExternalOutput").ap()
    dep = [p.last_step[n] for n in ENGINES]

    def fn():
        last = None
        for c in range(NCH):
            last = p.dma(XS[c * 128:(c + 1) * 128, :], io.cur[:, c * TOK:(c + 1) * TOK])
        return last
    s = p.step("sync", fn, deps=dep, kind="dma", n_dma=NCH, name="stream_out")
    p.step("sync", lambda: p.nc.sync.nop(), deps=[s], name="stream_out_wait")


def final_out(p, io):
    OUT = p.nc.dram_tensor("OUT", [1024, TOK], F32, kind="ExternalOutput").ap()
    dep = [p.last_step[n] for n in ENGINES]

    def fn():
        last = None
        for c in range(NCH):
            last = p.dma(OUT[c * 128:(c + 1) * 128, :], io.out_f32[:, c * TOK:(c + 1) * TOK])
        return last
    s = p.step("sync", fn, deps=dep, kind="dma", n_dma=NCH, name="final_out")
    p.step("sync", lambda: p.nc.sync.nop(), deps=[s], name="final_out_wait")


def emit_postln(p, io, ctx, y_deps_of_tt, final, tag):
    nc = p.nc
    e = ctx.enter_context
    y_sb = io.ybuf
    ysq = e(nc.sbuf_tensor(f"ln_ysq{tag}", [128, TT], BF16))
    ps_s = e(nc.psum_tensor(f"ln_ps_s{tag}", [1, TT], F32))
    ps_ss = e(nc.psum_tensor(f"ln_ps_ss{tag}", [1, TT], F32))
    stat = e(nc.sbuf_tensor(f"ln_stat{tag}", [1, NTT * TT], F32))
    stat2 = e(nc.sbuf_tensor(f"ln_stat2{tag}", [1, NTT * TT], F32))
    stat3 = e(nc.sbuf_tensor(f"ln_stat3{tag}", [1, NTT * TT], F32))
    ps_a = e(nc.psum_tensor(f"ln_ps_a{tag}", [128, TT], F32))
    ps_b = e(nc.psum_tensor(f"ln_ps_b{tag}", [128, TT], F32))
    a_sb = e(nc.sbuf_tensor(f"ln_a{tag}", [128, TT], BF16))
    b_sb = e(nc.sbuf_tensor(f"ln_b{tag}", [128, TT], BF16))
    tmp = e(nc.sbuf_tensor(f"ln_tmp{tag}", [128, TT], BF16))
    if final:
        io.out_f32 = e(nc.sbuf_tensor("out_f32", [128, NCH * TT], F32))
        OUT = p.nc.dram_tensor("OUT", [1024, TOK], F32, kind="ExternalOutput").ap()
    out_sb = io.out_f32 if final else io.cur

    apply_last = []
    prev_apply = None
    prev_bc = None
    for tt in range(NTT):
        dep_y = y_deps_of_tt(tt)

        def ysl(c, tt=tt):
            return y_sb[:, c * TOK + tt * TT: c * TOK + tt * TT + TT]

        def fn_sum(tt=tt, ysl=ysl):
            last = None
            for c in range(NCH):
                last = nc.tensor.matmul(ps_s[:], io.ones_col[:, 0:1],
                                        ysl(c), start=(c == 0), stop=(c == NCH - 1))
            return last
        s_sum = p.step("tensor", fn_sum, deps=dep_y + [io.s_const, prev_apply],
                       name=f"ln{tag}_sum{tt}")

        s_prev = prev_apply
        s_ssq = None
        for c in range(NCH):
            def fn_sq(c=c, ysl=ysl):
                return nc.scalar.activation(ysq[:], ysl(c), AF.Square)
            s_sq = p.step("scalar", fn_sq, deps=dep_y + [s_prev], name=f"ln{tag}_sq{tt}_{c}")

            def fn_ssq(c=c):
                return nc.tensor.matmul(ps_ss[:], io.ones_col[:, 0:1], ysq[:],
                                        start=(c == 0), stop=(c == NCH - 1))
            s_ssq = p.step("tensor", fn_ssq, deps=[s_sq, s_sum], name=f"ln{tag}_ssq{tt}_{c}")
            s_prev = s_ssq

        sl = slice(tt * TT, (tt + 1) * TT)

        def fn_mean(sl=sl):
            return nc.vector.tensor_scalar(stat[:, sl], ps_s[:], 1.0 / 1024.0, None, ALU.mult)
        s_mean = p.step("vector", fn_mean, deps=[s_sum], late=True, name=f"ln{tag}_mean{tt}")

        def fn_var(sl=sl):
            p.chain(nc.vector.tensor_mul(stat2[:, sl], stat[:, sl], stat[:, sl]))
            return nc.vector.scalar_tensor_tensor(
                stat2[:, sl], ps_ss[:], 1.0 / 1024.0, stat2[:, sl], ALU.mult, ALU.subtract)
        s_var = p.step("vector", fn_var, deps=[s_ssq, s_mean], late=True, name=f"ln{tag}_var{tt}")

        def fn_rstd(sl=sl):
            p.chain(nc.scalar.activation(stat2[:, sl], stat2[:, sl], AF.Ln, bias=1e-5))
            return nc.scalar.activation(stat2[:, sl], stat2[:, sl], AF.Exp, scale=-0.5)
        s_rstd = p.step("scalar", fn_rstd, deps=[s_var], late=True, name=f"ln{tag}_rstd{tt}")

        def fn_nmr(sl=sl):
            return nc.vector.scalar_tensor_tensor(
                stat3[:, sl], stat[:, sl], -1.0, stat2[:, sl], ALU.mult, ALU.mult)
        s_nmr = p.step("vector", fn_nmr, deps=[s_rstd, s_mean], late=True, name=f"ln{tag}_nmr{tt}")

        def fn_bc(sl=sl):
            nc.tensor.matmul(ps_a[:], io.ones_row[:], stat2[:, sl], start=True, stop=True)
            return nc.tensor.matmul(ps_b[:], io.ones_row[:], stat3[:, sl], start=True, stop=True)
        s_bc = p.step("tensor", fn_bc, deps=[s_rstd, s_nmr, prev_bc], name=f"ln{tag}_bc{tt}")

        def fn_ab():
            nc.scalar.copy(a_sb[:], ps_a[:])
            return nc.scalar.copy(b_sb[:], ps_b[:])
        s_ab = p.step("scalar", fn_ab, deps=[s_bc], name=f"ln{tag}_ab{tt}")

        s_app = None
        app_steps = []
        for c in range(NCH):
            def fn_app(c=c, tt=tt, ysl=ysl):
                if final:
                    osl = out_sb[:, c * TT:(c + 1) * TT]
                else:
                    osl = out_sb[:, c * TOK + tt * TT: c * TOK + tt * TT + TT]
                p.chain(nc.vector.tensor_mul(tmp[:], ysl(c), a_sb[:]))
                return nc.vector.tensor_add(osl, tmp[:], b_sb[:])
            s_app = p.step("vector", fn_app, deps=[s_ab, s_app], late=True, name=f"ln{tag}_app{tt}_{c}")
            app_steps.append(s_app)
        if final:
            def fn_od(tt=tt):
                last = None
                for c in range(NCH):
                    last = p.dma(OUT[c * 128:(c + 1) * 128, tt * TT:(tt + 1) * TT],
                                 out_sb[:, c * TT:(c + 1) * TT])
                return last
            s_od = p.step("sync", fn_od, deps=app_steps, kind="dma", n_dma=NCH,
                          name=f"ln{tag}_od{tt}")
            s_app = s_od
        prev_apply = s_app
        prev_bc = s_app
        apply_last.append(s_app)
    return apply_last


def load_weight(p, ctx, name, hbm_name, rows, cols, queue="gpsimd", dtype=F32,
                deps=(), col0=0, ncols=None):
    nc = p.nc
    Wap = p.input(hbm_name, [rows, cols], dtype)
    ncols = ncols if ncols is not None else cols
    rch = rows // 128
    sb = ctx.enter_context(nc.sbuf_tensor(f"w_{name}", [128, rch * ncols], BF16))

    def fn():
        last = None
        for c in range(rch):
            last = p.dma(sb[:, c * ncols:(c + 1) * ncols],
                         Wap[c * 128:(c + 1) * 128, col0:col0 + ncols])
        return last
    s = p.step(queue, fn, deps=list(deps), kind="dma", n_dma=rch, name=f"load_{name}")
    return sb, s



def phase_l0_attn(p, io):
    nc = p.nc
    ctx = ExitStack()
    e = ctx.enter_context
    NQB = TOK // 128

    X0 = p.input("X0", [1024, TH], F32)
    x0 = e(nc.sbuf_tensor("l0_x0", [128, NCH * TH], BF16))

    def fn_x0():
        last = None
        for c in range(NCH):
            last = p.dma(x0[:, c * TH:(c + 1) * TH], X0[c * 128:(c + 1) * 128, :])
        return last
    s_x0 = p.step("gpsimd", fn_x0, kind="dma", n_dma=NCH, name="l0_x0")
    x_res = lambda c: x0[:, c * TH + HALO: c * TH + HALO + TOK]

    BQ = p.input("B_qkv", [128, 24], F32)
    bq_sb = e(nc.sbuf_tensor("l0_bq", [128, 24], F32))
    BP = p.input("B_l0proj", [128, 8], F32)
    bp_sb = e(nc.sbuf_tensor("l0_bp", [128, 8], F32))

    def fn_b():
        p.dma(bq_sb[:], BQ)
        return p.dma(bp_sb[:], BP)
    s_b = p.step("sync", fn_b, kind="dma", n_dma=2, name="l0_b")

    WQKV = p.input("W_qkv", [1024, 3072], F32)
    qf = e(nc.sbuf_tensor("l0_qf", [128, NCH * TOK], BF16))
    kf = e(nc.sbuf_tensor("l0_kf", [128, NCH * TH], BF16))
    VS = nc.dram_tensor("l0_vs", [TH + 4, 1024], BF16)

    qkv_ctx = ExitStack()
    qe = qkv_ctx.enter_context
    wsec = [qe(nc.sbuf_tensor(f"l0_wsec{i}", [128, NCH * 1024], BF16)) for i in range(2)]
    vstage = qe(nc.sbuf_tensor("l0_vstage", [128, 2 * 1024], BF16))
    ps_pool = [qe(nc.psum_tensor(f"l0_ps{i}", [128, TT], F32)) for i in range(4)]
    sec_steps = []

    def mk_sec(sec, dep):
        def fn(sec=sec):
            last = None
            for c in range(NCH):
                last = p.dma(wsec[sec % 2][:, c * 1024:(c + 1) * 1024],
                             WQKV[c * 128:(c + 1) * 128, sec * 1024:(sec + 1) * 1024])
            return last
        return p.step("gpsimd", fn, deps=dep, kind="dma", n_dma=NCH, name=f"l0_wsec{sec}")

    sec_steps.append(mk_sec(0, []))
    sec_steps.append(mk_sec(1, []))

    q_steps = []
    for oc in range(NCH):
        for tt in range(NTT):
            def fn_mm(oc=oc, tt=tt):
                last = None
                for ic in range(NCH):
                    last = nc.tensor.matmul(
                        ps_pool[(oc * NTT + tt) % 2][:],
                        wsec[0][:, ic * 1024 + oc * 128: ic * 1024 + oc * 128 + 128],
                        x0[:, ic * TH + HALO + tt * TT: ic * TH + HALO + tt * TT + TT],
                        start=(ic == 0), stop=(ic == NCH - 1))
                return last
            dep = [sec_steps[0], s_x0, s_b] + ([q_steps[-2]] if len(q_steps) >= 2 else [])
            s_mm = p.step("tensor", fn_mm, deps=dep, name=f"l0_qmm_{oc}_{tt}")

            def fn_ev(oc=oc, tt=tt):
                return nc.vector.tensor_scalar(
                    qf[:, oc * TOK + tt * TT: oc * TOK + tt * TT + TT],
                    ps_pool[(oc * NTT + tt) % 2][:],
                    bq_sb[:, oc:oc + 1], 0.125, ALU.add, ALU.mult)
            q_steps.append(p.step("vector", fn_ev, deps=[s_mm], name=f"l0_qev_{oc}_{tt}"))

    sec_steps.append(mk_sec(2, [q_steps[-1]]))

    KT = [(0, 512), (512, 512), (1024, 512), (1536, 512), (2048, 12)]
    k_steps = []
    kidx = 0
    for oc in range(NCH):
        for ti, (t0, tl) in enumerate(KT):
            def fn_mm(oc=oc, t0=t0, tl=tl, kidx=kidx):
                last = None
                for ic in range(NCH):
                    last = nc.tensor.matmul(
                        ps_pool[2 + kidx % 2][:, 0:tl],
                        wsec[1][:, ic * 1024 + oc * 128: ic * 1024 + oc * 128 + 128],
                        x0[:, ic * TH + t0: ic * TH + t0 + tl],
                        start=(ic == 0), stop=(ic == NCH - 1))
                return last
            dep = [sec_steps[1], s_x0] + ([k_steps[-2]] if len(k_steps) >= 2 else [])
            s_mm = p.step("tensor", fn_mm, deps=dep, name=f"l0_kmm_{oc}_{ti}")

            def fn_ev(oc=oc, t0=t0, tl=tl, kidx=kidx):
                return nc.vector.tensor_copy(
                    kf[:, oc * TH + t0: oc * TH + t0 + tl],
                    ps_pool[2 + kidx % 2][:, 0:tl])
            k_steps.append(p.step("vector", fn_ev, deps=[s_mm], name=f"l0_kev_{oc}_{ti}"))
            kidx += 1

    v_sp = []
    NVC = 17
    for tc in range(NVC):
        t0 = tc * 128
        tl = min(128, TH - t0)
        ev_pair = []
        for nn in range(2):
            def fn_mm(t0=t0, tl=tl, nn=nn, tc=tc):
                last = None
                for ic in range(NCH):
                    last = nc.tensor.matmul(
                        ps_pool[(tc * 2 + nn) % 2][0:tl, 0:512],
                        x0[:, ic * TH + t0: ic * TH + t0 + tl],
                        wsec[0][:, ic * 1024 + nn * 512: ic * 1024 + nn * 512 + 512],
                        start=(ic == 0), stop=(ic == NCH - 1))
                return last
            dep = [sec_steps[2], s_x0]
            if tc == 0 and nn == 0:
                dep += [q_steps[-1], q_steps[-2]]
            if len(v_sp) >= 1:
                dep += [v_sp[-1]]
            s_mm = p.step("tensor", fn_mm, deps=dep, name=f"l0_vmm_{tc}_{nn}")

            def fn_ev(tl=tl, nn=nn, tc=tc):
                return nc.vector.tensor_copy(
                    vstage[0:tl, (tc % 2) * 1024 + nn * 512:(tc % 2) * 1024 + nn * 512 + 512],
                    ps_pool[(tc * 2 + nn) % 2][0:tl, 0:512])
            ev_pair.append(p.step("vector", fn_ev, deps=[s_mm], name=f"l0_vev_{tc}_{nn}"))

        def fn_sp(t0=t0, tl=tl, tc=tc):
            return p.dma(VS[t0:t0 + tl, :], vstage[0:tl, (tc % 2) * 1024:(tc % 2) * 1024 + 1024])
        v_sp.append(p.step("gpsimd", fn_sp, deps=ev_pair, kind="dma", n_dma=1, name=f"l0_vsp_{tc}"))

    qkv_done = [q_steps[-1], k_steps[-1], v_sp[-1]]
    qkv_ctx.close()

    w_pr, s_wpr = load_weight(p, ctx, "l0proj", "W_l0proj", 1024, 1024, queue="gpsimd",
                              deps=qkv_done)
    vb_bf = e(nc.sbuf_tensor("l0_vb", [128, 8], BF16))
    pb_eff = e(nc.sbuf_tensor("l0_pbeff", [128, 8], F32))

    att_ctx = ExitStack()
    ae = att_ctx.enter_context
    ps_ob = [ae(nc.psum_tensor(f"l0_ob{i}", [128, 512], F32)) for i in range(4)]
    ps_oh = [ps_ob[h // 4][:, (h % 4) * 128:(h % 4 + 1) * 128] for h in range(16)]
    ps_stb = [ae(nc.psum_tensor(f"l0_stb{i}", [128, 512], F32)) for i in range(2)]
    ps_st1 = [ps_stb[i][:, 0:128] for i in range(2)]
    ps_st2 = [ps_stb[i][0:12, 256:384] for i in range(2)]
    ps_rb = [ae(nc.psum_tensor(f"l0_rb_{i}", [128, 512], F32)) for i in range(2)]
    p1 = ae(nc.sbuf_tensor("l0_p1", [128, 2 * 128], BF16))
    vw = ae(nc.sbuf_tensor("l0_vw", [128, 1024], BF16))
    lrec = ae(nc.sbuf_tensor("l0_lrec", [1, 1024], F32))
    rb_sb = ae(nc.sbuf_tensor("l0_rb_sb", [128, 1024], BF16))
    of = io.xa

    expb1 = ae(nc.sbuf_tensor("l0_expb1", [128, 2 * 16 * 128], BF16))
    expb2_t = ae(nc.sbuf_tensor("l0_expb2", [12, 2 * 16 * 128], BF16))
    expb2 = expb2_t[:, :]
    vw2_t = ae(nc.sbuf_tensor("l0_vw2", [12, 1024], BF16))
    vw2 = vw2_t[:, :]
    p2_t = ae(nc.sbuf_tensor("l0_p2b", [12, 256], BF16))
    p2 = p2_t[:, :]
    RV = [p.input("RPB_INT", [16, 140, 128]), p.input("RPB_Q0", [16, 140, 128]),
          p.input("RPB_Q15", [16, 140, 128])]

    def mk_rpb_load(vslot, vsrc, deps):
        def fn():
            last = None
            for h in range(16):
                sl = slice(vslot * 2048 + h * 128, vslot * 2048 + h * 128 + 128)
                p.dma(expb1[:, sl], RV[vsrc][h, 0:128, :])
                last = p.dma(expb2[:, sl], RV[vsrc][h, 128:140, :])
            return last
        s_l = p.step("gpsimd", fn, deps=deps, kind="dma", n_dma=32, name=f"l0_rpb{vsrc}")

        def fn_e():
            sl = slice(vslot * 2048, (vslot + 1) * 2048)
            nc.scalar.activation(expb1[:, sl], expb1[:, sl], AF.Exp)
            return nc.scalar.activation(expb2[:, sl], expb2[:, sl], AF.Exp)
        return p.step("scalar", fn_e, deps=[s_l], name=f"l0_expb{vsrc}")

    s_expb0 = mk_rpb_load(0, 0, qkv_done)
    s_expb1 = mk_rpb_load(1, 1, qkv_done)
    s_expb = s_expb1
    s_expb_q15 = None


    prev = {}
    o_evac_by_qb = {}
    qb0_pms = []
    for qb in range(NQB):
        s0 = qb * 128
        variant = 1 if (qb == 0 or qb == NQB - 1) else 0
        if qb == NQB - 1:
            s_expb_q15 = mk_rpb_load(1, 2, qb0_pms)

        def fn_vw(s0=s0):
            p.dma(vw[:], VS[s0:s0 + 128, :])
            return p.dma(vw2[:], VS[s0 + 128:s0 + 140, :])
        dep_vw = [v_sp[min(s0 // 128 + 1, NVC - 1)], v_sp[-1]]
        if qb >= 1:
            dep_vw += o_evac_by_qb[qb - 1]
        s_vw = p.step("gpsimd", fn_vw, deps=dep_vw, kind="dma", n_dma=2, name=f"l0_vw_{qb}")

        l_deps = []
        for h in range(16):
            hc, hr = h // 2, (h % 2) * 64
            bi = h % 2

            def fn_st(s0=s0, hc=hc, hr=hr, bi=bi):
                kbase = hc * TH + s0
                qsl = qf[hr:hr + 64, hc * TOK + s0: hc * TOK + s0 + 128]
                nc.tensor.matmul(ps_st1[bi][:], kf[hr:hr + 64, kbase:kbase + 128], qsl,
                                 start=True, stop=True)
                return nc.tensor.matmul(ps_st2[bi][:], kf[hr:hr + 64, kbase + 128:kbase + 140],
                                        qsl, start=True, stop=True)
            dep = list(qkv_done)
            pk = prev.get(("exp", h - 2, qb)) or (prev.get(("exp", h + 14, qb - 1)) if qb else None)
            if pk is not None:
                dep.append(pk)
            s_st = p.step("tensor", fn_st, deps=dep, name=f"l0_st_{qb}_{h}")

            def fn_exp(bi=bi):
                nc.scalar.activation(p1[:, bi * 128:(bi + 1) * 128], ps_st1[bi][:], AF.Exp)
                return nc.scalar.activation(p2[:, bi * 128:(bi + 1) * 128], ps_st2[bi][:], AF.Exp)
            dep_e = [s_st, s_expb]
            pe = prev.get(("omm", h - 2, qb)) or (prev.get(("omm", h + 14, qb - 1)) if qb else None)
            if pe is not None:
                dep_e.append(pe)
            s_exp = p.step("scalar", fn_exp, deps=dep_e, name=f"l0_exp_{qb}_{h}")
            prev[("exp", h, qb)] = s_exp

            def fn_pm(bi=bi, h=h, variant=variant):
                esl = slice(variant * 2048 + h * 128, variant * 2048 + h * 128 + 128)
                nc.vector.tensor_mul(p1[:, bi * 128:(bi + 1) * 128],
                                     p1[:, bi * 128:(bi + 1) * 128], expb1[:, esl])
                return nc.vector.tensor_mul(p2[:, bi * 128:(bi + 1) * 128],
                                            p2[:, bi * 128:(bi + 1) * 128], expb2[:, esl])
            dep_pm = [s_exp, s_expb0, s_expb]
            if qb == NQB - 1:
                dep_pm.append(s_expb_q15)
            s_pm = p.step("vector", fn_pm, deps=dep_pm, name=f"l0_pm_{qb}_{h}")
            if qb == 0:
                qb0_pms.append(s_pm)

            def fn_o(h=h, bi=bi):
                po = ps_oh[h]
                nc.tensor.matmul(po[0:64, :], vw[:, h * 64:(h + 1) * 64],
                                 p1[:, bi * 128:(bi + 1) * 128], start=True, stop=False)
                nc.tensor.matmul(po[0:64, :], vw2[:, h * 64:(h + 1) * 64],
                                 p2[:, bi * 128:(bi + 1) * 128], start=False, stop=True)
                nc.tensor.matmul(po[64:65, :], io.ones_col[:, 0:1],
                                 p1[:, bi * 128:(bi + 1) * 128], start=True, stop=False)
                return nc.tensor.matmul(po[64:65, :], io.ones_col[0:12, 1:2],
                                        p2[:, bi * 128:(bi + 1) * 128], start=False, stop=True)
            dep_o = [s_pm, s_vw]
            if qb >= 1:
                dep_o.append(prev[("oevbank", h // 4, qb - 1)])
            s_o = p.step("tensor", fn_o, deps=dep_o, name=f"l0_omm_{qb}_{h}")
            prev[("omm", h, qb)] = s_o
            l_deps.append(s_o)

        evs = []
        for i in range(4):
            gb = qb * 4 + i
            sl = (gb % 2) * 512

            def fn_lr(i=i, sl=sl):
                last = None
                for j in range(4):
                    h = i * 4 + j
                    last = nc.vector.reciprocal(lrec[:, sl + j * 128: sl + (j + 1) * 128],
                                                ps_oh[h][64:65, :])
                return last
            dep_lr = [prev[("omm", i * 4 + j, qb)] for j in range(4)]
            dep_lr.append(prev.get(("rbmm", gb - 2)))
            s_lr = p.step("vector", fn_lr, deps=dep_lr, late=True, name=f"l0_lrec_{qb}_{i}")

            def fn_r(sl=sl, gb=gb):
                return nc.tensor.matmul(ps_rb[gb % 2][:], io.ones_row[:],
                                        lrec[:, sl:sl + 512], start=True, stop=True)
            s_r = p.step("tensor", fn_r, deps=[s_lr, prev.get(("rbc", gb - 2))],
                         name=f"l0_rb_{qb}_{i}")
            prev[("rbmm", gb)] = s_r

            def fn_c(sl=sl, gb=gb):
                return nc.scalar.copy(rb_sb[:, sl:sl + 512], ps_rb[gb % 2][:])
            s_c = p.step("scalar", fn_c, deps=[s_r, prev.get(("oevb", gb - 2))],
                         name=f"l0_rbc_{qb}_{i}")
            prev[("rbc", gb)] = s_c

            for j in range(4):
                h = i * 4 + j

                def fn_oe(h=h, qb=qb, sl=sl, j=j):
                    r0 = (h % 2) * 64
                    return nc.vector.tensor_mul(
                        of[r0:r0 + 64, (h // 2) * TOK + qb * 128:(h // 2) * TOK + qb * 128 + 128],
                        ps_oh[h][0:64, :],
                        rb_sb[r0:r0 + 64, sl + j * 128: sl + (j + 1) * 128])
                s_oe = p.step("vector", fn_oe, deps=[s_c, prev[("omm", h, qb)]],
                              name=f"l0_oev_{qb}_{h}")
                prev[("oev", h, qb)] = s_oe
                evs.append(s_oe)
            prev[("oevb", gb)] = evs[-1]
            prev[("oevbank", i, qb)] = evs[-1]
        o_evac_by_qb[qb] = evs

    att_done = [o_evac_by_qb[NQB - 1][h] for h in range(16)]
    att_ctx.close()

    pr_ctx = ExitStack()
    pe_ = pr_ctx.enter_context
    ps_pr = [pe_(nc.psum_tensor(f"l0_pr{i}", [128, TT], F32)) for i in range(2)]
    ps_pb = pe_(nc.psum_tensor("l0_pspb", [128, 8], F32))

    def fn_vb():
        return nc.vector.tensor_copy(vb_bf[:], bq_sb[:, 16:24])
    s_vb = p.step("vector", fn_vb, deps=[s_b], name="l0_vb")

    def fn_pbe():
        last = None
        for oc in range(NCH):
            for ic in range(NCH):
                last = nc.tensor.matmul(
                    ps_pb[:, oc:oc + 1],
                    w_pr[:, (ic * 8 + oc) * 128:(ic * 8 + oc) * 128 + 128],
                    vb_bf[:, ic:ic + 1], start=(ic == 0), stop=(ic == NCH - 1))
        return last
    s_pbe = p.step("tensor", fn_pbe, deps=[s_vb, s_wpr, att_done[-1]], name="l0_pbe")

    def fn_pbe2():
        return nc.vector.tensor_add(pb_eff[:], ps_pb[:], bp_sb[:])
    s_pbe2 = p.step("vector", fn_pbe2, deps=[s_pbe], late=True, name="l0_pbe2")

    y_sb = io.ybuf
    ev_by_tt = {tt: [] for tt in range(NTT)}
    busy = {}
    k = 0
    for tt in range(NTT):
        for oc in range(NCH):
            ps = ps_pr[k % 2]

            def fn_mm(oc=oc, tt=tt, ps=ps):
                last = None
                for ic in range(NCH):
                    last = nc.tensor.matmul(
                        ps[:],
                        w_pr[:, (ic * 8 + oc) * 128:(ic * 8 + oc) * 128 + 128],
                        of[:, ic * TOK + tt * TT: ic * TOK + tt * TT + TT],
                        start=(ic == 0), stop=(ic == NCH - 1))
                return last
            dep = [s_wpr, s_pbe] + att_done + ([busy[k % 2]] if k >= 2 else [])
            s_mm = p.step("tensor", fn_mm, deps=dep, name=f"l0_pr_{tt}_{oc}")

            def fn_ev(oc=oc, tt=tt, ps=ps):
                sl = slice(oc * TOK + tt * TT, oc * TOK + tt * TT + TT)
                return nc.vector.scalar_tensor_tensor(
                    y_sb[:, sl], ps[:], pb_eff[:, oc:oc + 1],
                    x_res(oc)[:, tt * TT:tt * TT + TT], ALU.add, ALU.add)
            s_ev = p.step("vector", fn_ev, deps=[s_mm, s_pbe2], late=True,
                          name=f"l0_prev_{tt}_{oc}")
            busy[k % 2] = s_ev
            ev_by_tt[tt].append(s_ev)
            k += 1

    def ydeps(tt):
        d = list(ev_by_tt[tt])
        if tt + 1 < NTT:
            pass
        return d
    emit_postln(p, io, pr_ctx, ydeps, final=(io.cur_phase == TOTAL_PHASES - 1), tag="p0")
    pr_ctx.close()
    ctx.close()




def phase_l0_ffn(p, io):
    nc = p.nc
    ctx = ExitStack()
    e = ctx.enter_context
    x = io.cur

    def xsl(ic, t0, tl=512):
        return x[:, ic * TOK + t0: ic * TOK + t0 + tl]

    WG = p.input("W_gate_r", [1024, 16], F32)
    EB = p.input("EBIAS", [128, 16], F32)
    comb_fm = e(nc.sbuf_tensor("r_comb", [16, TOK], BF16))
    oh_sb = e(nc.sbuf_tensor("r_oh", [16, 16 * 128], BF16))
    rs_ctx = ExitStack()
    re_ = rs_ctx.enter_context
    wg_sb = re_(nc.sbuf_tensor("r_wg", [128, NCH * 16], BF16))
    eb_sb = re_(nc.sbuf_tensor("r_eb", [128, 16], F32))
    sct = re_(nc.sbuf_tensor("r_sct", [128, 256], F32))
    selv = re_(nc.sbuf_tensor("r_sel", [128, 256], F32))
    selo = re_(nc.sbuf_tensor("r_selo", [128, 256], F32))
    eqb = re_(nc.sbuf_tensor("r_eq", [128, 256], F32))
    smb = re_(nc.sbuf_tensor("r_sm", [128, 256], F32))
    rcp = re_(nc.sbuf_tensor("r_rcp", [128, 16], F32))
    comb_tm = re_(nc.sbuf_tensor("r_combtm", [128, 256], BF16))
    BIG = 65536.0

    OH = p.input("ONEHOT16", [16, 16 * 128], F32)
    rt_ctx = ExitStack()
    ps_r = rt_ctx.enter_context(nc.psum_tensor("r_ps", [128, 256], F32))
    ps_t = [rt_ctx.enter_context(nc.psum_tensor(f"r_pst{i}", [16, 128], BF16))
            for i in range(2)]

    def fn_rwl():
        p.dma(eb_sb[:], EB)
        p.dma(oh_sb[:], OH)
        last = None
        for c in range(NCH):
            last = p.dma(wg_sb[:, c * 16:(c + 1) * 16], WG[c * 128:(c + 1) * 128, :])
        return last
    s_rwl = p.step("gpsimd", fn_rwl, kind="dma", n_dma=10, name="rt_wl")

    def fn_gmm():
        last = None
        for tc in range(16):
            for ic in range(NCH):
                last = nc.tensor.matmul(
                    ps_r[:, tc * 16:(tc + 1) * 16],
                    x[:, ic * TOK + tc * 128: ic * TOK + tc * 128 + 128],
                    wg_sb[:, ic * 16:(ic + 1) * 16],
                    start=(ic == 0), stop=(ic == NCH - 1))
        return last
    s_gmm = p.step("tensor", fn_gmm, deps=[s_rwl], name="rt_gmm")

    def fn_sig():
        return nc.scalar.activation(sct[:], ps_r[:], AF.Sigmoid)
    s_sig = p.step("scalar", fn_sig, deps=[s_gmm], name="rt_sig")

    g3 = lambda ap: ap.rearrange("p (t e) -> p t e", e=16)

    def fn_top8():
        ebb = eb_sb[:].unsqueeze(1).broadcast_to([128, 16, 16])
        p.chain(nc.vector.tensor_tensor(g3(selv[:]), g3(sct[:]), ebb, ALU.add))
        p.chain(nc.vector.tensor_copy(selo[:], selv[:]))
        for it in range(8):
            p.chain(nc.vector.tensor_reduce(rcp[:], g3(selv[:]), AX.X, ALU.max))
            p.chain(nc.vector.tensor_tensor(
                g3(eqb[:]), g3(selv[:]),
                rcp[:].unsqueeze(2).broadcast_to([128, 16, 16]), ALU.is_equal))
            p.chain(nc.vector.scalar_tensor_tensor(
                selv[:], eqb[:], -BIG, selv[:], ALU.mult, ALU.add))
        p.chain(nc.vector.tensor_sub(eqb[:], selo[:], selv[:]))
        p.chain(nc.vector.tensor_scalar(eqb[:], eqb[:], BIG / 2, None, ALU.is_gt))
        p.chain(nc.vector.tensor_mul(smb[:], sct[:], eqb[:]))
        p.chain(nc.vector.tensor_reduce(rcp[:], g3(smb[:]), AX.X, ALU.add))
        p.chain(nc.vector.reciprocal(rcp[:], rcp[:]))
        p.chain(nc.vector.scalar_tensor_tensor(
            g3(smb[:]), g3(smb[:]), 2.826,
            rcp[:].unsqueeze(2).broadcast_to([128, 16, 16]), ALU.mult, ALU.mult))
        return nc.vector.tensor_copy(comb_tm[:], smb[:])
    s_top8 = p.step("vector", fn_top8, deps=[s_sig, io.s_const], late=True, name="rt_top8")

    tr_steps = []
    for tc in range(16):
        def fn_tr(tc=tc):
            return nc.tensor.transpose(ps_t[tc % 2][:],
                                       comb_tm[:, tc * 16:(tc + 1) * 16], io.idn[:])
        dep = [s_top8, io.s_const] + ([tr_steps[-2][1]] if len(tr_steps) >= 2 else [])
        s_tr = p.step("tensor", fn_tr, deps=[d for d in dep if d is not None], name=f"rt_tr{tc}")

        def fn_te(tc=tc):
            return nc.vector.tensor_copy(comb_fm[0:16, tc * 128:(tc + 1) * 128],
                                         ps_t[tc % 2][:])
        s_te = p.step("vector", fn_te, deps=[s_tr], name=f"rt_te{tc}")
        tr_steps.append((s_tr, s_te))
    s_comb = tr_steps[-1][1]
    rt_ctx.close()
    rs_ctx.close()

    W_EG = p.input("W_eg", [16, 1024, 1024], F32)
    W_EU = p.input("W_eu", [16, 1024, 1024], F32)
    W_ED = p.input("W_ed", [16, 1024, 1024], F32)
    W_SG = p.input("W_sg", [1024, 1024], F32)
    W_SU = p.input("W_su", [1024, 1024], F32)
    W_SD = p.input("W_sd", [1024, 1024], F32)

    y_acc = e(nc.sbuf_tensor("ff_y", [128, NCH * 1024], F32))
    h_sb = e(nc.sbuf_tensor("ff_h", [128, NCH * 512], BF16))
    hh = e(nc.sbuf_tensor("ff_hh", [128, 2 * 512], BF16))
    cb_sb = e(nc.sbuf_tensor("ff_cb", [128, 2 * 512], BF16))

    wctx = ExitStack()
    we = wctx.enter_context
    wgu = [we(nc.sbuf_tensor(f"ff_wgu{i}", [128, 16 * 1024], BF16)) for i in range(2)]
    wd = [we(nc.sbuf_tensor("ff_wd0", [128, 8 * 1024], BF16))]
    ps_g = [we(nc.psum_tensor(f"ff_psg{i}", [128, 512], F32)) for i in range(2)]
    ps_u = [we(nc.psum_tensor(f"ff_psu{i}", [128, 512], F32)) for i in range(2)]
    ps_d = [we(nc.psum_tensor(f"ff_psd{i}", [128, 512], F32)) for i in range(2)]
    ps_cb = we(nc.psum_tensor("ff_pscb", [128, 512], F32))

    gu_reader = [None, None]
    d_reader = [None, None]
    cb_reader = {}
    prev_silu = {}
    prev_h1 = {}
    prev_hc = {}
    prev_y = {}
    prev_cbc = None
    prev_cbm = None
    yform = {}
    wl = {}

    order = [(TB, j) for TB in range(2) for j in range(17)]
    for oi_, (TB, j) in enumerate(order):
        ex = j - 1
        buf = oi_ % 2
        bufd = 0

        def fn_wl(ex=ex, buf=buf):
            last = None
            srcs = ((W_SG, W_SU) if ex < 0 else (W_EG[ex], W_EU[ex]))
            for s, SRC in enumerate(srcs):
                for c in range(NCH):
                    last = p.dma(wgu[buf][:, s * 8192 + c * 1024: s * 8192 + (c + 1) * 1024],
                                 SRC[c * 128:(c + 1) * 128, :])
            return last
        wl[oi_] = p.step("gpsimd", fn_wl, deps=[gu_reader[buf]], kind="dma",
                         n_dma=16, name=f"ff_wl{oi_}")

        def fn_wld(ex=ex, bufd=bufd):
            last = None
            SRC = W_SD if ex < 0 else W_ED[ex]
            for c in range(NCH):
                last = p.dma(wd[bufd][:, c * 1024:(c + 1) * 1024],
                             SRC[c * 128:(c + 1) * 128, :])
            return last
        wld = p.step("gpsimd", fn_wld, deps=[d_reader[bufd]], kind="dma",
                     n_dma=8, name=f"ff_wld{oi_}")

        last_d_step = None
        for st in range(2):
            t0 = TB * 1024 + st * 512
            sl5 = slice(st * 512, st * 512 + 512)
            if ex >= 0:
                def fn_cbm(ex=ex, t0=t0):
                    return nc.tensor.matmul(ps_cb[:], oh_sb[0:16, ex * 128:(ex + 1) * 128],
                                            comb_fm[0:16, t0:t0 + 512],
                                            start=True, stop=True)
                s_cbm = p.step("tensor", fn_cbm, deps=[s_comb, prev_cbc, s_rwl],
                               name=f"ff_cbm{oi_}_{st}")
                prev_cbm = s_cbm

                def fn_cbc(st=st, sl5=sl5):
                    return nc.scalar.copy(cb_sb[:, sl5], ps_cb[:])
                s_cbc = p.step("scalar", fn_cbc,
                               deps=[s_cbm, cb_reader.get(st)], name=f"ff_cbc{oi_}_{st}")
                prev_cbc = s_cbc
            else:
                s_cbc = None

            h_steps = []
            for oi in range(8):
                k = oi % 2

                def fn_g(oi=oi, t0=t0, buf=buf, k=k):
                    last = None
                    for ic in range(NCH):
                        last = nc.tensor.matmul(
                            ps_g[k][:],
                            wgu[buf][:, ic * 1024 + oi * 128: ic * 1024 + oi * 128 + 128],
                            xsl(ic, t0), start=(ic == 0), stop=(ic == NCH - 1))
                    return last
                dep_g = [wl[oi_], prev_silu.get(k)]
                if oi_ == 0:
                    dep_g.append(s_comb)
                s_g = p.step("tensor", fn_g, deps=dep_g, name=f"ff_g{oi_}_{st}_{oi}")

                def fn_u(oi=oi, t0=t0, buf=buf, k=k):
                    last = None
                    for ic in range(NCH):
                        last = nc.tensor.matmul(
                            ps_u[k][:],
                            wgu[buf][:, 8192 + ic * 1024 + oi * 128: 8192 + ic * 1024 + oi * 128 + 128],
                            xsl(ic, t0), start=(ic == 0), stop=(ic == NCH - 1))
                    return last
                dep_u = [wl[oi_], prev_h1.get(k)]
                if oi_ == 0:
                    dep_u.append(s_comb)
                s_u = p.step("tensor", fn_u, deps=dep_u, name=f"ff_u{oi_}_{st}_{oi}")
                gu_reader[buf] = s_u

                def fn_silu(k=k):
                    return nc.scalar.activation(hh[:, k * 512:(k + 1) * 512], ps_g[k][:], AF.Silu)
                s_silu = p.step("scalar", fn_silu, deps=[s_g, prev_h1.get(k)],
                                name=f"ff_si{oi_}_{st}_{oi}")
                prev_silu[k] = s_silu

                def fn_h1(oi=oi, k=k):
                    return nc.vector.tensor_mul(h_sb[:, oi * 512:(oi + 1) * 512],
                                                hh[:, k * 512:(k + 1) * 512], ps_u[k][:])
                dep_h1 = [s_silu, s_u, last_d_step, prev_hc.get(oi)]
                s_h1 = p.step("vector", fn_h1, deps=dep_h1, name=f"ff_h1{oi_}_{st}_{oi}")
                prev_h1[k] = s_h1

                if ex >= 0:
                    def fn_hc(oi=oi, sl5=sl5):
                        return nc.gpsimd.tensor_mul(h_sb[:, oi * 512:(oi + 1) * 512],
                                                    h_sb[:, oi * 512:(oi + 1) * 512],
                                                    cb_sb[:, sl5])
                    s_hc = p.step("gpsimd", fn_hc, deps=[s_h1, s_cbc], name=f"ff_hc{oi_}_{st}_{oi}")
                else:
                    s_hc = s_h1
                prev_hc[oi] = s_hc
                h_steps.append(s_hc)
            cb_reader[st] = h_steps[-1]

            for od in range(8):
                k = od % 2

                def fn_d(od=od, bufd=bufd, k=k):
                    last = None
                    for ic in range(NCH):
                        last = nc.tensor.matmul(
                            ps_d[k][:],
                            wd[bufd][:, ic * 1024 + od * 128: ic * 1024 + od * 128 + 128],
                            h_sb[:, ic * 512:(ic + 1) * 512],
                            start=(ic == 0), stop=(ic == NCH - 1))
                    return last
                s_d = p.step("tensor", fn_d,
                             deps=h_steps + [prev_y.get(k), wld], name=f"ff_d{oi_}_{st}_{od}")
                last_d_step = s_d
                d_reader[bufd] = s_d

                def fn_y(od=od, st=st, TB=TB, k=k, ex=ex):
                    ysl = y_acc[:, od * 1024 + st * 512: od * 1024 + st * 512 + 512]
                    if ex < 0:
                        return nc.vector.tensor_copy(ysl, ps_d[k][:])
                    return nc.vector.tensor_add(ysl, ysl, ps_d[k][:])
                dep_y = [s_d, prev_y.get(("ysl", od, st))]
                if ex < 0 and TB == 1:
                    dep_y += yform.get(0, []) + yform.get(1, [])
                s_y = p.step("vector", fn_y, deps=dep_y, late=True, name=f"ff_y{oi_}_{st}_{od}")
                prev_y[k] = s_y
                prev_y[("ysl", od, st)] = s_y

        if j == 16:
            for st in range(2):
                tt = TB * 2 + st
                steps = []
                for c in range(NCH):
                    def fn_yf(c=c, st=st, TB=TB, tt=tt):
                        ysl = y_acc[:, c * 1024 + st * 512: c * 1024 + st * 512 + 512]
                        osl = io.ybuf[:, c * TOK + tt * TT: c * TOK + tt * TT + TT]
                        return nc.vector.tensor_add(
                            osl, ysl, x[:, c * TOK + tt * TT: c * TOK + tt * TT + TT])
                    steps.append(p.step("vector", fn_yf,
                                        deps=[prev_y[("ysl", c, st)]], late=True,
                                        name=f"ff_yf{TB}_{st}_{c}"))
                yform[tt] = steps

    wctx.close()

    ln_ctx = ExitStack()
    emit_postln(p, io, ln_ctx, lambda tt: yform[tt],
                final=(io.cur_phase == TOTAL_PHASES - 1), tag="p1")
    ln_ctx.close()
    ctx.close()





def phase_l1_attn(p, io):
    nc = p.nc
    ctx = ExitStack()
    e = ctx.enter_context
    x = io.cur
    NC16 = 16
    SC = 128 ** -0.5

    def xsl(ic, t0, tl=512):
        return x[:, ic * TOK + t0: ic * TOK + t0 + tl]

    FLAG = p.input("PFLAG", [128, 1], F32)
    TRI = p.input("TRIU", [128, 128], F32)
    GNW = p.input("GNW", [128, 8], F32)
    flag_sb = e(nc.sbuf_tensor("l1_flag", [128, 1], F32))
    tri_sb = e(nc.sbuf_tensor("l1_tri", [128, 128], BF16))
    gnw_sb = e(nc.sbuf_tensor("l1_gnw", [128, 8], F32))

    def fn_cst():
        p.dma(flag_sb[:], FLAG)
        p.dma(gnw_sb[:], GNW)
        return p.dma(tri_sb[:], TRI)
    s_cst = p.step("gpsimd", fn_cst, kind="dma", n_dma=3, name="l1_cst")
    ones_big = e(nc.sbuf_tensor("l1_ones", [128, TOK], BF16))

    def fn_zos():
        p.chain(nc.vector.memset(ones_big[:], 1.0))
        return nc.vector.memset(send_sb[:], 0.0)
    s_szero = p.step("vector", fn_zos, name="l1_szero")

    v_tm = e(nc.sbuf_tensor("l1_vtm", [128, NC16 * 1024], BF16))
    g1_sb = e(nc.sbuf_tensor("l1_g1", [128, TOK], BF16))
    N_sb = e(nc.sbuf_tensor("l1_nsb", [128, 128], BF16))
    nacc = e(nc.sbuf_tensor("l1_nacc", [128, 128], F32))
    recv_sb = e(nc.sbuf_tensor("l1_recv", [128, 132], F32))
    dh_col = e(nc.sbuf_tensor("l1_dhc", [128, 1], F32))
    send_sb = e(nc.sbuf_tensor("l1_send", [128, 132], F32))
    ek_sb = e(nc.sbuf_tensor("l1_ek", [128, TOK], BF16))
    ekT_sb = e(nc.sbuf_tensor("l1_ekt", [128, TOK], BF16))
    qs_sb = e(nc.sbuf_tensor("l1_qs", [128, TOK], F32))
    d_sb = e(nc.sbuf_tensor("l1_d", [128, TOK], F32))
    qp_sb = e(nc.sbuf_tensor("l1_qp", [128, TOK], BF16))
    p_sb = e(nc.sbuf_tensor("l1_pp", [128, 2 * 128], BF16))

    w_q, s_wq = load_weight(p, ctx, "l1q", "W_q1", 1024, 1024)
    w_k, s_wk = load_weight(p, ctx, "l1k", "W_k1", 1024, 1024)
    w_g1, s_wg1 = load_weight(p, ctx, "l1g1", "W_g1", 1024, 128)
    w_g2, s_wg2 = load_weight(p, ctx, "l1g2", "W_g2", 128, 1024)

    cc_in = [nc.dram_tensor(f"l1cc_in{h}", [128, 132], F32) for h in range(8)]
    cc_out = [nc.dram_tensor(f"l1cc_out{h}", [256, 132], F32) for h in range(8)]

    va_ctx = ExitStack()
    w_v, s_wv = load_weight(p, va_ctx, "l1v", "W_v1", 1024, 1024)
    ps_a = [va_ctx.enter_context(nc.psum_tensor(f"l1_psa{i}", [128, 512], F32))
            for i in range(2)]
    v_steps = []
    vi = 0
    for tc in range(NC16):
        for nn in range(2):
            def fn_mm(tc=tc, nn=nn, vi=vi):
                last = None
                for ic in range(NCH):
                    last = nc.tensor.matmul(
                        ps_a[vi % 2][:],
                        x[:, ic * TOK + tc * 128: ic * TOK + tc * 128 + 128],
                        w_v[:, ic * 1024 + nn * 512: ic * 1024 + nn * 512 + 512],
                        start=(ic == 0), stop=(ic == NCH - 1))
                return last
            dep = [s_wv] + ([v_steps[-2]] if len(v_steps) >= 2 else [])
            s_mm = p.step("tensor", fn_mm, deps=dep, name=f"l1_vmm_{tc}_{nn}")

            def fn_ev(tc=tc, nn=nn, vi=vi):
                return nc.vector.tensor_copy(
                    v_tm[:, tc * 1024 + nn * 512: tc * 1024 + nn * 512 + 512],
                    ps_a[vi % 2][:])
            v_steps.append(p.step("vector", fn_ev, deps=[s_mm], name=f"l1_vev_{tc}_{nn}"))
            vi += 1

    g1_steps = []
    for tt in range(NTT):
        def fn_mm(tt=tt, vi=vi):
            last = None
            for ic in range(NCH):
                last = nc.tensor.matmul(
                    ps_a[vi % 2][:],
                    w_g1[:, ic * 128: (ic + 1) * 128],
                    xsl(ic, tt * TT), start=(ic == 0), stop=(ic == NCH - 1))
            return last
        dep = [s_wg1] + [g1_steps[-2] if len(g1_steps) >= 2 else
                         (v_steps[-1] if len(g1_steps) == 1 else v_steps[-2])]
        s_mm = p.step("tensor", fn_mm, deps=dep, name=f"l1_g1m_{tt}")

        def fn_ev(tt=tt, vi=vi):
            return nc.vector.tensor_copy(g1_sb[:, tt * TT:(tt + 1) * TT], ps_a[vi % 2][:])
        g1_steps.append(p.step("vector", fn_ev, deps=[s_mm], name=f"l1_g1e_{tt}"))
        vi += 1
    s_g1 = g1_steps[-1]
    s_vdone = v_steps[-1]
    va_ctx.close()

    at_ctx = ExitStack()
    ae = at_ctx.enter_context
    ps_pk = [ae(nc.psum_tensor(f"l1_pspk{i}", [128, 512], F32)) for i in range(2)]
    ps_st = [ae(nc.psum_tensor(f"l1_psst{i}", [128, 128], F32)) for i in range(2)]
    ps_o = ae(nc.psum_tensor("l1_pso", [128, 128], F32))
    ps_nt = [ae(nc.psum_tensor(f"l1_psnt{i}", [128, 128], BF16)) for i in range(2)]
    ps_nn = ae(nc.psum_tensor("l1_ps_nn", [128, 256], F32))
    ps_ntot = ps_nn[:, 0:128]
    ps_np = ps_nn[:, 128:256]

    of = io.ybuf
    prev = {}
    for h in range(8):
        ek_steps = []
        for tt in range(NTT):
            def fn_km(h=h, tt=tt):
                last = None
                for ic in range(NCH):
                    last = nc.tensor.matmul(
                        ps_pk[tt % 2][:],
                        w_k[:, ic * 1024 + h * 128: ic * 1024 + h * 128 + 128],
                        xsl(ic, tt * TT), start=(ic == 0), stop=(ic == NCH - 1))
                return last
            dep = [s_wk, s_g1 if h == 0 else None]
            if tt >= 2:
                dep.append(ek_steps[-2])
            elif h > 0:
                dep.append(prev[("qe", h - 1, NTT - 1)])
            s_km = p.step("tensor", fn_km, deps=dep, name=f"l1_km_{h}_{tt}")

            def fn_ke(h=h, tt=tt):
                return nc.scalar.activation(ek_sb[:, tt * TT:(tt + 1) * TT],
                                            ps_pk[tt % 2][:], AF.Exp)
            dep_ke = [s_km]
            if h > 0:
                dep_ke += [prev[("st", h - 1, NC16 - 1)], prev[("nt", h - 1, NC16 - 1)]]
            ek_steps.append(p.step("scalar", fn_ke, deps=dep_ke, name=f"l1_ke_{h}_{tt}"))

        q_steps = []
        for tt in range(NTT):
            def fn_qm(h=h, tt=tt):
                last = None
                for ic in range(NCH):
                    last = nc.tensor.matmul(
                        ps_pk[tt % 2][:],
                        w_q[:, ic * 1024 + h * 128: ic * 1024 + h * 128 + 128],
                        xsl(ic, tt * TT), start=(ic == 0), stop=(ic == NCH - 1))
                return last
            dep = [s_wq, ek_steps[-2] if tt == 0 else None,
                   ek_steps[-1] if tt == 1 else None]
            if tt >= 2:
                dep.append(q_steps[-2])
            s_qm = p.step("tensor", fn_qm, deps=dep, name=f"l1_qm_{h}_{tt}")

            def fn_qe(h=h, tt=tt):
                return nc.scalar.activation(qs_sb[:, tt * TT:(tt + 1) * TT],
                                            ps_pk[tt % 2][:], AF.Silu)
            dep_qe = [s_qm]
            if h > 0:
                dep_qe.append(prev[("qdiv", h - 1)])
            s_qe = p.step("scalar", fn_qe, deps=dep_qe, name=f"l1_qe_{h}_{tt}")
            prev[("qe", h, tt)] = s_qe
            q_steps.append(s_qe)

        def fn_dscan():
            return nc.vector.tensor_tensor_scan(
                d_sb[:], ones_big[:], ek_sb[:], 0.0, ALU.mult, ALU.add)
        dep_ds = ek_steps + [io.s_const, s_szero, prev.get(("qdiv", h - 1))]
        s_dscan = p.step("vector", fn_dscan, deps=dep_ds, late=True, name=f"l1_ds_{h}")

        nt_steps = []
        for i in range(NC16):
            def fn_nt(i=i):
                return nc.tensor.transpose(ps_nt[i % 2][:],
                                           ek_sb[:, i * 128:(i + 1) * 128], io.idn[:])
            dep = [ek_steps[-1], io.s_const]
            if i >= 2:
                dep.append(nt_steps[-2][1])
            elif h > 0:
                dep.append(prev[("nu", h - 1, NC16 - 1)])
            s_nt = p.step("tensor", fn_nt, deps=dep, name=f"l1_nt_{h}_{i}")

            def fn_nte(i=i):
                return nc.vector.tensor_copy(ekT_sb[:, i * 128:(i + 1) * 128],
                                             ps_nt[i % 2][:])
            s_nte = p.step("vector", fn_nte, deps=[s_nt], name=f"l1_nte_{h}_{i}")
            nt_steps.append((s_nt, s_nte))
            prev[("nt", h, i)] = s_nt

        def fn_ntot(h=h):
            last = None
            for i in range(NC16):
                last = nc.tensor.matmul(
                    ps_ntot, ekT_sb[:, i * 128:(i + 1) * 128],
                    v_tm[:, i * 1024 + h * 128: i * 1024 + (h + 1) * 128],
                    start=(i == 0), stop=(i == NC16 - 1))
            return last
        dep_ntot = [s[1] for s in nt_steps] + [s_vdone, prev.get(("stage", h - 1)),
                    prev.get(("ne", h - 1, NC16 - 1))]
        s_ntot = p.step("tensor", fn_ntot, deps=dep_ntot, name=f"l1_ntot_{h}")

        def fn_stage():
            p.chain(nc.vector.tensor_copy(send_sb[:, 0:128], ps_ntot))
            return nc.vector.tensor_copy(send_sb[:, 128:129], d_sb[:, TOK - 1:TOK])
        s_stage = p.step("vector", fn_stage, deps=[s_ntot, s_dscan, s_szero,
                                                   prev.get(("snd", h - 1))],
                         late=True, name=f"l1_stage_{h}")
        prev[("stage", h)] = s_stage

        def fn_snd(h=h):
            return p.dma(cc_in[h].ap(), send_sb[:])
        s_snd = p.step("gpsimd", fn_snd, deps=[s_stage], kind="dma", n_dma=1,
                       name=f"l1_snd_{h}")
        prev[("snd", h)] = s_snd

        def fn_cc(h=h):
            return nc.gpsimd.collective_compute(
                "AllGather", ALU.bypass, replica_groups=PAIRS,
                ins=[cc_in[h].ap().opt()], outs=[cc_out[h].ap().opt()])
        s_cc = p.step("gpsimd", fn_cc, deps=[s_snd], kind="collective", name=f"l1_cc_{h}")

        def fn_rcv(h=h):
            return p.dma(recv_sb[:], cc_out[h][0:128, :])
        s_rcv = p.step("gpsimd", fn_rcv, deps=[s_cc, prev.get(("ne", h - 1, NC16 - 1)),
                                               prev.get(("qdiv", h - 1))],
                       kind="dma", n_dma=1, name=f"l1_rcv_{h}")

        def fn_qdiv():
            p.chain(nc.vector.tensor_scalar(dh_col[:], recv_sb[:, 128:129],
                                            flag_sb[:, 0:1], None, ALU.mult))
            p.chain(nc.vector.tensor_scalar(d_sb[:], d_sb[:], dh_col[:, 0:1],
                                            None, ALU.add))
            p.chain(nc.vector.reciprocal(d_sb[:], d_sb[:]))
            return nc.vector.tensor_mul(qp_sb[:], qs_sb[:], d_sb[:])
        dep_qd = [s_rcv, s_dscan, s_cst] + q_steps
        if h > 0:
            dep_qd += [prev[("st", h - 1, NC16 - 1)], prev[("oi", h - 1, NC16 - 1)]]
        s_qdiv = p.step("vector", fn_qdiv, deps=dep_qd, late=True, name=f"l1_qdiv_{h}")
        prev[("qdiv", h)] = s_qdiv

        def fn_ninit():
            return nc.vector.tensor_scalar(N_sb[:], recv_sb[:, 0:128],
                                           flag_sb[:, 0:1], None, ALU.mult)
        dep_ni = [s_rcv, s_cst, prev.get(("oi", h - 1, NC16 - 1))]
        s_ninit = p.step("vector", fn_ninit, deps=dep_ni, late=True, name=f"l1_ninit_{h}")
        prev[("ne", h, -1)] = s_ninit

        for i in range(NC16):
            def fn_st(h=h, i=i):
                return nc.tensor.matmul(
                    ps_st[i % 2][:], ek_sb[:, i * 128:(i + 1) * 128],
                    qp_sb[:, i * 128:(i + 1) * 128], start=True, stop=True)
            dep = [s_qdiv, prev.get(("pm", h, i - 2))]
            s_st = p.step("tensor", fn_st, deps=dep, name=f"l1_st_{h}_{i}")
            prev[("st", h, i)] = s_st

            def fn_pm(i=i):
                return nc.vector.tensor_mul(p_sb[:, (i % 2) * 128:(i % 2 + 1) * 128],
                                            ps_st[i % 2][:], tri_sb[:])
            dep = [s_st, s_cst, prev.get(("oi", h, i - 2))]
            s_pm = p.step("vector", fn_pm, deps=dep, name=f"l1_pm_{h}_{i}")
            prev[("pm", h, i)] = s_pm

            def fn_oi(h=h, i=i):
                nc.tensor.matmul(ps_o[:], N_sb[:],
                                 qp_sb[:, i * 128:(i + 1) * 128], start=True, stop=False)
                return nc.tensor.matmul(
                    ps_o[:], v_tm[:, i * 1024 + h * 128: i * 1024 + (h + 1) * 128],
                    p_sb[:, (i % 2) * 128:(i % 2 + 1) * 128], start=False, stop=True)
            dep = [s_pm, prev[("ne", h, i - 1)], prev.get(("oe", h, i - 1))]
            if i == 0 and h > 0:
                dep.append(prev[("oe", h - 1, NC16 - 1)])
            s_oi = p.step("tensor", fn_oi, deps=dep, name=f"l1_oi_{h}_{i}")
            prev[("oi", h, i)] = s_oi

            def fn_nu(h=h, i=i):
                return nc.tensor.matmul(
                    ps_np, ekT_sb[:, i * 128:(i + 1) * 128],
                    v_tm[:, i * 1024 + h * 128: i * 1024 + (h + 1) * 128],
                    start=True, stop=True)
            dep = [nt_steps[i][1], prev.get(("ne", h, i - 1)), s_oi if i == 0 else None]
            if i == 0 and h > 0:
                dep.append(prev[("ne", h - 1, NC16 - 1)])
            s_nu = p.step("tensor", fn_nu, deps=dep, name=f"l1_nu_{h}_{i}")
            prev[("nu", h, i)] = s_nu

            def fn_ne(i=i):
                if i == 0:
                    p.chain(nc.vector.tensor_copy(nacc[:], ps_np))
                else:
                    p.chain(nc.vector.tensor_add(nacc[:], nacc[:], ps_np))
                return nc.vector.scalar_tensor_tensor(
                    N_sb[:], recv_sb[:, 0:128], flag_sb[:, 0:1], nacc[:],
                    ALU.mult, ALU.add)
            dep = [s_nu, s_oi, s_rcv, s_cst]
            s_ne = p.step("vector", fn_ne, deps=dep, late=True, name=f"l1_ne_{h}_{i}")
            prev[("ne", h, i)] = s_ne

            def fn_oe(h=h, i=i):
                return nc.vector.tensor_scalar(
                    of[:, h * TOK + i * 128: h * TOK + (i + 1) * 128],
                    ps_o[:], SC, None, ALU.mult)
            s_oe = p.step("vector", fn_oe, deps=[s_oi], name=f"l1_oe_{h}_{i}")
            prev[("oe", h, i)] = s_oe

    heads_done = [prev[("oe", h, NC16 - 1)] for h in range(8)] + \
                 [prev[("ne", 7, NC16 - 1)], prev[("qdiv", 7)]]
    at_ctx.close()

    rm_ctx = ExitStack()
    re_ = rm_ctx.enter_context
    ps_ss = re_(nc.psum_tensor("l1_ps_ss", [1, 512], F32))
    ps_rb = [re_(nc.psum_tensor(f"l1_ps_rb{i}", [128, 512], F32)) for i in range(2)]
    ps_g = [re_(nc.psum_tensor(f"l1_ps_g{i}", [128, 512], F32)) for i in range(2)]
    ssq = re_(nc.sbuf_tensor("l1_ssq", [128, 512], BF16))
    mrow = re_(nc.sbuf_tensor("l1_mrow", [1, 512], F32))
    rb_sb = re_(nc.sbuf_tensor("l1_rb", [128, 512], BF16))
    sg_sb = re_(nc.sbuf_tensor("l1_sg", [128, 2 * 512], BF16))
    t1_sb = re_(nc.sbuf_tensor("l1_t1", [128, 2 * 512], BF16))
    gg_sb = re_(nc.sbuf_tensor("l1_gg", [128, 2 * 512], BF16))

    prev_rb = None
    prev_oms = {}
    for tt in range(NTT):
        s_ssm = None
        s_sq_prev = heads_done[-1] if tt == 0 else prev_oms[(tt - 1, 7)]
        for oc in range(NCH):
            def fn_sq(oc=oc, tt=tt):
                return nc.scalar.activation(ssq[:], of[:, oc * TOK + tt * TT: oc * TOK + tt * TT + TT],
                                            AF.Square)
            s_sq = p.step("scalar", fn_sq, deps=heads_done + [s_sq_prev], name=f"l1_sq_{tt}_{oc}")

            def fn_ssm(oc=oc):
                return nc.tensor.matmul(ps_ss[:], io.ones_col[:, 0:1], ssq[:],
                                        start=(oc == 0), stop=(oc == NCH - 1))
            dep = [s_sq, s_ssm, prev.get(("ms", tt - 1))]
            s_ssm = p.step("tensor", fn_ssm, deps=dep, name=f"l1_ssm_{tt}_{oc}")
            s_sq_prev = s_ssm

        def fn_ms():
            return nc.vector.tensor_scalar(mrow[:], ps_ss[:], 1.0 / 1024.0, None, ALU.mult)
        s_ms = p.step("vector", fn_ms, deps=[s_ssm], late=True, name=f"l1_ms_{tt}")
        prev[("ms", tt)] = s_ms

        def fn_rstd():
            p.chain(nc.scalar.activation(mrow[:], mrow[:], AF.Ln, bias=1e-5))
            return nc.scalar.activation(mrow[:], mrow[:], AF.Exp, scale=-0.5)
        s_rstd = p.step("scalar", fn_rstd, deps=[s_ms], late=True, name=f"l1_rstd_{tt}")

        def fn_rbm(tt=tt):
            return nc.tensor.matmul(ps_rb[tt % 2][:], io.ones_row[:], mrow[:],
                                    start=True, stop=True)
        s_rbm = p.step("tensor", fn_rbm, deps=[s_rstd, prev.get(("rbc", tt - 2))],
                       name=f"l1_rbm_{tt}")

        def fn_rbc(tt=tt):
            return nc.scalar.copy(rb_sb[:], ps_rb[tt % 2][:])
        s_rbc = p.step("scalar", fn_rbc, deps=[s_rbm, prev_oms.get((tt - 1, 7))],
                       name=f"l1_rbc_{tt}")
        prev[("rbc", tt)] = s_rbc

        for oc in range(NCH):
            k2 = oc % 2

            def fn_gm(oc=oc, tt=tt, k2=k2):
                return nc.tensor.matmul(
                    ps_g[k2][:], w_g2[:, oc * 128:(oc + 1) * 128],
                    g1_sb[:, tt * TT:(tt + 1) * TT], start=True, stop=True)
            dep = [s_g1, s_wg2, prev_oms.get((tt, oc - 2)), s_rbm]
            s_gm = p.step("tensor", fn_gm, deps=dep, name=f"l1_gm_{tt}_{oc}")

            def fn_sg(k2=k2):
                return nc.scalar.activation(sg_sb[:, k2 * 512:(k2 + 1) * 512],
                                            ps_g[k2][:], AF.Sigmoid)
            s_sg = p.step("scalar", fn_sg, deps=[s_gm, prev_oms.get((tt, oc - 2))],
                          name=f"l1_sg_{tt}_{oc}")

            def fn_t1(oc=oc, tt=tt, k2=k2):
                return nc.vector.scalar_tensor_tensor(
                    t1_sb[:, k2 * 512:(k2 + 1) * 512],
                    of[:, oc * TOK + tt * TT: oc * TOK + tt * TT + TT],
                    gnw_sb[:, oc:oc + 1], rb_sb[:], ALU.mult, ALU.mult)
            s_t1 = p.step("vector", fn_t1, deps=[s_rbc, s_cst,
                                                 prev_oms.get((tt, oc - 2))],
                          name=f"l1_t1_{tt}_{oc}")

            def fn_gg(k2=k2):
                return nc.vector.tensor_mul(gg_sb[:, k2 * 512:(k2 + 1) * 512],
                                            ps_g[k2][:], sg_sb[:, k2 * 512:(k2 + 1) * 512])
            s_gg = p.step("vector", fn_gg, deps=[s_sg, prev_oms.get((tt, oc - 2))],
                          name=f"l1_gg_{tt}_{oc}")

            def fn_om(oc=oc, tt=tt, k2=k2):
                return nc.vector.tensor_mul(
                    of[:, oc * TOK + tt * TT: oc * TOK + tt * TT + TT],
                    t1_sb[:, k2 * 512:(k2 + 1) * 512], gg_sb[:, k2 * 512:(k2 + 1) * 512])
            s_om = p.step("vector", fn_om, deps=[s_t1, s_gg], late=True,
                          name=f"l1_om_{tt}_{oc}")
            prev_oms[(tt, oc)] = s_om

    om_all = [prev_oms[(tt, oc)] for tt in range(NTT) for oc in range(NCH)]
    rm_ctx.close()

    pr_ctx = ExitStack()
    w_o, s_wo = load_weight(p, pr_ctx, "l1o", "W_o1", 1024, 1024, deps=[om_all[-1]])
    ps_pr = [pr_ctx.enter_context(nc.psum_tensor(f"l1_pr{i}", [128, 512], F32))
             for i in range(8)]
    yform = {}
    prev_ev = {}
    for tt in range(NTT):
        mm_steps = []
        for oc in range(NCH):
            def fn_pm(oc=oc, tt=tt):
                last = None
                for ic in range(NCH):
                    last = nc.tensor.matmul(
                        ps_pr[oc][:],
                        w_o[:, ic * NCH * 128 + oc * 128: ic * NCH * 128 + oc * 128 + 128],
                        of[:, ic * TOK + tt * TT: ic * TOK + tt * TT + TT],
                        start=(ic == 0), stop=(ic == NCH - 1))
                return last
            dep = om_all + [s_wo, prev_ev.get((tt - 1, oc))]
            mm_steps.append(p.step("tensor", fn_pm, deps=dep, name=f"l1_prm_{tt}_{oc}"))
        evs = []
        for oc in range(NCH):
            def fn_pe(oc=oc, tt=tt):
                sl = slice(oc * TOK + tt * TT, oc * TOK + tt * TT + TT)
                return nc.vector.tensor_add(io.ybuf[:, sl], ps_pr[oc][:], x[:, sl])
            s_pe = p.step("vector", fn_pe, deps=mm_steps, late=True,
                          name=f"l1_pre_{tt}_{oc}")
            prev_ev[(tt, oc)] = s_pe
            evs.append(s_pe)
        yform[tt] = evs
    pr_ctx.close()

    ln_ctx = ExitStack()
    emit_postln(p, io, ln_ctx, lambda tt: yform[tt] + yform[NTT - 1],
                final=(io.cur_phase == TOTAL_PHASES - 1), tag="p2")
    ln_ctx.close()
    ctx.close()




def phase_l1_ffn(p, io):
    nc = p.nc
    ctx = ExitStack()
    e = ctx.enter_context
    x = io.cur
    Q = 1.0 / 127.0

    W1G = p.input("W1G", [1024, 3072], I32)
    W1U = p.input("W1U", [1024, 3072], I32)
    W1D = p.input("W1D", [3072, 1024], I32)

    y_acc = e(nc.sbuf_tensor("f1_y", [128, NCH * 1024], F32))
    h_sec = e(nc.sbuf_tensor("f1_h", [128, 6 * 512], BF16))
    hh = e(nc.sbuf_tensor("f1_hh", [128, 2 * 512], BF16))
    wbuf = [e(nc.sbuf_tensor(f"f1_w{i}", [128, 18 * 1024], BF16)) for i in range(2)]

    f1ps_ctx = ExitStack()
    fe = f1ps_ctx.enter_context
    ps_g = [fe(nc.psum_tensor(f"f1_psg{i}", [128, 512], F32)) for i in range(2)]
    ps_u = [fe(nc.psum_tensor(f"f1_psu{i}", [128, 512], F32)) for i in range(2)]
    ps_d = [fe(nc.psum_tensor(f"f1_psd{i}", [128, 512], F32)) for i in range(2)]

    def xsl(ic, t0):
        return x[:, ic * TOK + t0: ic * TOK + t0 + 512]

    prev_silu = {}
    prev_h1 = {}
    prev_y = {}
    wl_reader = [None, None]
    yform = {}

    si = 0
    for TB in range(2):
        for sec in range(4):
            buf = si % 2

            def fn_wl(sec=sec, buf=buf):
                last = None
                for c in range(NCH):
                    last = p.dma(wbuf[buf][:, c * 768: c * 768 + 768],
                                 W1G[c * 128:(c + 1) * 128, sec * 768:(sec + 1) * 768])
                for c in range(NCH):
                    last = p.dma(wbuf[buf][:, 6144 + c * 768: 6144 + c * 768 + 768],
                                 W1U[c * 128:(c + 1) * 128, sec * 768:(sec + 1) * 768])
                for c in range(6):
                    last = p.dma(wbuf[buf][:, 12288 + c * 1024: 12288 + (c + 1) * 1024],
                                 W1D[(sec * 6 + c) * 128:(sec * 6 + c + 1) * 128, :])
                return last
            s_wl = p.step("gpsimd", fn_wl, deps=[wl_reader[buf]], kind="dma",
                          n_dma=22, name=f"f1_wl{si}")

            for st in range(2):
                t0 = TB * 1024 + st * 512
                h_steps = []
                for c6 in range(6):
                    k = c6 % 2

                    def fn_g(c6=c6, t0=t0, buf=buf, k=k):
                        last = None
                        for ic in range(NCH):
                            last = nc.tensor.matmul(
                                ps_g[k][:],
                                wbuf[buf][:, ic * 768 + c6 * 128: ic * 768 + c6 * 128 + 128],
                                xsl(ic, t0), start=(ic == 0), stop=(ic == NCH - 1))
                        return last
                    s_g = p.step("tensor", fn_g, deps=[s_wl, prev_silu.get(k)],
                                 name=f"f1_g{si}_{st}_{c6}")

                    def fn_u(c6=c6, t0=t0, buf=buf, k=k):
                        last = None
                        for ic in range(NCH):
                            last = nc.tensor.matmul(
                                ps_u[k][:],
                                wbuf[buf][:, 6144 + ic * 768 + c6 * 128: 6144 + ic * 768 + c6 * 128 + 128],
                                xsl(ic, t0), start=(ic == 0), stop=(ic == NCH - 1))
                        return last
                    s_u = p.step("tensor", fn_u, deps=[s_wl, prev_h1.get(k)],
                                 name=f"f1_u{si}_{st}_{c6}")

                    def fn_silu(k=k):
                        return nc.scalar.activation(hh[:, k * 512:(k + 1) * 512],
                                                    ps_g[k][:], AF.Silu, scale=Q)
                    s_silu = p.step("scalar", fn_silu, deps=[s_g, prev_h1.get(k)],
                                    name=f"f1_si{si}_{st}_{c6}")
                    prev_silu[k] = s_silu

                    def fn_h1(c6=c6, k=k):
                        return nc.vector.scalar_tensor_tensor(
                            h_sec[:, c6 * 512:(c6 + 1) * 512], ps_u[k][:], Q,
                            hh[:, k * 512:(k + 1) * 512], ALU.mult, ALU.mult)
                    dep_h1 = [s_silu, s_u, prev_y.get("last_d")]
                    s_h1 = p.step("vector", fn_h1, deps=dep_h1, name=f"f1_h1{si}_{st}_{c6}")
                    prev_h1[k] = s_h1
                    h_steps.append(s_h1)

                for od in range(NCH):
                    k = od % 2

                    def fn_d(od=od, buf=buf, k=k):
                        last = None
                        for c6 in range(6):
                            last = nc.tensor.matmul(
                                ps_d[k][:],
                                wbuf[buf][:, 12288 + c6 * 1024 + od * 128: 12288 + c6 * 1024 + od * 128 + 128],
                                h_sec[:, c6 * 512:(c6 + 1) * 512],
                                start=(c6 == 0), stop=(c6 == 5))
                        return last
                    s_d = p.step("tensor", fn_d, deps=h_steps + [prev_y.get(k)],
                                 name=f"f1_d{si}_{st}_{od}")
                    wl_reader[buf] = s_d
                    prev_y["last_d"] = s_d

                    def fn_y(od=od, st=st, TB=TB, k=k, sec=sec):
                        ysl = y_acc[:, od * 1024 + st * 512: od * 1024 + st * 512 + 512]
                        if sec == 0:
                            return nc.vector.tensor_scalar(ysl, ps_d[k][:], Q, None, ALU.mult)
                        return nc.vector.scalar_tensor_tensor(ysl, ps_d[k][:], Q, ysl,
                                                              ALU.mult, ALU.add)
                    dep_y = [s_d, prev_y.get(("ysl", od, st))]
                    if sec == 0 and TB == 1:
                        dep_y += yform.get(0, []) + yform.get(1, [])
                    s_y = p.step("vector", fn_y, deps=dep_y, late=True,
                                 name=f"f1_y{si}_{st}_{od}")
                    prev_y[k] = s_y
                    prev_y[("ysl", od, st)] = s_y

            si += 1
        for st in range(2):
            tt = TB * 2 + st
            steps = []
            for c in range(NCH):
                def fn_yf(c=c, st=st, tt=tt):
                    ysl = y_acc[:, c * 1024 + st * 512: c * 1024 + st * 512 + 512]
                    osl = io.ybuf[:, c * TOK + tt * TT: c * TOK + tt * TT + TT]
                    return nc.vector.tensor_add(
                        osl, ysl, x[:, c * TOK + tt * TT: c * TOK + tt * TT + TT])
                steps.append(p.step("vector", fn_yf, deps=[prev_y[("ysl", c, st)]],
                                    late=True, name=f"f1_yf{TB}_{st}_{c}"))
            yform[tt] = steps

    f1ps_ctx.close()
    ln_ctx = ExitStack()
    emit_postln(p, io, ln_ctx, lambda tt: yform[tt] + yform[3],
                final=(io.cur_phase == TOTAL_PHASES - 1), tag="p3")
    ln_ctx.close()
    ctx.close()



def phase_l2_attn(p, io):
    nc = p.nc
    ctx = ExitStack()
    e = ctx.enter_context
    x = io.cur
    NQT = 8
    NKC = 32

    FLAG = p.input("PFLAG", [128, 1], F32)
    ALB = p.input("ALIBI2_LOC", [128, 256], F32)
    MK2 = p.input("MASK2", [128, 2048], F32)
    NEGR = p.input("NEGR2_LOC", [1, 4096], F32)
    BQ2 = p.input("B_qkv2", [128, 24], F32)
    BP2 = p.input("B_l2proj", [128, 8], F32)
    TEMP = p.input("TEMP", [1, 1], F32)

    flag_sb = e(nc.sbuf_tensor("l2_flag", [128, 1], F32))
    alb_sb = e(nc.sbuf_tensor("l2_alb", [128, 256], F32))
    mask2 = e(nc.sbuf_tensor("l2_mk", [128, 2048], BF16))
    bq_sb = e(nc.sbuf_tensor("l2_bq", [128, 24], F32))
    bp_sb = e(nc.sbuf_tensor("l2_bp", [128, 8], F32))
    temp_sb = e(nc.sbuf_tensor("l2_temp", [1, 1], F32))
    scol = e(nc.sbuf_tensor("l2_scol", [128, 2], F32))

    def fn_cst():
        p.dma(flag_sb[:], FLAG)
        p.dma(alb_sb[:], ALB)
        p.dma(mask2[:], MK2)
        p.dma(bq_sb[:], BQ2)
        p.dma(bp_sb[:], BP2)
        return p.dma(temp_sb[:], TEMP)
    s_cst = p.step("gpsimd", fn_cst, kind="dma", n_dma=6, name="l2_cst")

    cc1k_in = nc.dram_tensor("l2cc1ki", [1024, TOK], BF16)
    cc1k_out = nc.dram_tensor("l2cc1ko", [2048, TOK], BF16)
    cc1q_in = nc.dram_tensor("l2cc1qi", [1024, TOK], BF16)
    cc1q_out = nc.dram_tensor("l2cc1qo", [2048, TOK], BF16)
    cc2_in = nc.dram_tensor("l2cc2i", [TOK, 1024], BF16)
    cc2_out = nc.dram_tensor("l2cc2o", [2 * TOK, 1024], BF16)
    cc3_in = nc.dram_tensor("l2cc3i", [512, T], BF16)
    cc3_out = nc.dram_tensor("l2cc3o", [1024, T], BF16)

    sc_ctx = ExitStack()
    ps_sc = sc_ctx.enter_context(nc.psum_tensor("l2_pssc", [128, 2], F32))

    def fn_scm():
        return nc.tensor.matmul(ps_sc[:, 0:1], io.ones_row[:], temp_sb[:],
                                start=True, stop=True)
    s_scm = p.step("tensor", fn_scm, deps=[s_cst, io.s_const], name="l2_scm")

    def fn_scv():
        p.chain(nc.vector.reciprocal(scol[:, 0:1], ps_sc[:, 0:1]))
        return nc.vector.tensor_scalar(scol[:, 0:1], scol[:, 0:1], 0.125, None, ALU.mult)
    s_scv = p.step("vector", fn_scv, deps=[s_scm], late=True, name="l2_scv")

    k65 = e(nc.sbuf_tensor("l2_k65", [65, 8 * T], BF16))
    v_full = e(nc.sbuf_tensor("l2_vf", [128, NKC * 520], BF16))
    btmp = e(nc.sbuf_tensor("l2_btmp", [64, 2 * 2048], BF16))
    btmp2 = e(nc.sbuf_tensor("l2_btmp2", [128, 1024], BF16))

    def fn_kone():
        p.chain(nc.vector.memset(k65[64:65, :], 1.0))
        return nc.vector.memset(
            v_full[:].rearrange("p (t h c) -> p t h c", h=8, c=65)[:, :, :, 64:65], 1.0)
    s_kone = p.step("vector", fn_kone, name="l2_kone")

    WQ2 = p.input("W_qkv2", [1024, 3072], F32)
    qk_ctx = ExitStack()
    qe = qk_ctx.enter_context
    wsec = [qe(nc.sbuf_tensor(f"l2_ws{i}", [128, NCH * 512], BF16)) for i in range(2)]
    kstage = qe(nc.sbuf_tensor("l2_kst", [128, 2 * 512], BF16))
    ps_qk = [qe(nc.psum_tensor(f"l2_psqk{i}", [128, 512], F32)) for i in range(2)]

    prevq = {}
    state = {"qki": 0}
    wread = [None, None]
    k_staged, v_staged, q_staged = [], [], []

    def emit_half(hs, kind):
        buf = hs % 2

        def fn_sec(hs=hs, buf=buf):
            last = None
            for c in range(NCH):
                last = p.dma(wsec[buf][:, c * 512:(c + 1) * 512],
                             WQ2[c * 128:(c + 1) * 128, hs * 512:(hs + 1) * 512])
            return last
        s_sec = p.step("gpsimd", fn_sec, deps=[wread[buf]], kind="dma", n_dma=NCH,
                       name=f"l2_wsec{hs}")
        lastmm = None
        if kind in ("k", "q"):
            dst = cc1k_in if kind == "k" else cc1q_in
            stlist = k_staged if kind == "k" else q_staged
            for oc4 in range(4):
                oc = ((hs - 2) if kind == "k" else hs) * 4 + oc4
                for tt in range(NTT):
                    qki = state["qki"]

                    def fn_mm(oc4=oc4, tt=tt, buf=buf, qki=qki):
                        last = None
                        for ic in range(NCH):
                            last = nc.tensor.matmul(
                                ps_qk[qki % 2][:],
                                wsec[buf][:, ic * 512 + oc4 * 128:
                                          ic * 512 + oc4 * 128 + 128],
                                x[:, ic * TOK + tt * TT: ic * TOK + tt * TT + TT],
                                start=(ic == 0), stop=(ic == NCH - 1))
                        return last
                    s_mm = p.step("tensor", fn_mm,
                                  deps=[s_sec, prevq.get(("ev", qki - 2))],
                                  name=f"l2_{kind}mm_{oc}_{tt}")
                    lastmm = s_mm

                    if kind == "k":
                        def fn_ev(qki=qki):
                            return nc.vector.tensor_copy(
                                kstage[:, (qki % 2) * 512:(qki % 2) * 512 + 512],
                                ps_qk[qki % 2][:])
                        s_ev = p.step("vector", fn_ev,
                                      deps=[s_mm, prevq.get(("st", qki - 2))],
                                      name=f"l2_kev_{oc}_{tt}")
                    else:
                        def fn_ev(oc=oc, qki=qki):
                            return nc.vector.tensor_scalar(
                                kstage[:, (qki % 2) * 512:(qki % 2) * 512 + 512],
                                ps_qk[qki % 2][:], bq_sb[:, oc:oc + 1], scol[:, 0:1],
                                ALU.add, ALU.mult)
                        s_ev = p.step("vector", fn_ev,
                                      deps=[s_mm, prevq.get(("st", qki - 2)),
                                            s_cst, s_scv],
                                      name=f"l2_qev_{oc}_{tt}")
                    prevq[("ev", qki)] = s_ev

                    def fn_st(oc=oc, tt=tt, qki=qki, dst=dst):
                        return p.dma(dst[oc * 128:(oc + 1) * 128, tt * TT:(tt + 1) * TT],
                                     kstage[:, (qki % 2) * 512:(qki % 2) * 512 + 512])
                    s_st = p.step("gpsimd", fn_st, deps=[s_ev], kind="dma", n_dma=1,
                                  name=f"l2_{kind}stg_{oc}_{tt}")
                    prevq[("st", qki)] = s_st
                    stlist.append(s_st)
                    state["qki"] = qki + 1
        else:
            nn = hs - 4
            for tc in range(16):
                qki = state["qki"]

                def fn_mm(tc=tc, buf=buf, qki=qki):
                    last = None
                    for ic in range(NCH):
                        last = nc.tensor.matmul(
                            ps_qk[qki % 2][:],
                            x[:, ic * TOK + tc * 128: ic * TOK + tc * 128 + 128],
                            wsec[buf][:, ic * 512:(ic + 1) * 512],
                            start=(ic == 0), stop=(ic == NCH - 1))
                    return last
                s_mm = p.step("tensor", fn_mm,
                              deps=[s_sec, prevq.get(("ev", qki - 2))],
                              name=f"l2_vmm_{tc}_{nn}")
                lastmm = s_mm

                def fn_ev(qki=qki):
                    return nc.vector.tensor_copy(
                        kstage[:, (qki % 2) * 512:(qki % 2) * 512 + 512],
                        ps_qk[qki % 2][:])
                s_ev = p.step("vector", fn_ev,
                              deps=[s_mm, prevq.get(("st", qki - 2))],
                              name=f"l2_vev_{tc}_{nn}")
                prevq[("ev", qki)] = s_ev

                def fn_st(tc=tc, nn=nn, qki=qki):
                    return p.dma(cc2_in[tc * 128:(tc + 1) * 128, nn * 512:(nn + 1) * 512],
                                 kstage[:, (qki % 2) * 512:(qki % 2) * 512 + 512])
                s_st = p.step("gpsimd", fn_st, deps=[s_ev], kind="dma", n_dma=1,
                              name=f"l2_vstg_{tc}_{nn}")
                prevq[("st", qki)] = s_st
                v_staged.append(s_st)
                state["qki"] = qki + 1
        wread[buf] = lastmm

    emit_half(2, "k")
    emit_half(3, "k")

    def fn_cc1k():
        return nc.gpsimd.collective_compute(
            "AllGather", ALU.bypass, replica_groups=PAIRS,
            ins=[cc1k_in.ap().opt()], outs=[cc1k_out.ap().opt()])
    s_cc1k = p.step("gpsimd", fn_cc1k, deps=k_staged, kind="collective", name="l2_cc1k")

    emit_half(4, "v")
    emit_half(5, "v")

    def fn_cc2():
        return nc.gpsimd.collective_compute(
            "AllGather", ALU.bypass, replica_groups=PAIRS,
            ins=[cc2_in.ap().opt()], outs=[cc2_out.ap().opt()])
    s_cc2 = p.step("gpsimd", fn_cc2, deps=v_staged, kind="collective", name="l2_cc2")

    emit_half(0, "q")
    emit_half(1, "q")

    def fn_cc1q():
        return nc.gpsimd.collective_compute(
            "AllGather", ALU.bypass, replica_groups=PAIRS,
            ins=[cc1q_in.ap().opt()], outs=[cc1q_out.ap().opt()])
    s_cc1q = p.step("gpsimd", fn_cc1q, deps=q_staged, kind="collective", name="l2_cc1q")
    qk_ctx.close()
    sc_ctx.close()

    at_ctx = ExitStack()
    ae = at_ctx.enter_context
    q65 = ae(nc.sbuf_tensor("l2_q65", [65, 2 * 4096], BF16))
    p_sb = ae(nc.sbuf_tensor("l2_p", [128, 4 * 512], BF16))
    sm_sb = ae(nc.sbuf_tensor("l2_sm", [128, 2 * 512], F32))
    ostage = ae(nc.sbuf_tensor("l2_ost", [128, 2048], BF16))
    rb_sb = ae(nc.sbuf_tensor("l2_rb", [128, 512], BF16))
    lrec = ae(nc.sbuf_tensor("l2_lrec", [1, 512], F32))
    ps_st = [ae(nc.psum_tensor(f"l2_pst{i}", [128, 512], F32)) for i in range(4)]
    ps_ob = [ae(nc.psum_tensor(f"l2_ob{i}", [128, 512], F32)) for i in range(2)]
    ps_rb = [ae(nc.psum_tensor(f"l2_prb{i}", [128, 512], F32)) for i in range(2)]

    def fn_negr():
        p.dma(q65[64:65, 0:4096], NEGR)
        return p.dma(q65[64:65, 4096:8192], NEGR)
    s_negr = p.step("gpsimd", fn_negr, deps=[s_cc1q], kind="dma", n_dma=2,
                    name="l2_negr")

    bl_last = {}

    def blend(dst, tmp, fnA, fnB, n_dma, name, deps, key):
        prevs = bl_last.get(key)
        depsA = deps + [s_cst]
        sA = p.step("gpsimd", fnA, deps=depsA, kind="dma", n_dma=n_dma, name=name + "_a")
        sB = p.step("gpsimd", fnB, deps=depsA + [prevs], kind="dma", n_dma=n_dma,
                    name=name + "_b")

        def fn_bl(dst=dst, tmp=tmp):
            npart = dst.shape[0]
            p.chain(nc.vector.tensor_sub(tmp, tmp, dst))
            return nc.vector.scalar_tensor_tensor(dst, tmp, flag_sb[0:npart, 0:1], dst,
                                                  ALU.mult, ALU.add)
        s = p.step("vector", fn_bl, deps=[sA, sB], late=True, name=name + "_bl")
        bl_last[key] = s
        return s

    kf_blends = []
    kqi = 0
    for lh in range(8):
        kb = []
        for th in range(2):
            ks = kqi % 2

            def fnA(lh=lh, th=th):
                return p.dma(k65[0:64, lh * T + th * TOK: lh * T + (th + 1) * TOK],
                             cc1k_out[1024 * th + lh * 64: 1024 * th + lh * 64 + 64, :])

            def fnB(lh=lh, th=th, ks=ks):
                return p.dma(btmp[0:64, ks * 2048: ks * 2048 + TOK],
                             cc1k_out[1024 * th + 512 + lh * 64:
                                      1024 * th + 512 + lh * 64 + 64, :])
            kb.append(blend(k65[0:64, lh * T + th * TOK: lh * T + (th + 1) * TOK],
                            btmp[0:64, ks * 2048: ks * 2048 + TOK], fnA, fnB, 1,
                            f"l2_krb{lh}_{th}", [s_cc1k], f"kq{ks}"))
            kqi += 1
        kf_blends.append(kb[1])

    def vdst(tc):
        return v_full[:, tc * 520:(tc + 1) * 520].rearrange(
            "p (h c) -> p h c", h=8)[:, :, 0:64]

    vf_blends = []
    for tc in range(NKC):
        def fnA(tc=tc):
            return p.dma(vdst(tc), cc2_out[tc * 128:(tc + 1) * 128, 0:512]
                         .rearrange("p (h c) -> p h c", h=8))

        vs = tc % 2

        def fnB(tc=tc, vs=vs):
            return p.dma(btmp2[:, vs * 512:(vs + 1) * 512],
                         cc2_out[tc * 128:(tc + 1) * 128, 512:1024])
        vf_blends.append(blend(
            vdst(tc), btmp2[:, vs * 512:(vs + 1) * 512].rearrange("p (h c) -> p h c", h=8),
            fnA, fnB, 1, f"l2_vrb{tc}", [s_cc2], f"v{vs}"))

    slot_reader = [None] * 4
    slot_om = [None] * 4
    sm_free = [None, None]
    ob_free = [None, None]
    rb_free = [None, None]
    last_rbm = [None]
    prev = {}
    gi = 0
    for qt in range(NQT):
        qb2 = qt % 2
        rowbase = 0 if qt < 4 else 1024
        qcol = (qt % 4) * 512

        dep_q = [s_cc1q, s_negr] + ([prev[("sclast", qt - 2)]] if qt >= 2 else [])
        for hh in range(2):
            def fnA(qb2=qb2, rowbase=rowbase, qcol=qcol, hh=hh):
                return p.dma(
                    q65[0:64, qb2 * 4096 + hh * 2048: qb2 * 4096 + hh * 2048 + 2048]
                    .rearrange("d (h j) -> d h j", h=4),
                    cc1q_out[rowbase + hh * 256: rowbase + hh * 256 + 256,
                             qcol:qcol + 512]
                    .rearrange("(h d) j -> d h j", d=64))

            def fnB(rowbase=rowbase, qcol=qcol, hh=hh):
                return p.dma(
                    btmp[0:64, hh * 2048: hh * 2048 + 2048]
                    .rearrange("d (h j) -> d h j", h=4),
                    cc1q_out[rowbase + 512 + hh * 256: rowbase + 512 + hh * 256 + 256,
                             qcol:qcol + 512]
                    .rearrange("(h d) j -> d h j", d=64))
            s_qbl = blend(
                q65[0:64, qb2 * 4096 + hh * 2048: qb2 * 4096 + hh * 2048 + 2048],
                btmp[0:64, hh * 2048: hh * 2048 + 2048], fnA, fnB, 1,
                f"l2_qrb{qt}_{hh}", dep_q, f"kq{hh}")

        nkc = 4 * qt + 4
        s_sc = None
        for lh in range(8):
            bank = lh % 2
            oms = []

            def emit_om(item, lh=lh, bank=bank, nkc=nkc, qt=qt):
                kc, slot, s_exp = item

                def fn_om(kc=kc, slot=slot, lh=lh, bank=bank, nkc=nkc):
                    return nc.tensor.matmul(
                        ps_ob[bank][0:65, :],
                        v_full[:, kc * 520 + lh * 65: kc * 520 + (lh + 1) * 65],
                        p_sb[:, slot * 512: slot * 512 + 512],
                        start=(kc == 0), stop=(kc == nkc - 1))
                dep = [s_exp, vf_blends[kc], s_kone]
                if kc == 0:
                    dep.append(ob_free[bank])
                s_om = p.step("tensor", fn_om, deps=dep, name=f"l2_om_{qt}_{lh}_{kc}")
                slot_om[slot] = s_om
                oms.append(s_om)

            pend = []
            for kc in range(nkc):
                slot = gi % 4
                rel = kc - 4 * qt

                def fn_sc(lh=lh, kc=kc, qb2=qb2, slot=slot):
                    return nc.tensor.matmul(
                        ps_st[slot][:],
                        k65[0:65, lh * T + kc * 128: lh * T + kc * 128 + 128],
                        q65[0:65, qb2 * 4096 + lh * 512: qb2 * 4096 + (lh + 1) * 512],
                        start=True, stop=True)
                dep = [s_qbl, kf_blends[lh], s_kone, slot_reader[slot]]
                s_sc = p.step("tensor", fn_sc, deps=dep, name=f"l2_sc_{qt}_{lh}_{kc}")

                masked = rel >= 0
                smi = gi % 2
                if masked:
                    def fn_ma(slot=slot, rel=rel, smi=smi):
                        return nc.vector.tensor_add(
                            sm_sb[:, smi * 512: smi * 512 + 512],
                            ps_st[slot][:], mask2[:, rel * 512: rel * 512 + 512])
                    s_ma = p.step("vector", fn_ma, deps=[s_sc, s_cst, sm_free[smi]],
                                  late=True, name=f"l2_ma_{qt}_{lh}_{kc}")

                def fn_exp(lh=lh, rel=rel, slot=slot, masked=masked, smi=smi):
                    src = (sm_sb[:, smi * 512: smi * 512 + 512] if masked
                           else ps_st[slot][:])
                    return nc.scalar.activation(
                        p_sb[:, slot * 512: slot * 512 + 512], src, AF.Exp,
                        bias=alb_sb[:, lh * 32 + rel + 28: lh * 32 + rel + 29])
                dep = [(s_ma if masked else s_sc), s_cst, slot_om[slot]]
                s_exp = p.step("scalar", fn_exp, deps=dep, name=f"l2_exp_{qt}_{lh}_{kc}")
                slot_reader[slot] = s_ma if masked else s_exp
                if masked:
                    sm_free[smi] = s_exp

                pend.append((kc, slot, s_exp))
                if len(pend) > 2:
                    emit_om(pend.pop(0))
                gi += 1
            for item in pend:
                emit_om(item)

            def fn_lr(bank=bank):
                return nc.vector.reciprocal(lrec[:], ps_ob[bank][64:65, :])
            s_lr = p.step("vector", fn_lr, deps=[oms[-1], last_rbm[0]],
                          late=True, name=f"l2_lr_{qt}_{lh}")

            def fn_rbm(bank=bank):
                return nc.tensor.matmul(ps_rb[bank][:], io.ones_row[:], lrec[:],
                                        start=True, stop=True)
            s_rbm = p.step("tensor", fn_rbm, deps=[s_lr, rb_free[bank]],
                           name=f"l2_rbm_{qt}_{lh}")
            last_rbm[0] = s_rbm

            def fn_rbc(bank=bank):
                return nc.vector.tensor_copy(rb_sb[:], ps_rb[bank][:])
            s_rbc = p.step("vector", fn_rbc, deps=[s_rbm], late=True,
                           name=f"l2_rbc_{qt}_{lh}")
            rb_free[bank] = s_rbc

            def fn_oe(qt=qt, lh=lh, bank=bank):
                return nc.vector.tensor_mul(
                    ostage[bank * 64: bank * 64 + 64,
                           (lh // 2) * 512:(lh // 2) * 512 + 512],
                    ps_ob[bank][0:64, :],
                    rb_sb[bank * 64: bank * 64 + 64, :])
            dep = [s_rbc] + ([prev[("ostd", qt - 1)]] if qt >= 1 else [])
            s_oe = p.step("vector", fn_oe, deps=dep, late=True, name=f"l2_oe_{qt}_{lh}")
            ob_free[bank] = s_oe
            prev[("oe", qt, lh)] = s_oe
        prev[("sclast", qt)] = s_sc

        def fn_ostd(qt=qt):
            last = None
            for hc in range(4):
                last = p.dma(cc3_in[hc * 128:(hc + 1) * 128, qt * 512:(qt + 1) * 512],
                             ostage[:, hc * 512:(hc + 1) * 512])
            return last
        s_ostd = p.step("gpsimd", fn_ostd,
                        deps=[prev[("oe", qt, lh)] for lh in range(8)],
                        kind="dma", n_dma=4, name=f"l2_ostd_{qt}")
        prev[("ostd", qt)] = s_ostd
    at_ctx.close()

    def fn_cc3():
        return nc.gpsimd.collective_compute(
            "AllGather", ALU.bypass, replica_groups=PAIRS,
            ins=[cc3_in.ap().opt()], outs=[cc3_out.ap().opt()])
    s_cc3 = p.step("gpsimd", fn_cc3, deps=[prev[("ostd", g)] for g in range(NQT)],
                   kind="collective", name="l2_cc3")

    of = io.ybuf
    pr_ctx = ExitStack()
    otmp = pr_ctx.enter_context(nc.sbuf_tensor("l2_otmp", [128, 2 * TOK], BF16))
    vb_bf = pr_ctx.enter_context(nc.sbuf_tensor("l2_vb", [128, 8], BF16))
    pb_eff = pr_ctx.enter_context(nc.sbuf_tensor("l2_pbeff", [128, 8], F32))
    of_blends = []
    for oc in range(NCH):
        def fnA(oc=oc):
            return p.dma(of[:, oc * TOK:(oc + 1) * TOK],
                         cc3_out[oc * 128:(oc + 1) * 128, 0:TOK])

        osl = oc % 2

        def fnB(oc=oc, osl=osl):
            return p.dma(otmp[:, osl * TOK:(osl + 1) * TOK],
                         cc3_out[oc * 128:(oc + 1) * 128, TOK:T])
        of_blends.append(blend(of[:, oc * TOK:(oc + 1) * TOK],
                               otmp[:, osl * TOK:(osl + 1) * TOK],
                               fnA, fnB, 1, f"l2_orb{oc}", [s_cc3], f"o{osl}"))

    w_pr2, s_wpr2 = load_weight(p, pr_ctx, "l2proj", "W_l2proj", 1024, 1024,
                                deps=[prev[("ostd", NQT - 1)]])
    pb_ctx = ExitStack()
    ps_pb2 = pb_ctx.enter_context(nc.psum_tensor("l2_pspb", [128, 8], F32))

    def fn_vb():
        return nc.vector.tensor_copy(vb_bf[:], bq_sb[:, 16:24])
    s_vb = p.step("vector", fn_vb, deps=[s_cst], name="l2_vb")

    def fn_pbe():
        last = None
        for oc in range(NCH):
            for ic in range(NCH):
                last = nc.tensor.matmul(
                    ps_pb2[:, oc:oc + 1],
                    w_pr2[:, (ic * 8 + oc) * 128:(ic * 8 + oc) * 128 + 128],
                    vb_bf[:, ic:ic + 1], start=(ic == 0), stop=(ic == NCH - 1))
        return last
    s_pbe = p.step("tensor", fn_pbe, deps=[s_vb, s_wpr2, prev[("oe", NQT - 1, 7)]],
                   name="l2_pbe")

    def fn_pbe2():
        return nc.vector.tensor_add(pb_eff[:], ps_pb2[:], bp_sb[:])
    s_pbe2 = p.step("vector", fn_pbe2, deps=[s_pbe, s_cst], late=True, name="l2_pbe2")
    pb_ctx.close()

    ps_pr = [pr_ctx.enter_context(nc.psum_tensor(f"l2_pr{i}", [128, 512], F32))
             for i in range(8)]
    yform = {}
    prev_ev = {}
    for tt in range(NTT):
        mm_steps = []
        for oc in range(NCH):
            def fn_pm(oc=oc, tt=tt):
                last = None
                for ic in range(NCH):
                    last = nc.tensor.matmul(
                        ps_pr[oc][:],
                        w_pr2[:, (ic * 8 + oc) * 128:(ic * 8 + oc) * 128 + 128],
                        of[:, ic * TOK + tt * TT: ic * TOK + tt * TT + TT],
                        start=(ic == 0), stop=(ic == NCH - 1))
                return last
            dep = of_blends + [s_wpr2, s_pbe, s_pbe2, prev_ev.get((tt - 1, oc))]
            mm_steps.append(p.step("tensor", fn_pm, deps=dep, name=f"l2_prm_{tt}_{oc}"))
        evs = []
        for oc in range(NCH):
            def fn_pe(oc=oc, tt=tt):
                sl = slice(oc * TOK + tt * TT, oc * TOK + tt * TT + TT)
                return nc.vector.scalar_tensor_tensor(
                    io.ybuf[:, sl], ps_pr[oc][:], pb_eff[:, oc:oc + 1],
                    x[:, sl], ALU.add, ALU.add)
            s_pe = p.step("vector", fn_pe,
                          deps=[mm_steps[oc], s_pbe2, mm_steps[-1]], late=True,
                          name=f"l2_pre_{tt}_{oc}")
            prev_ev[(tt, oc)] = s_pe
            evs.append(s_pe)
        yform[tt] = evs
    pr_ctx.close()

    ln_ctx = ExitStack()
    emit_postln(p, io, ln_ctx, lambda tt: yform[tt] + yform[NTT - 1],
                final=(io.cur_phase == TOTAL_PHASES - 1), tag="p4")
    ln_ctx.close()
    ctx.close()




def phase_l2_ffn(p, io):
    nc = p.nc
    ctx = ExitStack()
    e = ctx.enter_context
    x = io.cur

    FLAG = p.input("PFLAG", [128, 1], F32)
    MUR = p.input("MU_R", [128, 8], F32)
    MUK = p.input("MU_K", [128, 8], F32)
    flag_sb = e(nc.sbuf_tensor("f2_flag", [128, 1], F32))
    mur_sb = e(nc.sbuf_tensor("f2_mur", [128, 8], F32))
    muk_sb = e(nc.sbuf_tensor("f2_muk", [128, 8], F32))
    bnd = e(nc.sbuf_tensor("f2_bnd", [128, 8], BF16))
    stg = e(nc.sbuf_tensor("f2_stg", [128, 8], BF16))

    def fn_cst():
        p.dma(flag_sb[:], FLAG)
        p.dma(mur_sb[:], MUR)
        return p.dma(muk_sb[:], MUK)
    s_cst = p.step("gpsimd", fn_cst, kind="dma", n_dma=3, name="f2_cst")

    cc_in = nc.dram_tensor("f2cc_i", [128, 8], BF16)
    cc_out = nc.dram_tensor("f2cc_o", [256, 8], BF16)

    def fn_stg():
        return nc.vector.tensor_copy(
            stg[:], x[:].rearrange("p (c t) -> p c t", t=TOK)[:, :, TOK - 1:TOK].squeeze(2))
    s_stg = p.step("vector", fn_stg, deps=[], name="f2_stg")

    def fn_snd():
        return p.dma(cc_in.ap(), stg[:])
    s_snd = p.step("gpsimd", fn_snd, deps=[s_stg], kind="dma", n_dma=1, name="f2_snd")

    def fn_cc():
        return nc.gpsimd.collective_compute(
            "AllGather", ALU.bypass, replica_groups=PAIRS,
            ins=[cc_in.ap().opt()], outs=[cc_out.ap().opt()])
    s_cc = p.step("gpsimd", fn_cc, deps=[s_snd], kind="collective", name="f2_cc")

    def fn_rcv():
        return p.dma(bnd[:], cc_out[0:128, :])
    s_rcv = p.step("gpsimd", fn_rcv, deps=[s_cc], kind="dma", n_dma=1, name="f2_rcv")

    def fn_bnd():
        return nc.vector.tensor_scalar(bnd[:], bnd[:], flag_sb[:, 0:1], None, ALU.mult)
    s_bnd = p.step("vector", fn_bnd, deps=[s_rcv, s_cst], late=True, name="f2_bnd")

    xp_t = e(nc.sbuf_tensor("f2_xp", [128, 2 * 512], BF16))
    dx_t = e(nc.sbuf_tensor("f2_dx", [128, 2 * 512], BF16))
    rt_t = e(nc.sbuf_tensor("f2_rt", [128, 2 * 512], BF16))
    big_ctx = ExitStack()
    be_ = big_ctx.enter_context
    w_r, s_wr = load_weight(p, big_ctx, "f2r", "W_r2", 1024, 1024)
    w_k, s_wk = load_weight(p, big_ctx, "f2k", "W_k2", 1024, 1024)
    w_v, s_wv = load_weight(p, big_ctx, "f2v", "W_v2", 1024, 1024)
    r_sb = be_(nc.sbuf_tensor("f2_r", [128, NCH * 1024], BF16))
    ka_sb = be_(nc.sbuf_tensor("f2_ka", [128, NCH * 1024], BF16))
    xr_sb = be_(nc.sbuf_tensor("f2_xr", [128, NCH * 1024], BF16))
    xk_sb = be_(nc.sbuf_tensor("f2_xk", [128, NCH * 1024], BF16))

    ps_ctx = ExitStack()
    pse = ps_ctx.enter_context
    ps_a = [pse(nc.psum_tensor(f"f2_psa{i}", [128, 512], F32)) for i in range(2)]
    ps_b = [pse(nc.psum_tensor(f"f2_psb{i}", [128, 512], F32)) for i in range(2)]

    yform = {}
    prev = {}
    for TB in range(2):
        for st in range(2):
            tt = TB * 2 + st
            t0 = tt * TT
            for oc in range(NCH):
                k2 = (tt * NCH + oc) % 2

                def fn_sh(oc=oc, t0=t0, k2=k2, tt=tt):
                    if tt == 0:
                        p.chain(nc.vector.tensor_copy(
                            xp_t[:, k2 * 512 + 1: k2 * 512 + 512],
                            x[:, oc * TOK: oc * TOK + 511]))
                        p.chain(nc.vector.tensor_copy(
                            xp_t[:, k2 * 512: k2 * 512 + 1], bnd[:, oc:oc + 1]))
                    else:
                        p.chain(nc.vector.tensor_copy(
                            xp_t[:, k2 * 512: k2 * 512 + 512],
                            x[:, oc * TOK + t0 - 1: oc * TOK + t0 + 511]))
                    p.chain(nc.vector.tensor_sub(
                        dx_t[:, k2 * 512:(k2 + 1) * 512],
                        x[:, oc * TOK + t0: oc * TOK + t0 + 512],
                        xp_t[:, k2 * 512:(k2 + 1) * 512]))
                    p.chain(nc.vector.scalar_tensor_tensor(
                        xr_sb[:, oc * 1024 + (tt % 2) * 512: oc * 1024 + (tt % 2) * 512 + 512],
                        dx_t[:, k2 * 512:(k2 + 1) * 512], mur_sb[:, oc:oc + 1],
                        xp_t[:, k2 * 512:(k2 + 1) * 512], ALU.mult, ALU.add))
                    return nc.vector.scalar_tensor_tensor(
                        xk_sb[:, oc * 1024 + (tt % 2) * 512: oc * 1024 + (tt % 2) * 512 + 512],
                        dx_t[:, k2 * 512:(k2 + 1) * 512], muk_sb[:, oc:oc + 1],
                        xp_t[:, k2 * 512:(k2 + 1) * 512], ALU.mult, ALU.add)
                dep = [s_bnd, s_cst, prev.get(("mm_r_last", TB - 1)),
                       prev.get(("mm_k_last", TB - 1)), prev.get(("mm_v_last", TB - 1))]
                prev[("sh", tt, oc)] = p.step("vector", fn_sh, deps=dep, late=True,
                                              name=f"f2_sh_{tt}_{oc}")

        sh_all = [prev[("sh", TB * 2 + st, oc)] for st in range(2) for oc in range(NCH)]

        mi = 0
        for oc in range(NCH):
            for st in range(2):
                def fn_mm(oc=oc, st=st, mi=mi):
                    last = None
                    for ic in range(NCH):
                        last = nc.tensor.matmul(
                            ps_a[mi % 2][:],
                            w_r[:, ic * 1024 + oc * 128: ic * 1024 + oc * 128 + 128],
                            xr_sb[:, ic * 1024 + st * 512: ic * 1024 + st * 512 + 512],
                            start=(ic == 0), stop=(ic == NCH - 1))
                    return last
                dep = sh_all + [s_wr, prev.get(("ev_r", mi - 2))]
                s_mm = p.step("tensor", fn_mm, deps=dep, name=f"f2_rm_{TB}_{oc}_{st}")
                prev[("mm_r_last", TB)] = s_mm

                def fn_ev(oc=oc, st=st, mi=mi):
                    return nc.scalar.activation(
                        r_sb[:, oc * 1024 + st * 512: oc * 1024 + st * 512 + 512],
                        ps_a[mi % 2][:], AF.Sigmoid)
                prev[("ev_r", mi)] = p.step("scalar", fn_ev,
                                            deps=[s_mm, prev.get(("ev_r", mi - 2))],
                                            name=f"f2_re_{TB}_{oc}_{st}")
                mi += 1

        mi = 0
        for oc in range(NCH):
            for st in range(2):
                def fn_mm(oc=oc, st=st, mi=mi):
                    last = None
                    for ic in range(NCH):
                        last = nc.tensor.matmul(
                            ps_b[mi % 2][:],
                            w_k[:, ic * 1024 + oc * 128: ic * 1024 + oc * 128 + 128],
                            xk_sb[:, ic * 1024 + st * 512: ic * 1024 + st * 512 + 512],
                            start=(ic == 0), stop=(ic == NCH - 1))
                    return last
                dep = sh_all + [s_wk, prev.get(("ev_k", mi - 2))]
                s_mm = p.step("tensor", fn_mm, deps=dep, name=f"f2_km_{TB}_{oc}_{st}")
                prev[("mm_k_last", TB)] = s_mm

                def fn_rl(mi=mi):
                    return nc.scalar.activation(rt_t[:, (mi % 2) * 512:(mi % 2 + 1) * 512],
                                                ps_b[mi % 2][:], AF.Relu)
                s_rl = p.step("scalar", fn_rl, deps=[s_mm, prev.get(("ev_k", mi - 2))],
                              name=f"f2_kr_{TB}_{oc}_{st}")

                def fn_ev(oc=oc, st=st, mi=mi):
                    return nc.vector.tensor_mul(
                        ka_sb[:, oc * 1024 + st * 512: oc * 1024 + st * 512 + 512],
                        rt_t[:, (mi % 2) * 512:(mi % 2 + 1) * 512], ps_b[mi % 2][:])
                prev[("ev_k", mi)] = p.step("vector", fn_ev, deps=[s_rl],
                                            name=f"f2_ke_{TB}_{oc}_{st}")
                mi += 1

        ka_all = [prev[("ev_k", i)] for i in range(16)]
        r_all = [prev[("ev_r", i)] for i in range(16)]

        mi = 0
        for oc in range(NCH):
            for st in range(2):
                tt = TB * 2 + st

                def fn_mm(oc=oc, st=st, mi=mi):
                    last = None
                    for ic in range(NCH):
                        last = nc.tensor.matmul(
                            ps_a[mi % 2][:],
                            w_v[:, ic * 1024 + oc * 128: ic * 1024 + oc * 128 + 128],
                            ka_sb[:, ic * 1024 + st * 512: ic * 1024 + st * 512 + 512],
                            start=(ic == 0), stop=(ic == NCH - 1))
                    return last
                dep = ka_all + [s_wv, prev.get(("ev_v", mi - 2)), prev.get(("ev_r", 15))]
                s_mm = p.step("tensor", fn_mm, deps=dep, name=f"f2_vm_{TB}_{oc}_{st}")
                prev[("mm_v_last", TB)] = s_mm

                def fn_ev(oc=oc, st=st, mi=mi, tt=tt):
                    sl = slice(oc * TOK + tt * TT, oc * TOK + tt * TT + TT)
                    p.chain(nc.vector.tensor_mul(
                        rt_t[:, (mi % 2) * 512:(mi % 2 + 1) * 512],
                        r_sb[:, oc * 1024 + st * 512: oc * 1024 + st * 512 + 512],
                        ps_a[mi % 2][:]))
                    return nc.vector.tensor_add(io.ybuf[:, sl],
                                                rt_t[:, (mi % 2) * 512:(mi % 2 + 1) * 512],
                                                x[:, sl])
                s_ev = p.step("vector", fn_ev, deps=[s_mm] + r_all, late=True,
                              name=f"f2_ve_{TB}_{oc}_{st}")
                prev[("ev_v", mi)] = s_ev
                yform.setdefault(tt, []).append(s_ev)
                mi += 1

    ps_ctx.close()
    big_ctx.close()
    ln_ctx = ExitStack()
    emit_postln(p, io, ln_ctx, lambda tt: yform[tt] + yform[NTT - 1],
                final=(io.cur_phase == TOTAL_PHASES - 1), tag="p5")
    ln_ctx.close()
    ctx.close()




def phase_l3_ffn(p, io):
    nc = p.nc
    ctx = ExitStack()
    x = io.cur
    NT4 = 4

    RMP = p.input("RM_P", [16, 256 * 16], F32)
    RUPP = p.input("RUP_P", [16, 256 * 64], F32)
    WCP = p.input("W_cp", [16384, 1024], F32)

    C_scr = nc.dram_tensor("f3_cscr", [2, 16, 8192], BF16)
    XB_scr = nc.dram_tensor("f3_xb", [1024, 512], BF16)
    Z_scr = nc.dram_tensor("f3_z", [16384, TOK], BF16)

    cp_ctx = ExitStack()
    cp = cp_ctx.enter_context
    rm_sb = cp(nc.sbuf_tensor("f3_rm", [16, 256 * 16], BF16))
    rup_sb = cp(nc.sbuf_tensor("f3_rup", [16, 256 * 64], BF16))
    cstg = cp(nc.sbuf_tensor("f3_cstg", [16, 8192], BF16))
    ps_c = cp(nc.psum_tensor("f3_psc", [128, 512], F32))
    prev = {}

    def fn_rl():
        p.dma(rm_sb[:], RMP)
        return p.dma(rup_sb[:], RUPP)
    s_rl = p.step("gpsimd", fn_rl, kind="dma", n_dma=2, name="f3_rl")

    for j in range(2):
        for g in range(16):
            def fn_cm(j=j, g=g):
                last = None
                for s in range(8):
                    hrp = g * 8 + s
                    h, rp = hrp // 32, hrp % 32
                    hr = h * 64 + rp * 2 + j
                    last = nc.tensor.matmul(
                        ps_c[0:16, s * 64:(s + 1) * 64],
                        rm_sb[:, hr * 16:(hr + 1) * 16],
                        rup_sb[:, hr * 64:(hr + 1) * 64],
                        start=True, stop=True)
                return last
            dep = [s_rl, prev.get(("ce", j, g - 1))]
            if g == 0 and j > 0:
                dep.append(prev.get(("ce", j - 1, 15)))
            s_cm = p.step("tensor", fn_cm, deps=dep, name=f"f3_cm_{j}_{g}")

            def fn_ce(g=g):
                return nc.vector.tensor_copy(cstg[:, g * 512:(g + 1) * 512],
                                             ps_c[0:16, :])
            s_ce = p.step("vector", fn_ce, deps=[s_cm, prev.get(("cso", j - 1))],
                          name=f"f3_ce_{j}_{g}")
            prev[("ce", j, g)] = s_ce

        def fn_cso(j=j):
            return p.dma(C_scr[j], cstg[0:16, :].rearrange("p (a c) -> p a c", a=16))
        s_cso = p.step("gpsimd", fn_cso, deps=[prev[("ce", j, g)] for g in range(16)],
                       kind="dma", n_dma=1, name=f"f3_cso_{j}")
        prev[("cso", j)] = s_cso
    cp_ctx.close()

    z1_ctx = ExitStack()
    z1 = z1_ctx.enter_context
    C_sb = z1(nc.sbuf_tensor("f3_C", [128, 128 * 64], BF16))
    xr2 = z1(nc.sbuf_tensor("f3_xr2", [128, 32 * 512], BF16))
    zstage = z1(nc.sbuf_tensor("f3_zst", [128, 2 * 512], BF16))
    ps_z = [z1(nc.psum_tensor(f"f3_psz{i}", [128, 512], F32)) for i in range(2)]

    def fn_cli():
        last = None
        for j in range(2):
            last = p.dma(C_sb[64 * j:64 * j + 16, :]
                         .rearrange("p (a c) -> p a c", a=16), C_scr[j])
        return last
    s_cli = p.step("gpsimd", fn_cli, deps=[prev[("cso", 1)]], kind="dma", n_dma=2,
                   name="f3_cli")

    xb_view = XB_scr.ap().rearrange("(rp j i) t -> rp j i t", j=2, i=16)
    for t4 in range(NT4):
        t0 = t4 * 512

        def fn_xo(t0=t0):
            last = None
            for c in range(NCH):
                last = p.dma(XB_scr[c * 128:(c + 1) * 128, :],
                             x[:, c * TOK + t0: c * TOK + t0 + 512])
            return last
        dep = [prev.get(("xrl", t4 - 1))] + ([s_cli] if t4 == 0 else [])
        s_xo = p.step("gpsimd", fn_xo, deps=dep, kind="dma", n_dma=8, name=f"f3_xo_{t4}")

        def fn_xrl():
            last = None
            for j in range(2):
                src = xb_view[:, j, :, :].transpose([1, 0, 2])
                last = p.dma(xr2[64 * j:64 * j + 16, :]
                             .rearrange("p (rp t) -> p rp t", t=512), src)
            return last
        dep = [s_xo] + [prev[("zm", hh, 31, t4 - 1)] for hh in range(4) if t4 >= 1]
        s_xrl = p.step("gpsimd", fn_xrl, deps=dep, kind="dma", n_dma=2, name=f"f3_xrl_{t4}")
        prev[("xrl", t4)] = s_xrl

        for h in range(4):
            for rp in range(32):
                bk = rp % 2

                def fn_zm(h=h, rp=rp, bk=bk):
                    nc.tensor.matmul(
                        ps_z[bk][0:64, :],
                        C_sb[0:16, (h * 32 + rp) * 64:(h * 32 + rp + 1) * 64],
                        xr2[0:16, rp * 512:(rp + 1) * 512],
                        start=True, stop=True)
                    return nc.tensor.matmul(
                        ps_z[bk][64:128, :],
                        C_sb[64:80, (h * 32 + rp) * 64:(h * 32 + rp + 1) * 64],
                        xr2[64:80, rp * 512:(rp + 1) * 512],
                        start=True, stop=True)
                dep = [s_cli, s_xrl, prev.get(("ze", h, rp - 2, t4))]
                if rp < 2 and h > 0:
                    dep.append(prev.get(("ze", h - 1, 30 + rp, t4)))
                s_zm = p.step("tensor", fn_zm, deps=dep, name=f"f3_zm_{t4}_{h}_{rp}")
                prev[("zm", h, rp, t4)] = s_zm

                def fn_ze(bk=bk):
                    return nc.scalar.activation(zstage[:, bk * 512:(bk + 1) * 512],
                                                ps_z[bk][:], AF.Gelu)
                dep_e = [s_zm, prev.get(("zsp", h, rp - 2, t4)),
                         (prev.get(("zsp", h - 1, 30 + rp, t4)) if rp < 2 and h > 0 else None),
                         (prev.get(("zsp", 3, 30 + rp, t4 - 1)) if rp < 2 and h == 0 else None)]
                s_ze = p.step("scalar", fn_ze, deps=dep_e, name=f"f3_ze_{t4}_{h}_{rp}")
                prev[("ze", h, rp, t4)] = s_ze

                def fn_zs(h=h, rp=rp, bk=bk, t0=t0):
                    zr = h * 4096 + rp * 128
                    return p.dma(Z_scr[zr:zr + 128, t0:t0 + 512],
                                 zstage[:, bk * 512:(bk + 1) * 512])
                s_zs = p.step("gpsimd", fn_zs, deps=[s_ze], kind="dma", n_dma=1,
                              name=f"f3_zs_{t4}_{h}_{rp}")
                prev[("zsp", h, rp, t4)] = s_zs
    zsp_all = [prev[("zsp", h, rp, t4)] for h in range(4) for rp in range(32)
               for t4 in range(NT4)]
    z1_ctx.close()

    c2_ctx = ExitStack()
    c2 = c2_ctx.enter_context
    wcp_sb = [c2(nc.sbuf_tensor(f"f3_wcp{i}", [128, 8 * 1024], BF16)) for i in range(2)]
    zsec = [c2(nc.sbuf_tensor(f"f3_zsec{i}", [128, 8 * 512], BF16)) for i in range(2)]
    y_acc = c2(nc.sbuf_tensor("f3_y", [128, 32 * 512], F32))
    ps_y = [c2(nc.psum_tensor(f"f3_psy{i}", [128, 512], F32)) for i in range(2)]

    wl_reader = [None, None]
    zl_reader = [None, None]
    prev_y = {}
    zi = 0
    for ws in range(16):
        buf = ws % 2

        def fn_wl(ws=ws, buf=buf):
            last = None
            for c in range(8):
                last = p.dma(wcp_sb[buf][:, c * 1024:(c + 1) * 1024],
                             WCP[(ws * 8 + c) * 128:(ws * 8 + c + 1) * 128, :])
            return last
        dep = [wl_reader[buf]] + (zsp_all if ws < 2 else [])
        s_wl = p.step("gpsimd", fn_wl, deps=dep, kind="dma", n_dma=8,
                      name=f"f3_wl_{ws}")

        for tt in range(NTT):
            zbuf = zi % 2

            def fn_zl(ws=ws, zbuf=zbuf, tt=tt):
                last = None
                for c in range(8):
                    last = p.dma(zsec[zbuf][:, c * 512:(c + 1) * 512],
                                 Z_scr[(ws * 8 + c) * 128:(ws * 8 + c + 1) * 128,
                                       tt * TT: tt * TT + 512])
                return last
            dep = [zl_reader[zbuf]] + \
                [prev[("zsp", ws // 4, (ws % 4) * 8 + c, tt)] for c in range(8)] + \
                (zsp_all if ws < 1 else [])
            s_zl = p.step("sync", fn_zl, deps=dep, kind="dma", n_dma=8,
                          name=f"f3_zl_{ws}_{tt}")

            for oc in range(NCH):
                k2 = oc % 2

                def fn_ym(oc=oc, buf=buf, zbuf=zbuf, k2=k2):
                    last = None
                    for c in range(8):
                        last = nc.tensor.matmul(
                            ps_y[k2][:],
                            wcp_sb[buf][:, c * 1024 + oc * 128: c * 1024 + oc * 128 + 128],
                            zsec[zbuf][:, c * 512:(c + 1) * 512],
                            start=(c == 0), stop=(c == 7))
                    return last
                dep = [s_wl, s_zl, prev_y.get(k2)]
                s_ym = p.step("tensor", fn_ym, deps=dep, name=f"f3_ym_{ws}_{tt}_{oc}")
                wl_reader[buf] = s_ym
                zl_reader[zbuf] = s_ym

                def fn_ye(oc=oc, tt=tt, ws=ws, k2=k2):
                    ysl = y_acc[:, (tt * 8 + oc) * 512:(tt * 8 + oc + 1) * 512]
                    if ws == 0:
                        return nc.vector.tensor_copy(ysl, ps_y[k2][:])
                    return nc.vector.tensor_add(ysl, ysl, ps_y[k2][:])
                s_ye = p.step("vector", fn_ye, deps=[s_ym], late=True,
                              name=f"f3_ye_{ws}_{tt}_{oc}")
                prev_y[k2] = s_ye
                prev_y[("ysl", tt, oc)] = s_ye
            zi += 1

    yform = {}
    for tt in range(NTT):
        steps = []
        for oc in range(NCH):
            def fn_yf(oc=oc, tt=tt):
                sl = slice(oc * TOK + tt * TT, oc * TOK + tt * TT + TT)
                return nc.vector.tensor_add(
                    io.ybuf[:, sl],
                    y_acc[:, (tt * 8 + oc) * 512:(tt * 8 + oc + 1) * 512], x[:, sl])
            steps.append(p.step("vector", fn_yf, deps=[prev_y[("ysl", tt, oc)]],
                                late=True, name=f"f3_yf_{tt}_{oc}"))
        yform[tt] = steps
    c2_ctx.close()

    ln_ctx = ExitStack()
    emit_postln(p, io, ln_ctx, lambda tt: yform[tt] + yform[NTT - 1],
                final=(io.cur_phase == TOTAL_PHASES - 1), tag="p7")
    ln_ctx.close()
    ctx.close()





def phase_l3_attn(p, io):
    nc = p.nc
    ctx = ExitStack()
    e = ctx.enter_context
    x = io.cur
    DI = 2048
    NDC = 16

    FLAG = p.input("PFLAG", [128, 1], F32)
    FLAGR = p.input("PFLAGR", [128, 1], F32)
    CONVW = p.input("CONVW", [128, 48], F32)
    CONVB = p.input("CONVB", [128, 16], F32)
    WIN = p.input("W_in3", [1024, 4096], F32)
    WDT = p.input("W_dt3", [2048, 2048], F32)
    A3 = p.input("A3", [2048, 2048], F32)
    B3 = p.input("B3", [2048, 2048], F32)
    C3 = p.input("C3", [2048, 2048], F32)
    WOUT = p.input("W_out3", [2048, 1024], F32)

    flag_sb = e(nc.sbuf_tensor("l3_flag", [128, 1], F32))
    flagr_sb = e(nc.sbuf_tensor("l3_flagr", [128, 1], F32))
    cw_sb = e(nc.sbuf_tensor("l3_cw", [128, 48], F32))
    cb_sb = e(nc.sbuf_tensor("l3_cb", [128, 16], F32))
    stg16 = e(nc.sbuf_tensor("l3_stg16", [128, 16], BF16))
    recv16 = e(nc.sbuf_tensor("l3_recv16", [128, 16], BF16))
    recv16b = e(nc.sbuf_tensor("l3_recv16b", [128, 16], BF16))
    xhl = e(nc.sbuf_tensor("l3_xhl", [128, 8], BF16))
    xhr = e(nc.sbuf_tensor("l3_xhr", [128, 8], BF16))
    ones512 = e(nc.sbuf_tensor("l3_ones", [128, 512], BF16))
    carry = e(nc.sbuf_tensor("l3_carry", [128, 4 * 16], F32))
    recv32 = e(nc.sbuf_tensor("l3_recv32", [128, 32], F32))
    xs0c = e(nc.sbuf_tensor("l3_xs0c", [128, 16], F32))
    dsoff = e(nc.sbuf_tensor("l3_dsoff", [128, 16], F32))

    U_scr = nc.dram_tensor("l3_u", [2048, 2050], BF16)
    Z_scr = nc.dram_tensor("l3_zs", [2048, TOK], BF16)
    UCV_scr = nc.dram_tensor("l3_ucv", [2048, TOK], BF16)
    XS_scr = nc.dram_tensor("l3_xs", [2048, TOK], BF16)
    PP_scr = nc.dram_tensor("l3_pp", [2048, TOK], BF16)
    DS_scr = nc.dram_tensor("l3_ds", [2048, TOK], BF16)
    CPU_scr = nc.dram_tensor("l3_cpu", [2048, TOK], BF16)
    cc1_in = nc.dram_tensor("l3cc1i", [128, 16], BF16)
    cc1_out = nc.dram_tensor("l3cc1o", [256, 16], BF16)
    cc2_in = nc.dram_tensor("l3cc2i", [128, 32], F32)
    cc2_out = nc.dram_tensor("l3cc2o", [256, 32], F32)

    def fn_cst():
        p.dma(flag_sb[:], FLAG)
        p.dma(flagr_sb[:], FLAGR)
        p.dma(cw_sb[:], CONVW)
        return p.dma(cb_sb[:], CONVB)
    s_cst = p.step("gpsimd", fn_cst, kind="dma", n_dma=4, name="l3_cst")

    C3B_scr = nc.dram_tensor("l3_c3b", [2048, 2048], BF16)
    WOUTB_scr = nc.dram_tensor("l3_wob", [2048, 1024], BF16)
    pcstg = e(nc.sbuf_tensor("l3_pcstg", [128, 2 * 2048], BF16))
    pc_steps = []
    pc_prev = {}
    for ch in range(32):
        sl = ch % 2

        def fn_pcl(ch=ch, sl=sl):
            if ch < 16:
                return p.dma(pcstg[:, sl * 2048:(sl + 1) * 2048],
                             C3[ch * 128:(ch + 1) * 128, :])
            return p.dma(pcstg[:, sl * 2048: sl * 2048 + 1024],
                         WOUT[(ch - 16) * 128:(ch - 15) * 128, :])
        s_pcl = p.step("gpsimd", fn_pcl, deps=[pc_prev.get(ch - 2)], kind="dma",
                       n_dma=1, name=f"l3_pcl{ch}")

        def fn_pcs(ch=ch, sl=sl):
            if ch < 16:
                return p.dma(C3B_scr[ch * 128:(ch + 1) * 128, :],
                             pcstg[:, sl * 2048:(sl + 1) * 2048])
            return p.dma(WOUTB_scr[(ch - 16) * 128:(ch - 15) * 128, :],
                         pcstg[:, sl * 2048: sl * 2048 + 1024])
        s_pcs = p.step("gpsimd", fn_pcs, deps=[s_pcl], kind="dma", n_dma=1,
                       name=f"l3_pcs{ch}")
        pc_prev[ch] = s_pcs
        pc_steps.append(s_pcs)

    def fn_ini():
        p.chain(nc.vector.memset(ones512[:], 1.0))
        p.chain(nc.vector.memset(carry[:, 0:16], 0.0))
        p.chain(nc.vector.memset(carry[:, 16:32], 1.0))
        p.chain(nc.vector.memset(carry[:, 32:48], 1.0))
        return nc.vector.memset(carry[:, 48:64], 0.0)
    s_ini = p.step("vector", fn_ini, name="l3_ini")

    def fn_stg():
        xv = x[:].rearrange("p (c t) -> p c t", t=TOK)
        p.chain(nc.vector.tensor_copy(stg16[:, 0:8], xv[:, :, 0:1].squeeze(2)))
        return nc.vector.tensor_copy(stg16[:, 8:16], xv[:, :, TOK - 1:TOK].squeeze(2))
    s_stg = p.step("vector", fn_stg, deps=[], name="l3_stg")

    def fn_snd():
        return p.dma(cc1_in.ap(), stg16[:])
    s_snd = p.step("gpsimd", fn_snd, deps=[s_stg], kind="dma", n_dma=1, name="l3_snd")

    def fn_cc1():
        return nc.gpsimd.collective_compute(
            "AllGather", ALU.bypass, replica_groups=PAIRS,
            ins=[cc1_in.ap().opt()], outs=[cc1_out.ap().opt()])
    s_cc1 = p.step("gpsimd", fn_cc1, deps=[s_snd], kind="collective", name="l3_cc1")

    def fn_rcv():
        p.dma(recv16[:], cc1_out[0:128, :])
        return p.dma(recv16b[:], cc1_out[128:256, :])
    s_rcv = p.step("gpsimd", fn_rcv, deps=[s_cc1], kind="dma", n_dma=2, name="l3_rcv")

    def fn_xh():
        p.chain(nc.vector.tensor_scalar(xhl[:], recv16[:, 8:16], flag_sb[:, 0:1],
                                        None, ALU.mult))
        return nc.vector.tensor_scalar(xhr[:], recv16b[:, 0:8], flagr_sb[:, 0:1],
                                       None, ALU.mult)
    s_xh = p.step("vector", fn_xh, deps=[s_rcv, s_cst], late=True, name="l3_xh")

    l1_ctx = ExitStack()
    l1 = l1_ctx.enter_context
    wsec = [l1(nc.sbuf_tensor(f"l3_ws{i}", [128, NCH * 1024], BF16)) for i in range(2)]
    xzst = l1(nc.sbuf_tensor("l3_xzst", [128, 2 * 512], BF16))
    hst = l1(nc.sbuf_tensor("l3_hst", [128, 4], BF16))
    ps_1 = [l1(nc.psum_tensor(f"l3_ps1{i}", [128, 512], F32)) for i in range(2)]
    ps_h = l1(nc.psum_tensor("l3_psh", [128, 4], F32))

    prev = {}
    mi = 0
    for sec in range(4):
        buf = sec % 2

        def fn_wl(sec=sec, buf=buf):
            last = None
            for c in range(NCH):
                last = p.dma(wsec[buf][:, c * 1024:(c + 1) * 1024],
                             WIN[c * 128:(c + 1) * 128, sec * 1024:(sec + 1) * 1024])
            return last
        s_wl = p.step("gpsimd", fn_wl, deps=[prev.get(("l1mm_last", sec - 2))],
                      kind="dma", n_dma=NCH, name=f"l3_wl{sec}")

        for oc8 in range(8):
            ocg = sec * 8 + oc8
            if sec < 2:
                def fn_hm2(oc8=oc8, buf=buf):
                    last = None
                    for ic in range(NCH):
                        last = nc.tensor.matmul(
                            ps_h[:, 0:1],
                            wsec[buf][:, ic * 1024 + oc8 * 128: ic * 1024 + oc8 * 128 + 128],
                            xhl[:, ic:ic + 1], start=(ic == 0), stop=(ic == NCH - 1))
                    for ic in range(NCH):
                        last = nc.tensor.matmul(
                            ps_h[:, 1:2],
                            wsec[buf][:, ic * 1024 + oc8 * 128: ic * 1024 + oc8 * 128 + 128],
                            xhr[:, ic:ic + 1], start=(ic == 0), stop=(ic == NCH - 1))
                    return last
                s_hm = p.step("tensor", fn_hm2, deps=[s_wl, s_xh, prev.get(("he", ocg - 1))],
                              name=f"l3_hm_{ocg}")

                def fn_he():
                    return nc.vector.tensor_copy(hst[:, 0:2], ps_h[:, 0:2])
                s_he = p.step("vector", fn_he, deps=[s_hm, prev.get(("hs", ocg - 1))],
                              name=f"l3_he_{ocg}")
                prev[("he", ocg)] = s_he

                def fn_hs(ocg=ocg):
                    p.dma(U_scr[ocg * 128:(ocg + 1) * 128, 0:1], hst[:, 0:1],
                          allow_slow_non_contiguous=True)
                    return p.dma(U_scr[ocg * 128:(ocg + 1) * 128, 2049:2050], hst[:, 1:2],
                                 allow_slow_non_contiguous=True)
                s_hs = p.step("gpsimd", fn_hs, deps=[s_he], kind="dma", n_dma=2,
                              name=f"l3_hs_{ocg}")
                prev[("hs", ocg)] = s_hs

            for tile in range(NTT):
                def fn_mm(oc8=oc8, tile=tile, buf=buf, mi=mi):
                    last = None
                    for ic in range(NCH):
                        last = nc.tensor.matmul(
                            ps_1[mi % 2][:],
                            wsec[buf][:, ic * 1024 + oc8 * 128: ic * 1024 + oc8 * 128 + 128],
                            x[:, ic * TOK + tile * TT: ic * TOK + tile * TT + TT],
                            start=(ic == 0), stop=(ic == NCH - 1))
                    return last
                dep = [s_wl, prev.get(("l1e", mi - 2))]
                s_mm = p.step("tensor", fn_mm, deps=dep, name=f"l3_xzm_{ocg}_{tile}")
                prev[("l1mm_last", sec)] = s_mm

                def fn_ev(mi=mi):
                    return nc.vector.tensor_copy(
                        xzst[:, (mi % 2) * 512:(mi % 2 + 1) * 512], ps_1[mi % 2][:])
                s_ev = p.step("vector", fn_ev, deps=[s_mm, prev.get(("l1s", mi - 2))],
                              name=f"l3_xze_{ocg}_{tile}")
                prev[("l1e", mi)] = s_ev

                def fn_sp(ocg=ocg, tile=tile, mi=mi):
                    if ocg < 16:
                        return p.dma(U_scr[ocg * 128:(ocg + 1) * 128,
                                           1 + tile * TT: 1 + tile * TT + TT],
                                     xzst[:, (mi % 2) * 512:(mi % 2 + 1) * 512])
                    return p.dma(Z_scr[(ocg - 16) * 128:(ocg - 15) * 128,
                                       tile * TT: tile * TT + TT],
                                 xzst[:, (mi % 2) * 512:(mi % 2 + 1) * 512])
                s_sp = p.step("gpsimd", fn_sp, deps=[s_ev], kind="dma", n_dma=1,
                              name=f"l3_xzs_{ocg}_{tile}")
                prev[("l1s", mi)] = s_sp
                mi += 1
    l1_done = [prev[("l1s", i)] for i in range(mi)] + \
              [prev[("hs", g)] for g in range(16)]
    l1_ctx.close()

    l2_ctx = ExitStack()
    l2 = l2_ctx.enter_context
    ub = [l2(nc.sbuf_tensor(f"l3_ub{i}", [128, 2050], BF16)) for i in range(2)]
    ucb = [l2(nc.sbuf_tensor(f"l3_ucb{i}", [128, TOK], BF16)) for i in range(2)]
    ct1 = l2(nc.sbuf_tensor("l3_ct1", [128, TOK], F32))

    for oc in range(NDC):
        def fn_ul(oc=oc):
            return p.dma(ub[oc % 2][:], U_scr[oc * 128:(oc + 1) * 128, :])
        dep = l1_done if oc < 2 else [prev[("cve", oc - 2)]]
        s_ul = p.step("gpsimd", fn_ul, deps=dep, kind="dma", n_dma=1, name=f"l3_ul{oc}")

        def fn_cv(oc=oc):
            b = ub[oc % 2]
            p.chain(nc.vector.tensor_scalar(ct1[:], b[:, 0:2048],
                                            cw_sb[:, oc * 3:oc * 3 + 1], None, ALU.mult))
            p.chain(nc.vector.scalar_tensor_tensor(
                ct1[:], b[:, 1:2049], cw_sb[:, oc * 3 + 1:oc * 3 + 2], ct1[:],
                ALU.mult, ALU.add))
            return nc.vector.scalar_tensor_tensor(
                ct1[:], b[:, 2:2050], cw_sb[:, oc * 3 + 2:oc * 3 + 3], ct1[:],
                ALU.mult, ALU.add)
        s_cv = p.step("vector", fn_cv, deps=[s_ul, s_cst, prev.get(("cva", oc - 1))],
                      late=True, name=f"l3_cv{oc}")

        def fn_cva(oc=oc):
            return nc.scalar.activation(ucb[oc % 2][:], ct1[:], AF.Silu,
                                        bias=cb_sb[:, oc:oc + 1])
        s_cva = p.step("scalar", fn_cva, deps=[s_cv, prev.get(("cve", oc - 2))],
                       name=f"l3_cva{oc}")
        prev[("cva", oc)] = s_cva

        def fn_cve(oc=oc):
            return p.dma(UCV_scr[oc * 128:(oc + 1) * 128, :], ucb[oc % 2][:])
        s_cve = p.step("gpsimd", fn_cve, deps=[s_cva], kind="dma", n_dma=1,
                       name=f"l3_cve{oc}")
        prev[("cve", oc)] = s_cve
    l2_done = [prev[("cve", NDC - 1)], prev[("cve", NDC - 2)]]
    l2_ctx.close()

    l34_ctx = ExitStack()
    delt_full = l34_ctx.enter_context(
        nc.sbuf_tensor("l3_delt", [128, NDC * 2048], BF16))

    l3_ctx = ExitStack()
    l3 = l3_ctx.enter_context
    wdt = [l3(nc.sbuf_tensor(f"l3_wdt{i}", [128, 16 * 512], BF16)) for i in range(2)]
    ucvt = [l3(nc.sbuf_tensor(f"l3_ucvt{i}", [128, 16 * 512], BF16)) for i in range(2)]
    sps = l3(nc.sbuf_tensor("l3_sps", [128, 512], F32))
    ps_3 = [l3(nc.psum_tensor(f"l3_ps3{i}", [128, 512], F32)) for i in range(2)]

    di = 0
    for sec in range(4):
        buf = sec % 2

        def fn_wdl(sec=sec, buf=buf):
            last = None
            for c in range(NDC):
                last = p.dma(wdt[buf][:, c * 512:(c + 1) * 512],
                             WDT[c * 128:(c + 1) * 128, sec * 512:(sec + 1) * 512])
            return last
        dep = l2_done + [prev.get(("l3mm_last", sec - 2))]
        s_wdl = p.step("gpsimd", fn_wdl, deps=dep, kind="dma", n_dma=NDC,
                       name=f"l3_wdl{sec}")

        for tile in range(NTT):
            tix = sec * NTT + tile
            tslot = tix % 2

            def fn_uld(tile=tile, tslot=tslot):
                last = None
                for c in range(NDC):
                    last = p.dma(ucvt[tslot][:, c * 512:(c + 1) * 512],
                                 UCV_scr[c * 128:(c + 1) * 128, tile * TT: tile * TT + TT])
                return last
            dep = l2_done + [prev.get(("l3mmt", tix - 2))]
            s_uld = p.step("gpsimd", fn_uld, deps=dep, kind="dma", n_dma=NDC,
                           name=f"l3_uld{sec}_{tile}")

            for oc4 in range(4):
                ocg = sec * 4 + oc4

                def fn_mm(oc4=oc4, buf=buf, di=di, tslot=tslot):
                    last = None
                    for ic in range(NDC):
                        last = nc.tensor.matmul(
                            ps_3[di % 2][:],
                            wdt[buf][:, ic * 512 + oc4 * 128: ic * 512 + oc4 * 128 + 128],
                            ucvt[tslot][:, ic * 512:(ic + 1) * 512],
                            start=(ic == 0), stop=(ic == NDC - 1))
                    return last
                dep = [s_wdl, s_uld, prev.get(("l3sx", di - 2))]
                s_mm = p.step("tensor", fn_mm, deps=dep, name=f"l3_dm_{ocg}_{tile}")
                prev[("l3mm_last", sec)] = s_mm
                prev[("l3mmt", tix)] = s_mm

                def fn_spx(di=di, ocg=ocg, tile=tile):
                    p.chain(nc.scalar.activation(sps[:], ps_3[di % 2][:], AF.Exp))
                    return nc.scalar.activation(
                        delt_full[:, ocg * 2048 + tile * TT: ocg * 2048 + tile * TT + TT],
                        sps[:], AF.Ln, bias=1.0)
                s_sx = p.step("scalar", fn_spx, deps=[s_mm, prev.get(("l3sx", di - 1))],
                              name=f"l3_sx_{ocg}_{tile}")
                prev[("l3sx", di)] = s_sx
                di += 1
    l3_sx_last = prev[("l3sx", di - 1)]
    l3_mm_last = prev[("l3mm_last", 3)]
    l3_ctx.close()

    l4_ctx = ExitStack()
    l4 = l4_ctx.enter_context
    af4 = [l4(nc.sbuf_tensor(f"l3_af4{i}", [128, 2048], F32)) for i in range(2)]
    ea = [l4(nc.sbuf_tensor(f"l3_ea{i}", [128, 2048], BF16)) for i in range(2)]
    bb = [l4(nc.sbuf_tensor(f"l3_bb{i}", [128, 2048], BF16)) for i in range(2)]
    ucv4 = [l4(nc.sbuf_tensor(f"l3_ucv4{i}", [128, 512], BF16)) for i in range(2)]
    dA = l4(nc.sbuf_tensor("l3_dA", [128, 512], F32))
    dbu = l4(nc.sbuf_tensor("l3_dbu", [128, 512], BF16))
    xs4 = [l4(nc.sbuf_tensor(f"l3_x4{i}", [128, 512], BF16)) for i in range(2)]
    pp4 = [l4(nc.sbuf_tensor(f"l3_p4{i}", [128, 512], BF16)) for i in range(2)]
    ds4 = [l4(nc.sbuf_tensor(f"l3_d4{i}", [128, 512], BF16)) for i in range(2)]
    cp4 = [l4(nc.sbuf_tensor(f"l3_c4{i}", [128, 512], BF16)) for i in range(2)]
    ym4 = l4(nc.sbuf_tensor("l3_ym4", [128, 512], BF16))
    up4 = l4(nc.sbuf_tensor("l3_up4", [128, 512], BF16))
    ps_a4 = l4(nc.psum_tensor("l3_psa4", [128, 512], F32))
    ps_b4 = l4(nc.psum_tensor("l3_psb4", [128, 512], F32))

    li = 0
    for oc in range(NDC):
        def fn_al(oc=oc):
            return p.dma(af4[oc % 2][:].rearrange("p (c d) -> p c d", d=128),
                         A3.rearrange("(c p) d -> c p d", p=128)
                         [:, :, oc * 128:(oc + 1) * 128].transpose([1, 0, 2]))
        dep = [prev.get(("ax", oc - 2))] + ([l3_mm_last, l3_sx_last] if oc < 2 else [])
        s_al = p.step("sync", fn_al, deps=dep, kind="dma", n_dma=1, name=f"l3_al{oc}")

        def fn_bl(oc=oc):
            return p.dma(bb[oc % 2][:].rearrange("p (c d) -> p c d", d=128),
                         B3.rearrange("(c p) d -> c p d", p=128)
                         [:, :, oc * 128:(oc + 1) * 128].transpose([1, 0, 2]))
        dep = [prev.get(("mm4last", oc - 2))] + ([l3_mm_last, l3_sx_last] if oc < 2 else [])
        s_bl = p.step("gpsimd", fn_bl, deps=dep, kind="dma", n_dma=1, name=f"l3_bl{oc}")

        def fn_ax(oc=oc):
            return nc.scalar.activation(ea[oc % 2][:], af4[oc % 2][:], AF.Exp)
        s_ax = p.step("scalar", fn_ax, deps=[s_al, prev.get(("mm4last", oc - 2))],
                      name=f"l3_ax{oc}")
        prev[("ax", oc)] = s_ax

        for TB in range(NTT):
            slot = li % 2

            def fn_ul4(oc=oc, TB=TB, slot=slot):
                return p.dma(ucv4[slot][:],
                             UCV_scr[oc * 128:(oc + 1) * 128, TB * TT: TB * TT + TT])
            s_ul4 = p.step("sync", fn_ul4, deps=[prev.get(("sc4", li - 2))],
                           kind="dma", n_dma=1, name=f"l3_ul4_{oc}_{TB}")

            def fn_mm4(oc=oc, TB=TB):
                last = None
                for ic in range(NDC):
                    nc.tensor.matmul(
                        ps_a4[:], ea[oc % 2][:, ic * 128:(ic + 1) * 128],
                        delt_full[:, ic * 2048 + TB * TT: ic * 2048 + TB * TT + TT],
                        start=(ic == 0), stop=(ic == NDC - 1))
                for ic in range(NDC):
                    last = nc.tensor.matmul(
                        ps_b4[:], bb[oc % 2][:, ic * 128:(ic + 1) * 128],
                        delt_full[:, ic * 2048 + TB * TT: ic * 2048 + TB * TT + TT],
                        start=(ic == 0), stop=(ic == NDC - 1))
                return last
            dep = [s_ax, s_bl, l3_sx_last,
                   prev.get(("da", li - 1)), prev.get(("dbu", li - 1))]
            s_mm4 = p.step("tensor", fn_mm4, deps=dep, name=f"l3_mm4_{oc}_{TB}")
            prev[("mm4last", oc)] = s_mm4

            def fn_da():
                return nc.scalar.activation(dA[:], ps_a4[:], AF.Exp, scale=-1.0)
            s_da = p.step("scalar", fn_da, deps=[s_mm4, prev.get(("sc4", li - 1))],
                          name=f"l3_da_{oc}_{TB}")
            prev[("da", li)] = s_da

            def fn_dbu(slot=slot):
                return nc.vector.tensor_mul(dbu[:], ps_b4[:], ucv4[slot][:])
            s_dbu = p.step("vector", fn_dbu, deps=[s_mm4, s_ul4,
                                                   prev.get(("sc4", li - 1))],
                           late=True, name=f"l3_dbu_{oc}_{TB}")
            prev[("dbu", li)] = s_dbu

            def fn_sc4(oc=oc, slot=slot):
                b = slot
                p.chain(nc.vector.tensor_tensor_scan(
                    xs4[b][:], dA[:], dbu[:], carry[:, oc:oc + 1], ALU.mult, ALU.add))
                p.chain(nc.vector.tensor_copy(carry[:, oc:oc + 1], xs4[b][:, 511:512]))
                p.chain(nc.vector.tensor_tensor_scan(
                    pp4[b][:], dA[:], ones512[:], carry[:, 16 + oc:17 + oc],
                    ALU.mult, ALU.mult))
                p.chain(nc.vector.tensor_copy(carry[:, 16 + oc:17 + oc], pp4[b][:, 511:512]))
                p.chain(nc.vector.tensor_mul(ym4[:], ucv4[slot][:], xs4[b][:]))
                p.chain(nc.vector.tensor_tensor_scan(
                    ds4[b][:], ones512[:], ym4[:], carry[:, 32 + oc:33 + oc],
                    ALU.mult, ALU.add))
                p.chain(nc.vector.tensor_copy(carry[:, 32 + oc:33 + oc], ds4[b][:, 511:512]))
                p.chain(nc.vector.tensor_mul(up4[:], ucv4[slot][:], pp4[b][:]))
                p.chain(nc.vector.tensor_tensor_scan(
                    cp4[b][:], ones512[:], up4[:], carry[:, 48 + oc:49 + oc],
                    ALU.mult, ALU.add))
                return nc.vector.tensor_copy(carry[:, 48 + oc:49 + oc], cp4[b][:, 511:512])
            dep = [s_da, s_dbu, s_ini, prev.get(("sp4", li - 2))]
            s_sc4 = p.step("vector", fn_sc4, deps=dep, late=True,
                           name=f"l3_sc4_{oc}_{TB}")
            prev[("sc4", li)] = s_sc4

            def fn_sp4(oc=oc, TB=TB, slot=slot):
                b = slot
                sl = slice(oc * 128, (oc + 1) * 128)
                tsl = slice(TB * TT, TB * TT + TT)
                p.dma(XS_scr[sl, tsl], xs4[b][:])
                p.dma(PP_scr[sl, tsl], pp4[b][:])
                p.dma(DS_scr[sl, tsl], ds4[b][:])
                return p.dma(CPU_scr[sl, tsl], cp4[b][:])
            s_sp4 = p.step("gpsimd", fn_sp4, deps=[s_sc4], kind="dma", n_dma=4,
                           name=f"l3_sp4_{oc}_{TB}")
            prev[("sp4", li)] = s_sp4
            li += 1
    l4_done = [prev[("sp4", i)] for i in range(li)]
    sc4_last = prev[("sc4", li - 1)]
    l4_ctx.close()
    l34_ctx.close()

    def fn_stg2():
        p.chain(nc.vector.tensor_copy(recv32[:, 0:16], carry[:, 0:16]))
        return nc.vector.tensor_copy(recv32[:, 16:32], carry[:, 32:48])
    s_stg2 = p.step("vector", fn_stg2, deps=[sc4_last],
                    late=True, name="l3_stg2")

    def fn_snd2():
        return p.dma(cc2_in.ap(), recv32[:])
    s_snd2 = p.step("gpsimd", fn_snd2, deps=[s_stg2], kind="dma", n_dma=1, name="l3_snd2")

    def fn_cc2():
        return nc.gpsimd.collective_compute(
            "AllGather", ALU.bypass, replica_groups=PAIRS,
            ins=[cc2_in.ap().opt()], outs=[cc2_out.ap().opt()])
    s_cc2 = p.step("gpsimd", fn_cc2, deps=[s_snd2], kind="collective", name="l3_cc2")

    def fn_rcv2():
        return p.dma(recv32[:], cc2_out[0:128, :])
    s_rcv2 = p.step("gpsimd", fn_rcv2, deps=[s_cc2, s_stg2], kind="dma", n_dma=1,
                    name="l3_rcv2")

    def fn_fixc():
        p.chain(nc.vector.tensor_scalar(xs0c[:], recv32[:, 0:16], flag_sb[:, 0:1],
                                        None, ALU.mult))
        p.chain(nc.vector.tensor_scalar(dsoff[:], recv32[:, 16:32], -1.0, None, ALU.add))
        return nc.vector.tensor_scalar(dsoff[:], dsoff[:], flag_sb[:, 0:1],
                                       None, ALU.mult)
    s_fixc = p.step("vector", fn_fixc, deps=[s_rcv2, s_cst], late=True, name="l3_fixc")

    l6_ctx = ExitStack()
    l6 = l6_ctx.enter_context
    xst = l6(nc.sbuf_tensor("l3_xst", [128, 16 * 512], BF16))
    dst6 = l6(nc.sbuf_tensor("l3_dst6", [128, 512], BF16))
    zt6 = l6(nc.sbuf_tensor("l3_zt6", [128, 16 * 512], BF16))
    dsf = l6(nc.sbuf_tensor("l3_dsf", [128, 16 * 512], BF16))
    ybr = l6(nc.sbuf_tensor("l3_ybr", [128, 16 * 512], BF16))
    cw6 = [l6(nc.sbuf_tensor(f"l3_cw6{i}", [128, 16 * 128], BF16)) for i in range(2)]
    wo6 = [l6(nc.sbuf_tensor(f"l3_wo6{i}", [128, 16 * 128], BF16)) for i in range(2)]
    pt6 = l6(nc.sbuf_tensor("l3_pt6", [128, 2 * 512], BF16))
    ps_6 = [l6(nc.psum_tensor(f"l3_ps6{i}", [128, 512], F32)) for i in range(2)]

    yform = {}
    for TB in range(NTT):
        tsl = slice(TB * TT, TB * TT + TT)

        def fn_ld6(TB=TB, tsl=tsl):
            last = None
            for c in range(NDC):
                p.dma(xst[:, c * 512:(c + 1) * 512], XS_scr[c * 128:(c + 1) * 128, tsl])
                p.dma(zt6[:, c * 512:(c + 1) * 512], Z_scr[c * 128:(c + 1) * 128, tsl])
                last = p.dma(dsf[:, c * 512:(c + 1) * 512],
                             DS_scr[c * 128:(c + 1) * 128, tsl])
            return last
        dep = l4_done + [prev.get(("xcm_last", TB - 1)), prev.get(("ye6", TB - 1, NDC - 1)),
                         prev.get(("we_last", TB - 1))]
        s_ld6 = p.step("gpsimd", fn_ld6, deps=dep, kind="dma", n_dma=3 * NDC,
                       name=f"l3_ld6_{TB}")

        fix_steps = []
        for oc in range(NDC):
            def fn_ldp(oc=oc, tsl=tsl):
                p.dma(pt6[:, 0:512], PP_scr[oc * 128:(oc + 1) * 128, tsl])
                return p.dma(pt6[:, 512:1024], CPU_scr[oc * 128:(oc + 1) * 128, tsl])
            dep = [prev.get(("fx", oc - 1))] + ([s_ld6] if oc == 0 else [])
            s_ldp = p.step("gpsimd", fn_ldp, deps=dep + l4_done, kind="dma", n_dma=2,
                           name=f"l3_ldp_{TB}_{oc}")

            def fn_fx(oc=oc):
                p.chain(nc.vector.scalar_tensor_tensor(
                    xst[:, oc * 512:(oc + 1) * 512], pt6[:, 0:512],
                    xs0c[:, oc:oc + 1], xst[:, oc * 512:(oc + 1) * 512],
                    ALU.mult, ALU.add))
                p.chain(nc.vector.scalar_tensor_tensor(
                    dsf[:, oc * 512:(oc + 1) * 512], pt6[:, 512:1024],
                    xs0c[:, oc:oc + 1], dsf[:, oc * 512:(oc + 1) * 512],
                    ALU.mult, ALU.add))
                return nc.vector.tensor_scalar(
                    dsf[:, oc * 512:(oc + 1) * 512], dsf[:, oc * 512:(oc + 1) * 512],
                    dsoff[:, oc:oc + 1], None, ALU.add)
            s_fx = p.step("vector", fn_fx, deps=[s_ldp, s_ld6, s_fixc,
                                                 prev.get(("fx", oc - 1))],
                          late=True, name=f"l3_fx_{TB}_{oc}")
            prev[("fx", oc)] = s_fx
            fix_steps.append(s_fx)

        for ro in range(NDC):
            def fn_cwl(ro=ro):
                return p.dma(cw6[ro % 2][:].rearrange("p (c r) -> p c r", r=128),
                             C3B_scr.ap().rearrange(
                    "(c p) r -> c p r", p=128)[:, :, ro * 128:(ro + 1) * 128]
                    .transpose([1, 0, 2]))
            dep = [prev.get(("ye6", TB, ro - 2))] + pc_steps[:16] + \
                ([fix_steps[0]] if ro == 0 else [])
            s_cwl = p.step("sync", fn_cwl, deps=dep, kind="dma", n_dma=1,
                           name=f"l3_cwl_{TB}_{ro}")

            def fn_xcm(ro=ro):
                last = None
                for ic in range(NDC):
                    last = nc.tensor.matmul(
                        ps_6[ro % 2][:], cw6[ro % 2][:, ic * 128:(ic + 1) * 128],
                        xst[:, ic * 512:(ic + 1) * 512],
                        start=(ic == 0), stop=(ic == NDC - 1))
                return last
            dep = fix_steps + [s_cwl, prev.get(("ye6", TB, ro - 2))]
            s_xcm = p.step("tensor", fn_xcm, deps=dep, name=f"l3_xcm_{TB}_{ro}")
            prev[("xcm_last", TB)] = s_xcm

            def fn_ye6(ro=ro):
                p.chain(nc.vector.tensor_mul(ybr[:, ro * 512:(ro + 1) * 512],
                                             ps_6[ro % 2][:],
                                             zt6[:, ro * 512:(ro + 1) * 512]))
                return nc.vector.tensor_mul(ybr[:, ro * 512:(ro + 1) * 512],
                                            ybr[:, ro * 512:(ro + 1) * 512],
                                            dsf[:, ro * 512:(ro + 1) * 512])
            s_ye6 = p.step("vector", fn_ye6, deps=[s_xcm], late=True,
                           name=f"l3_ye6_{TB}_{ro}")
            prev[("ye6", TB, ro)] = s_ye6
        ye_all = [prev[("ye6", TB, ro)] for ro in range(NDC)]

        steps = []
        for od in range(NCH):
            def fn_wol(od=od):
                return p.dma(wo6[od % 2][:].rearrange("p (c r) -> p c r", r=128),
                             WOUTB_scr.ap().rearrange(
                    "(c p) r -> c p r", p=128)[:, :, od * 128:(od + 1) * 128]
                    .transpose([1, 0, 2]))
            dep = [prev.get(("we6", TB, od - 2)), ye_all[-1] if od == 0 else None] + \
                pc_steps[16:]
            s_wol = p.step("sync", fn_wol, deps=dep, kind="dma", n_dma=1,
                           name=f"l3_wol_{TB}_{od}")

            def fn_wm(od=od):
                last = None
                for ic in range(NDC):
                    last = nc.tensor.matmul(
                        ps_6[od % 2][:], wo6[od % 2][:, ic * 128:(ic + 1) * 128],
                        ybr[:, ic * 512:(ic + 1) * 512],
                        start=(ic == 0), stop=(ic == NDC - 1))
                return last
            dep = ye_all + [s_wol, prev.get(("we6", TB, od - 2))]
            s_wm = p.step("tensor", fn_wm, deps=dep, name=f"l3_wm_{TB}_{od}")
            prev[("we_last", TB)] = s_wm

            def fn_we(od=od, TB=TB):
                sl = slice(od * TOK + TB * TT, od * TOK + TB * TT + TT)
                return nc.vector.tensor_add(io.ybuf[:, sl], ps_6[od % 2][:], x[:, sl])
            s_we = p.step("vector", fn_we, deps=[s_wm], late=True,
                          name=f"l3_we_{TB}_{od}")
            prev[("we6", TB, od)] = s_we
            steps.append(s_we)
        yform[TB] = steps
    l6_ctx.close()

    ln_ctx = ExitStack()
    emit_postln(p, io, ln_ctx, lambda tt: yform[tt] + yform[NTT - 1],
                final=(io.cur_phase == TOTAL_PHASES - 1), tag="p6")
    ln_ctx.close()
    ctx.close()


PHASES = {0: phase_l0_attn, 1: phase_l0_ffn, 2: phase_l1_attn, 3: phase_l1_ffn, 4: phase_l2_attn, 5: phase_l2_ffn, 7: phase_l3_ffn, 6: phase_l3_attn}



def build_program(phase_ids):
    nc = bass.Bass("TRN2", target_bir_lowering=False)
    ctx = ExitStack()
    p = P(nc, ctx)
    io = IO()
    with ctx:
        alloc_globals(p, io)
        p.barrier()
        if phase_ids[0] != 0:
            stream_in(p, io)
            p.barrier()
        for pid in phase_ids:
            io.cur_phase = pid
            PHASES[pid](p, io)
            p.barrier()
        if phase_ids[-1] == TOTAL_PHASES - 1:
            def fn_fin():
                return p.nc.sync.nop()
            p.step("sync", fn_fin, deps=[p.last_step[n] for n in ENGINES],
                   name="final_wait")
        else:
            stream_out(p, io)
        with nc.Block() as block:
            p.emit(block)
    return nc, p


_idn = np.eye(128, dtype=np.float32)


def _launch(nc, p, sh, per, stream):
    in_maps = []
    for c in range(NCORES):
        m = {}
        for n in p.input_names:
            if n == "IDN":
                m[n] = _idn
            elif n == "XS_IN":
                m[n] = stream[c]
            elif n in per[c]:
                m[n] = per[c][n]
            else:
                m[n] = sh[n]
        in_maps.append(m)
    return run_bass_kernel_spmd(nc, in_maps, core_ids=list(range(NCORES)))


def _run(inputs, groups):
    sh, per = host_prep(inputs)
    stream = None
    out = None
    for g in groups:
        nc, p = build_program(g)
        res = _launch(nc, p, sh, per, stream)
        if "XS_OUT" in res.results[0]:
            stream = [res.results[c]["XS_OUT"] for c in range(NCORES)]
        if "OUT" in res.results[0]:
            out = [res.results[c]["OUT"] for c in range(NCORES)]
    return stream, out


FUSED = True


def kernel(**inputs):
    if FUSED:
        groups = [sorted(PHASES)]
    else:
        groups = [[i] for i in sorted(PHASES)]
    stream, out = _run(inputs, groups)
    full = np.concatenate([o.T for o in out], axis=0)
    return np.ascontiguousarray(full.reshape(B, T, D))
